# version 11
# baseline (speedup 1.0000x reference)
import numpy as np
import ml_dtypes
import contextlib
import concourse.bass as bass
import concourse.mybir as mybir
from concourse.bass_utils import run_bass_kernel_spmd

F32 = mybir.dt.float32
BF16 = mybir.dt.bfloat16
FP8 = mybir.dt.float8e4
AF = mybir.ActivationFunctionType
ALU = mybir.AluOpType
AX = mybir.AxisListType
DR = mybir.MatmulPerfMode.DoubleRow

B, S, D = 4, 4096, 2048
HQ, HKV, HD = 16, 4, 128
WIN = 512
CAP = 50.0
THETA = 10000.0
EPS = 1e-5
N_OWN, HALO, N_TOK = 2048, 512, 2560
T_OWN, T_ALL = 16, 20
NCH = 4
FMAX = 448.0
NEG = -1e30

ENGINES = ("sync", "scalar", "vector", "tensor", "gpsimd")


class Sched:
    def __init__(self):
        self.seq = {e: [] for e in ENGINES}
        self.cnt = {}
        self.known = set()

    def new_sem(self, name):
        self.cnt[name] = 0

    def do(self, eng, fn, waits=(), inc=None, amt=1):
        w = {}
        for e in waits:
            if e is None:
                continue
            s, v = e
            assert (s, v) in self.known, f"wait on future/unknown event {s}:{v}"
            w[s] = max(w.get(s, 0), v)
        self.seq[eng].append((w, fn, inc, amt))
        if inc is not None:
            self.cnt[inc] += amt
            e = (inc, self.cnt[inc])
            self.known.add(e)
            return e
        return None


def build_nc():
    nc = bass.Bass("TRN2", target_bir_lowering=False)
    sch = Sched()

    x_in = nc.dram_tensor("x", [N_TOK, D], F32, kind="ExternalInput")
    cosT_d = nc.dram_tensor("cosT", [HD, N_TOK], F32, kind="ExternalInput")
    sinT_d = nc.dram_tensor("sinT", [HD, N_TOK], F32, kind="ExternalInput")
    masks_d = nc.dram_tensor("masks", [22, HD, 512], BF16, kind="ExternalInput")
    ones_d = nc.dram_tensor("ones_in", [HD, HD], BF16, kind="ExternalInput")
    ident_d = nc.dram_tensor("ident_in", [HD, HD], BF16, kind="ExternalInput")
    wqT_d = nc.dram_tensor("wqT", [D, D], F32, kind="ExternalInput")
    wkT_d = nc.dram_tensor("wkT", [D, 512], F32, kind="ExternalInput")
    wvT_d = nc.dram_tensor("wvT", [D, 512], F32, kind="ExternalInput")
    woT_d = nc.dram_tensor("woT", [D, D], F32, kind="ExternalInput")
    wfc_p = nc.dram_tensor("wfc_p", [8, 128, D], F32, kind="ExternalInput")
    wp_p = nc.dram_tensor("wp_p", [4, 4, 128, 1024], F32, kind="ExternalInput")
    y_out = nc.dram_tensor("y", [N_OWN, D], F32, kind="ExternalOutput")

    wqT_bf = nc.dram_tensor("wqT_bf", [D, D], BF16)
    wkT_bf = nc.dram_tensor("wkT_bf", [D, 512], BF16)
    wvT_bf = nc.dram_tensor("wvT_bf", [D, 512], BF16)
    woT_bf = nc.dram_tensor("woT_bf", [D, D], BF16)
    wfc8_ag = nc.dram_tensor("wfc8_ag", [8 * 128, D], FP8)
    wfc8 = nc.dram_tensor("wfc8", [64 * 128, D], FP8)
    wp8_ag = nc.dram_tensor("wp8_ag", [16 * 128, 1024], FP8)
    wp8 = nc.dram_tensor("wp8", [128 * 128, 1024], FP8)
    xnT_scr = nc.dram_tensor("xnT_scr", [16, 128, N_TOK], BF16)
    x1_scr = nc.dram_tensor("x1_scr", [N_OWN, D], F32)
    r_spill = nc.dram_tensor("r_spill", [4 * D, N_OWN], BF16)
    cc_in = [nc.dram_tensor(f"cc{i}_in", [1, 8], F32) for i in range(3)]
    cc_out = [nc.dram_tensor(f"cc{i}_out", [1, 8], F32) for i in range(3)]
    col_scr = nc.dram_tensor("col_scr", [HD, 1], F32)
    row_scr = nc.dram_tensor("row_scr", [HD, 1], F32)

    SB_BASE = 16512
    def sb(name, shape, dtype, off_kb):
        return nc.alloc_sbuf_tensor_at(name, shape, dtype, offset=SB_BASE + int(off_kb * 1024))

    _cur = [SB_BASE]

    def sbp(name, shape, dtype):
        ds = {F32: 4, BF16: 2, FP8: 1}[dtype]
        n = 1
        for s_ in shape[1:]:
            n *= s_
        t = nc.alloc_sbuf_tensor_at(name, shape, dtype, offset=_cur[0])
        _cur[0] += -(-n * ds // 32) * 32
        assert _cur[0] <= SB_BASE + 6 * 1024
        return t

    ones_bf = sbp("ones_bf", [HD, HD], BF16)
    ident = sbp("ident", [HD, HD], BF16)
    identf = sbp("identf", [HD, HD], F32)
    onesrow = sbp("onesrow", [1, 128], F32)
    scp = sbp("scp", [1, 64], F32)
    bc = sbp("bc", [HD, 8], F32)
    grow = sbp("grow", [1, 128], F32)
    pam = sbp("pam", [HD, 40], F32)
    s1a = sbp("s1a", [HD, 24], F32)
    s1b = sbp("s1b", [HD, 24], F32)
    s1m = sbp("s1m", [HD, 24], F32)
    s1r = sbp("s1r", [HD, 24], F32)
    s1t = sbp("s1t", [HD, 24], F32)
    w1 = sbp("wst1", [HD, 16], F32)
    w2 = sbp("wst2", [HD, 16], F32)
    wm = sbp("wstm", [HD, 16], F32)
    wr = sbp("wstr", [HD, 16], F32)
    wt = sbp("wstt", [HD, 16], F32)
    m2a = sbp("m2a", [HD, 16], F32)
    m2b = sbp("m2b", [HD, 16], F32)
    m2m = sbp("m2m", [HD, 16], F32)
    m2r = sbp("m2r", [HD, 16], F32)
    m2t = sbp("m2t", [HD, 16], F32)
    ntb = sbp("ntb", [HD, 16], F32)
    ntb2 = sbp("ntb2", [HD, 16], F32)
    am_x = sbp("am_x", [HD, 16], F32)
    am_r = sbp("am_r", [HD, 1], F32)
    f3a = sbp("f3a", [HD, 16], F32)
    f3b = sbp("f3b", [HD, 16], F32)
    f3m = sbp("f3m", [HD, 16], F32)
    f3r = sbp("f3r", [HD, 16], F32)
    f3t = sbp("f3t", [HD, 16], F32)
    cst_eps = sbp("cst_eps", [HD, 1], F32)
    cst_ncap = sbp("cst_ncap", [HD, 1], F32)

    ax = [sb("ax0", [HD, D], F32, 6), sb("ax1", [HD, D], F32, 14)]
    xnc = [sb("xnc0", [HD, 16, 512], BF16, 22), sb("xnc1", [HD, 16, 512], BF16, 38)]
    xn_tile = sb("xn_tile", [HD, D], F32, 54)
    at3 = sb("at3", [HD, 16, 16, 128], BF16, 6)
    h8 = sb("h8", [HD, 32, 2, 1024], FP8, 6)
    pwb = [sb("pw0", [HD, 1024], F32, 62), sb("pw1", [HD, 1024], F32, 66)]
    p8 = sb("p8", [HD, 1024], FP8, 134)
    wqs = sb("wqs", [HD, 2, 16, 512], BF16, 70)
    xnT_sb = sb("xnT_sb", [HD, 16, 512], BF16, 102)
    qc = sb("qc", [HD, 16 * 512], BF16, 118)
    xn_tile_b = sb("xn_tile_b", [HD, D], F32, 118)
    woT_sb = sb("woT_sb", [HD, 16, D], BF16, 70)
    xq8 = sb("xq8", [HD, 16, N_OWN], FP8, 70)
    f1w = [sb("f1w0", [HD, D], FP8, 102), sb("f1w1", [HD, D], FP8, 104)]
    o2 = sb("o2", [HD, 8, D], F32, 70)
    kT = sb("kT", [HD, HKV * N_TOK], BF16, 134)
    vT = sb("vT", [HD, T_ALL * 512], BF16, 154)
    a4x = [sb("a4x0", [HD, D], F32, 134), sb("a4x1", [HD, D], F32, 142)]
    sq2 = sb("sq2", [HD, D], F32, 150)
    o1b = [sb("o10", [HD, D], F32, 158), sb("o11", [HD, D], F32, 166)]
    f1r = [sb("f1r0", [HD, D], BF16, 134), sb("f1r1", [HD, D], BF16, 138)]
    f2w = [sb("f2w0", [HD, 1024], FP8, 142), sb("f2w1", [HD, 1024], FP8, 143)]
    f2r = [sb("f2r0", [HD, 1024], BF16, 144), sb("f2r1", [HD, 1024], BF16, 146)]
    f2x = [sb("f2x0", [HD, D], F32, 148), sb("f2x1", [HD, D], F32, 156)]
    f2t = sb("f2t", [HD, D], F32, 164)
    tabc = sb("tabc", [HD, 2 * 1024], F32, 174)
    sq1 = sb("sq1", [HD, D], F32, 182)
    mask5 = sb("mask5", [HD, 5 * 512], BF16, 182)
    mpers = sb("mpers", [HD, 2 * 512], BF16, 187)
    pTs = sb("pTs", [HD, 5 * 512], BF16, 189)
    ropa = sb("ropa", [HD, 512], F32, 194)
    ropb = sb("ropb", [HD, 512], F32, 196)
    rinv2 = [sb("rinv", [HD, 512], F32, 198), sb("rinvB", [HD, 512], F32, 205)]
    pTs2 = sb("pTsB", [HD, 5 * 512], BF16, 200)
    xn2_tile = sb("xn2_tile", [HD, D], F32, 182)

    pA = nc.alloc_psum_tensor("pA", [HD, D], F32)
    pB = nc.alloc_psum_tensor("pB", [HD, D], F32)

    SEM_NAMES = ("g_d", "s_d", "cc", "te", "ve", "se", "ge",
                 "d_x0", "d_x1", "st_xnt", "d_xnt", "d_wkv",
                 "d_wq0", "d_wq1", "d_wo", "d_wfc", "d_wp0", "d_wp1",
                 "d_r0", "d_r1", "st_r0", "st_r1", "st_x10", "st_x11",
                 "st_y0", "st_y1", "d_tab0", "d_tab1", "d_mask")
    CHAINQ = {"gpsimd": "g_d", "scalar": "s_d"}
    _chain_last = {}
    for s in SEM_NAMES:
        sch.new_sem(s)

    def dma(q, sem, out, in_, waits=()):
        waits = [w for w in waits if w is not None]
        if q in CHAINQ:
            sem = CHAINQ[q]
            if sem in _chain_last:
                waits = waits + [_chain_last[sem]]
        ev = sch.do(q, lambda e, o=out, i=in_: e.dma_start(out=o, in_=i),
                    waits=waits, inc=sem, amt=16)
        if q in CHAINQ:
            _chain_last[sem] = ev
        return ev

    def vop(fn, waits=()):
        return sch.do("vector", fn, waits=waits, inc="ve")

    def sop(fn, waits=()):
        return sch.do("scalar", fn, waits=waits, inc="se")

    def top(fn, waits=()):
        return sch.do("tensor", fn, waits=waits, inc="te")

    def gop(fn, waits=()):
        return sch.do("gpsimd", fn, waits=waits, inc="ge")

    def coll(kind, op, i_, o_, waits):
        ev = sch.do("gpsimd", lambda e, k=kind, o=op, a=i_, b=o_: e.collective_compute(
            k, o, replica_groups=[list(range(8))], ins=[a], outs=[b]),
            waits=waits + ([_chain_last["g_d"]] if "g_d" in _chain_last else []),
            inc="cc", amt=1)
        return ev

    e_grow = [None]

    def part_max(src_col, slot, waits):
        w = list(waits) + ([e_grow[0]] if e_grow[0] else [])
        e1 = dma("gpsimd", "g_d", col_scr[:, :], src_col, waits=w)
        e2 = dma("gpsimd", "g_d", grow[0:1, 0:128], col_scr[:, :], waits=[e1])
        e3 = vop(lambda e, sl=slot: e.tensor_reduce(
            out=scp[0:1, sl:sl + 1], in_=grow[0:1, 0:128], axis=AX.X, op=ALU.max),
            waits=[e2])
        e_grow[0] = e3
        return e3

    def bcast(slot, col, waits):
        w = list(waits) + ([e_grow[0]] if e_grow[0] else [])
        e1 = vop(lambda e, sl=slot: e.tensor_scalar(
            out=grow[0:1, 0:128], in0=onesrow[0:1, 0:128],
            scalar1=scp[0:1, sl:sl + 1], scalar2=None, op0=ALU.mult), waits=w)
        e2 = dma("gpsimd", "g_d", row_scr[:, :], grow[0:1, 0:128], waits=[e1])
        e3 = dma("gpsimd", "g_d", bc[:, col:col + 1], row_scr[:, :], waits=[e2])
        e_grow[0] = e3
        return e3

    def allred_max(idx, slot_in, slot_out, waits):
        w = list(waits) + ([e_grow[0]] if e_grow[0] else [])
        e1 = vop(lambda e, sl=slot_in: e.tensor_scalar(
            out=grow[0:1, 0:8], in0=onesrow[0:1, 0:8],
            scalar1=scp[0:1, sl:sl + 1], scalar2=None, op0=ALU.mult), waits=w)
        e2 = dma("gpsimd", "g_d", cc_in[idx][:, :], grow[0:1, 0:8], waits=[e1])
        e3 = coll("AllReduce", ALU.max, cc_in[idx][:, :], cc_out[idx][:, :], waits=[e2])
        e4 = dma("gpsimd", "g_d", grow[0:1, 0:8], cc_out[idx][:, :], waits=[e3])
        e5 = vop(lambda e, sl=slot_out: e.tensor_copy(
            out=scp[0:1, sl:sl + 1], in_=grow[0:1, 0:1]), waits=[e4])
        e_grow[0] = e5
        return e5

    e_ones = dma("gpsimd", "g_d", ones_bf[:, :], ones_d[:, :])
    e_id0 = dma("gpsimd", "g_d", ident[:, :], ident_d[:, :])
    e_ident = vop(lambda e: e.tensor_copy(out=identf[:, :], in_=ident[:, :]),
                  waits=[e_id0])
    e_or = gop(lambda e: e.memset(onesrow[0:1, 0:128], 1.0))
    e_ceps = gop(lambda e: e.memset(cst_eps[:, :], float(EPS)), waits=[e_or])
    e_ncap = gop(lambda e: e.memset(cst_ncap[:, :], float(-8.0)), waits=[e_ceps])

    e_wkbf = dma("gpsimd", "g_d", wkT_bf[:, :], wkT_d[:, :])
    e_wvbf = dma("gpsimd", "g_d", wvT_bf[:, :], wvT_d[:, :])
    e_wqbf = []
    for hq in range(4):
        e_wqbf.append(dma("gpsimd", "g_d", wqT_bf[:, hq * 512:(hq + 1) * 512],
                          wqT_d[:, hq * 512:(hq + 1) * 512]))
    e_wobf = dma("gpsimd", "g_d", woT_bf[:, :], woT_d[:, :])

    def prep_src(j):
        if j < 16:
            return wfc_p[j // 2, :, (j % 2) * 1024:(j % 2 + 1) * 1024]
        i, q = (j - 16) // 4, (j - 16) % 4
        return wp_p[i, q, :, :]

    def prep_dst(j):
        if j < 16:
            return wfc8_ag[(j // 2) * 128:(j // 2 + 1) * 128,
                           (j % 2) * 1024:(j % 2 + 1) * 1024]
        i, q = (j - 16) // 4, (j - 16) % 4
        base = (i * 4 + q) * 128
        return wp8_ag[base:base + 128, :]

    prep_state = {"red": {}, "last_red": None}

    def emit_amax(j):
        w = []
        if j >= 2:
            w.append(prep_state["red"][j - 2])
        el = dma("gpsimd", "g_d", pwb[j % 2][:, :], prep_src(j), waits=w)
        er = vop(lambda e, j=j: e.tensor_reduce(
            out=pam[:, j:j + 1], in_=pwb[j % 2][:, :], axis=AX.X, op=ALU.max,
            apply_absolute_value=True), waits=[el])
        prep_state["red"][j] = er
        prep_state["last_red"] = er

    at4 = at3[:, :, :, :]
    wkv3 = wqs[:, :, :, :].rearrange("p a c m -> p (a c) m")
    e_wkv = dma("sync", "d_wkv", wkv3[:, 0:16, :],
                wkT_bf[:, :].rearrange("(cc p) m -> p cc m", cc=16), waits=[e_wkbf])
    e_wvv = dma("sync", "d_wkv", wkv3[:, 16:32, :],
                wvT_bf[:, :].rearrange("(cc p) m -> p cc m", cc=16), waits=[e_wvbf])

    def rope_evac(pt, pcol, tb, out_ap, waits):
        ea = vop(lambda e: e.tensor_tensor(
            out=ropa[:, :], in0=pt[0:128, pcol:pcol + 512],
            in1=tabc[0:128, tb:tb + 512], op=ALU.mult), waits=waits)
        eb = vop(lambda e: e.tensor_tensor(
            out=ropb[0:64, :], in0=pt[64:128, pcol:pcol + 512],
            in1=tabc[0:64, tb + 512:tb + 1024], op=ALU.mult), waits=[ea])
        ec = vop(lambda e: e.tensor_tensor(
            out=ropb[64:128, :], in0=pt[0:64, pcol:pcol + 512],
            in1=tabc[64:128, tb + 512:tb + 1024], op=ALU.mult), waits=[eb])
        ed = vop(lambda e: e.tensor_tensor(
            out=out_ap, in0=ropa[:, :], in1=ropb[:, :], op=ALU.add), waits=[ec])
        return ed

    ev_tab_prev = None
    e_prev_kmm = None
    e_prev_rope = None
    e_prev_vcopy = None
    ev_xload = {}
    ev_xn_ecp = {}
    ev_xnc_store = {}
    e_exn = {}
    ev_sq = {}
    e_tp_evac = {"A": None, "B": None}
    e_sq1_rd = None
    e_ln_prev = None
    for c in range(5):
        c0 = c * 4
        for tt in range(4):
            t = c0 + tt
            w = []
            if t >= 2:
                w.append(e_exn.get(t - 2))
                w.append(ev_sq.get(t - 2))
            el = dma("sync", f"d_x{t % 2}", ax[t % 2][:, :], x_in[t * 128:(t + 1) * 128, :],
                     waits=w)
            ev_xload[t] = el
            e1 = vop(lambda e, b=ax[t % 2], t=t: e.reduce_sum(
                out=s1a[:, t:t + 1], in_=b[:, :], axis=AX.X), waits=[el])
            wsq = [el]
            if e_sq1_rd is not None:
                wsq.append(e_sq1_rd)
            esq = sop(lambda e, b=ax[t % 2]: e.activation(out=sq1[:, :], in_=b[:, :],
                                                          func=AF.Square), waits=wsq)
            e2 = vop(lambda e, t=t: e.reduce_sum(out=s1b[:, t:t + 1], in_=sq1[:, :],
                                                 axis=AX.X), waits=[esq])
            e_sq1_rd = e2
            ev_sq[t] = e2
            eM = vop(lambda e, t=t: e.tensor_scalar(
                out=s1m[:, t:t + 1], in0=s1a[:, t:t + 1], scalar1=float(1.0 / D),
                scalar2=None, op0=ALU.mult), waits=[e1])
            eE = vop(lambda e, t=t: e.tensor_scalar(
                out=s1b[:, t:t + 1], in0=s1b[:, t:t + 1], scalar1=float(1.0 / D),
                scalar2=None, op0=ALU.mult), waits=[e2])
            eQ = vop(lambda e, t=t: e.tensor_tensor(
                out=s1t[:, t:t + 1], in0=s1m[:, t:t + 1], in1=s1m[:, t:t + 1],
                op=ALU.mult), waits=[eM])
            eV = vop(lambda e, t=t: e.tensor_tensor(
                out=s1b[:, t:t + 1], in0=s1b[:, t:t + 1], in1=s1t[:, t:t + 1],
                op=ALU.subtract), waits=[eE, eQ])
            wl_ = [eV, e_ceps]
            if e_ln_prev is not None:
                wl_.append(e_ln_prev)
            eL = sop(lambda e, t=t: e.activation(out=s1r[:, t:t + 1], in_=s1b[:, t:t + 1],
                                                 func=AF.Ln, bias=cst_eps[:, 0:1]),
                     waits=wl_)
            eR = sop(lambda e, t=t: e.activation(out=s1r[:, t:t + 1], in_=s1r[:, t:t + 1],
                                                 func=AF.Exp, scale=-0.5), waits=[eL])
            e_ln_prev = eR
            xnb = xn_tile if (t % 2 == 0) else xn_tile_b
            wn = [eR, el]
            if t >= 2:
                wn.append(ev_xn_ecp.get(t - 2))
            exn = vop(lambda e, b=ax[t % 2], o=xnb, t=t: e.tensor_scalar(
                out=o[:, :], in0=b[:, :], scalar1=s1m[:, t:t + 1],
                scalar2=s1r[:, t:t + 1], op0=ALU.subtract, op1=ALU.mult), waits=wn)
            e_exn[t] = exn
            ps = pA if (tt % 2 == 0) else pB
            key = "A" if (tt % 2 == 0) else "B"
            wtp = [exn, e_ident]
            if e_tp_evac[key] is not None:
                wtp.append(e_tp_evac[key])
            if c > 0:
                if key == "A" and e_prev_rope is not None:
                    wtp.append(e_prev_rope)
                if key == "B" and e_prev_vcopy is not None:
                    wtp.append(e_prev_vcopy)
            etp = None
            for cc in range(16):
                etp = top(lambda e, ps=ps, o=xnb, cc=cc: e.matmul(
                    ps[:, cc * 128:(cc + 1) * 128], o[:, cc * 128:(cc + 1) * 128],
                    identf[:, :], start=True, stop=True, is_transpose=True),
                    waits=(wtp if cc == 0 else []))
            wev = [etp]
            if c >= 2:
                wev.append(e_prev_kmm)
                wev.append(ev_xnc_store.get(c - 2))
            ecp = sop(lambda e, ps=ps, c=c, tt=tt: e.copy(
                out=xnc[c % 2][:, :, tt * 128:(tt + 1) * 128],
                in_=ps[:, :].rearrange("p (cc m) -> p cc m", cc=16)), waits=wev)
            ev_xn_ecp[t] = ecp
            e_tp_evac[key] = ecp
        ev_xnc_store[c] = dma("scalar", "st_xnt",
                              xnT_scr[:, :, c * 512:(c + 1) * 512]
                              .rearrange("cc p t -> p cc t"),
                              xnc[c % 2][:, :, :], waits=[ev_xn_ecp[c0 + 3]])
        tb = (c % 2) * 1024
        wtab = []
        if ev_tab_prev:
            wtab.append(ev_tab_prev)
        if c >= 2:
            wtab.append(e_prev_rope)
        etc = dma("sync", f"d_tab{c % 2}", tabc[:, tb:tb + 512],
                  cosT_d[:, c * 512:(c + 1) * 512], waits=wtab)
        ets = dma("sync", f"d_tab{c % 2}", tabc[:, tb + 512:tb + 1024],
                  sinT_d[:, c * 512:(c + 1) * 512], waits=[etc])
        ev_tab_prev = ets
        mmw = [ev_xn_ecp[c0 + 3], e_wkv, e_wvv]
        if e_prev_rope is not None:
            mmw += [e_prev_rope, e_prev_vcopy]
        ek = []
        for hh in range(HKV):
            for cc in range(16):
                ek.append(top(lambda e, hh=hh, cc=cc, c=c: e.matmul(
                    pA[:, hh * 512:(hh + 1) * 512],
                    wkv3[:, cc, hh * 128:(hh + 1) * 128],
                    xnc[c % 2][:, cc, :],
                    start=(cc == 0), stop=(cc == 15)),
                    waits=(mmw if (hh == 0 and cc == 0) else [])))
        evm = []
        for i in range(4):
            for cc in range(16):
                evm.append(top(lambda e, i=i, cc=cc, c=c: e.matmul(
                    pB[:, i * 512:(i + 1) * 512],
                    xnc[c % 2][:, cc, i * 128:(i + 1) * 128],
                    wkv3[:, 16 + cc, :],
                    start=(cc == 0), stop=(cc == 15)), waits=[]))
        e_prev_kmm = evm[-1]
        for hh in range(HKV):
            w = [ek[hh * 16 + 15], ets]
            if e_prev_rope is not None:
                w.append(e_prev_rope)
            e_prev_rope = rope_evac(
                pA, hh * 512, tb,
                kT[:, hh * N_TOK + c * 512:hh * N_TOK + c * 512 + 512], w)
        for i in range(4):
            w = [evm[i * 16 + 15]]
            if e_prev_vcopy is not None:
                w.append(e_prev_vcopy)
            e_prev_vcopy = sop(lambda e, i=i, c=c: e.copy(
                out=vT[:, (c * 4 + i) * 512:(c * 4 + i) * 512 + 512],
                in_=pB[:, i * 512:(i + 1) * 512]), waits=w)
        for j in range(c * 7, min(32, c * 7 + 7)):
            emit_amax(j)
    e_a1_alldone = ev_xn_ecp[19]
    e_fam = vop(lambda e: e.tensor_reduce(
        out=pam[:, 32:33], in_=pam[:, 0:16], axis=AX.X, op=ALU.max),
        waits=[prep_state["last_red"]])
    e_pam = vop(lambda e: e.tensor_reduce(
        out=pam[:, 33:34], in_=pam[:, 16:32], axis=AX.X, op=ALU.max), waits=[e_fam])
    e_fm = part_max(pam[:, 32:33], 0, [e_fam])
    e_pm = part_max(pam[:, 33:34], 1, [e_pam])
    e_r0 = vop(lambda e: e.tensor_scalar(
        out=grow[0:1, 0:8], in0=onesrow[0:1, 0:8], scalar1=scp[0:1, 0:1],
        scalar2=None, op0=ALU.mult), waits=[e_or, e_fm, e_pm, e_grow[0]])
    e_r1 = vop(lambda e: e.tensor_scalar(
        out=grow[0:1, 1:2], in0=onesrow[0:1, 1:2], scalar1=scp[0:1, 1:2],
        scalar2=None, op0=ALU.mult), waits=[e_r0])
    e_grow[0] = e_r1
    e_ci = dma("gpsimd", "g_d", cc_in[0][:, :], grow[0:1, 0:8], waits=[e_r1])
    e_ar1 = coll("AllReduce", ALU.max, cc_in[0][:, :], cc_out[0][:, :], waits=[e_ci])
    e_mp0 = dma("sync", "d_mask", mpers[:, 0:512], masks_d[20, :, :],
                waits=[e_a1_alldone])
    e_mp = dma("sync", "d_mask", mpers[:, 512:1024], masks_d[21, :, :], waits=[e_mp0])

    qc3 = qc[:, :].rearrange("p (h t) -> p h t", h=HQ)
    at4f = at3[:, :, :, :].rearrange("p a b c -> p (a b c)")
    ev_a3_attn_last = None
    e_prev_qrope = e_prev_rope
    e_attn_mm_last = None
    ev_mask_prev = None
    e_tanh = {}
    e_exp_prev = None
    e_pT_use = {}
    e_sum_rd = {0: None, 1: None}
    e_pv_rd = {0: None, 1: None}
    e_rinv_rd = {0: None, 1: None}
    e_first_score = True
    e_mask_tt_last = None
    e_xnt_rd = None
    prev_chunk_qmm = None
    e_q_mm_prev = None
    e_prep_sbuf_free = None
    e_p8_free = None
    for c in range(NCH):
        if c == 2:
            e_c1l = dma("gpsimd", "g_d", grow[0:1, 0:8], cc_out[0][:, :],
                        waits=[e_ar1, e_grow[0]])
            e_grow[0] = e_c1l
            e_swf = vop(lambda e: e.tensor_scalar(
                out=scp[0:1, 2:3], in0=grow[0:1, 0:1], scalar1=float(1.0 / FMAX),
                scalar2=1e-6, op0=ALU.mult, op1=ALU.max), waits=[e_c1l])
            e_swp = vop(lambda e: e.tensor_scalar(
                out=scp[0:1, 4:5], in0=grow[0:1, 1:2], scalar1=float(1.0 / FMAX),
                scalar2=1e-6, op0=ALU.mult, op1=ALU.max), waits=[e_swf])
            e_grow[0] = e_swp
            e_k0 = vop(lambda e: e.reciprocal(out=scp[0:1, 3:4], in_=scp[0:1, 2:3]),
                       waits=[e_swf])
            e_kwf = vop(lambda e: e.tensor_scalar(
                out=scp[0:1, 3:4], in0=scp[0:1, 3:4], scalar1=0.5, scalar2=None,
                op0=ALU.mult), waits=[e_k0])
            e_k1 = vop(lambda e: e.reciprocal(out=scp[0:1, 5:6], in_=scp[0:1, 4:5]),
                       waits=[e_swp])
            e_kwp = vop(lambda e: e.tensor_scalar(
                out=scp[0:1, 5:6], in0=scp[0:1, 5:6], scalar1=0.5, scalar2=None,
                op0=ALU.mult), waits=[e_k1])
            e_bwf = bcast(3, 2, [e_kwf])
            e_bwp = bcast(5, 3, [e_kwp])
            e_q_prev = {}
            e_st_prev = None
            for j in range(32):
                w = [e_bwf if j < 16 else e_bwp]
                if j >= 2:
                    w.append(e_q_prev[j - 2])
                el = dma("gpsimd", "g_d", pwb[j % 2][:, :], prep_src(j), waits=w)
                eq = vop(lambda e, j=j: e.tensor_scalar(
                    out=p8[:, :], in0=pwb[j % 2][:, :],
                    scalar1=bc[:, 2:3] if j < 16 else bc[:, 3:4],
                    scalar2=None, op0=ALU.mult), waits=[el, e_bwf, e_bwp] +
                    ([e_st_prev] if e_st_prev else []))
                e_q_prev[j] = eq
                e_st_prev = dma("gpsimd", "g_d", prep_dst(j), p8[:, :], waits=[eq])
            e_prep_sbuf_free = e_q_prev[31]
            e_p8_free = e_st_prev
            e_ag1 = coll("AllGather", ALU.bypass, wfc8_ag[:, :], wfc8[:, :],
                         waits=[e_st_prev])
            e_ag2 = coll("AllGather", ALU.bypass, wp8_ag[:, :], wp8[:, :], waits=[e_ag1])
        tb = (c % 2) * 1024
        wtab = [ev_tab_prev]
        wtab.append(e_prev_qrope if c >= 1 else e_prev_rope)
        etc = dma("sync", f"d_tab{c % 2}", tabc[:, tb:tb + 512],
                  cosT_d[:, 512 + c * 512:512 + (c + 1) * 512], waits=wtab)
        ets = dma("sync", f"d_tab{c % 2}", tabc[:, tb + 512:tb + 1024],
                  sinT_d[:, 512 + c * 512:512 + (c + 1) * 512], waits=[etc])
        ev_tab_prev = ets
        wxl = [ev_xnc_store[c + 1]]
        if e_xnt_rd is not None:
            wxl.append(e_xnt_rd)
        exl = dma("sync", "d_xnt", xnT_sb[:, :, :],
                  xnT_scr[:, :, 512 + c * 512:512 + (c + 1) * 512]
                  .rearrange("cc p t -> p cc t"), waits=wxl)
        eq_mm_done = []
        for hq in range(4):
            wb = hq % 2
            w = [e_wqbf[hq]]
            if c == 0 and hq < 2:
                w.append(e_prev_kmm)
            if hq >= 2:
                w.append(eq_mm_done[hq - 2])
            elif c > 0:
                w.append(prev_chunk_qmm[hq - 2])
            ewq = dma("sync", f"d_wq{hq % 2}", wqs[:, wb, :, :],
                      wqT_bf[:, hq * 512:(hq + 1) * 512]
                      .rearrange("(cc p) m -> p cc m", cc=16), waits=w)
            mmw = [exl, ewq]
            if e_prev_qrope is not None:
                mmw.append(e_prev_qrope)
            if c == 0 and hq == 0:
                mmw.append(e_prev_kmm)
                mmw.append(e_prev_rope)
            if c > 0 and hq == 0:
                mmw.append(e_exp_prev)
                mmw.append(ev_a3_attn_last)
            emm = []
            for cc in range(16):
                for hh in range(4):
                    emm.append(top(lambda e, cc=cc, hh=hh, wb=wb: e.matmul(
                        pA[:, hh * 512:(hh + 1) * 512],
                        wqs[:, wb, cc, hh * 128:(hh + 1) * 128],
                        xnT_sb[:, cc, :],
                        start=(cc == 0), stop=(cc == 15)),
                        waits=(mmw if (cc == 0 and hh == 0) else [])))
            eq_mm_done.append(emm[-1])
            e_xnt_rd = emm[-1]
            for hh in range(4):
                hg = hq * 4 + hh
                w = [emm[15 * 4 + hh], ets]
                if e_prev_qrope is not None:
                    w.append(e_prev_qrope)
                e_prev_qrope = rope_evac(pA, hh * 512, tb,
                                         qc[:, hg * 512:hg * 512 + 512], w)
        prev_chunk_qmm = eq_mm_done
        e_q_mm_prev = eq_mm_done[-1]
        for i in range(4):
            n = 4 * c + i
            if n < 4:
                ml = []
                for j in range(5):
                    if j == 0:
                        w0 = [e_mp, e_a1_alldone]
                        if e_mask_tt_last is not None:
                            w0 = [e_mask_tt_last]
                    else:
                        w0 = [ml[-1]]
                    ml.append(dma("sync", "d_mask", mask5[:, j * 512:j * 512 + 512],
                                  masks_d[n * 5 + j, :, :], waits=w0))
                ev_mask_prev = ml[-1]
            for g in range(HKV):
                rhs_q = qc3[:, 4 * g:4 * g + 4, i * 128:(i + 1) * 128]
                p2 = (n * 4 + g) % 2
                e_expj = {}
                PS = pB if p2 == 0 else pA
                pt = pTs if p2 == 0 else pTs2
                JORD = (1, 2, 3, 0, 4) if n >= 4 else (0, 1, 2, 3, 4)
                for j in JORD:
                    kt = n + j
                    bk = (j % 2)
                    w = [e_prev_qrope]
                    if e_first_score:
                        w.append(e_prev_vcopy)
                        e_first_score = False
                    if e_tanh.get((p2, bk)) is not None:
                        w.append(e_tanh[(p2, bk)])
                    emm = top(lambda e, kt=kt, g=g, bk=bk, r=rhs_q, PS=PS: e.matmul(
                        PS[:, bk * 512:bk * 512 + 512],
                        kT[:, g * N_TOK + kt * 128:g * N_TOK + kt * 128 + 128], r,
                        start=True, stop=True), waits=w)
                    wt_ = [emm]
                    if n == 0 and g == 0 and j == 0:
                        wt_.append(e_a1_alldone)
                    if e_pT_use.get((p2, j)) is not None:
                        wt_.append(e_pT_use[(p2, j)])
                    et = sop(lambda e, bk=bk, j=j, PS=PS, pt=pt: e.activation(
                        out=pt[:, j * 512:j * 512 + 512],
                        in_=PS[:, bk * 512:bk * 512 + 512], func=AF.Tanh,
                        scale=float(1.0 / (CAP * float(np.sqrt(HD))))), waits=wt_)
                    e_tanh[(p2, bk)] = et
                    em = None
                    if n < 4:
                        em = vop(lambda e, j=j, pt=pt: e.tensor_tensor(
                            out=pt[:, j * 512:j * 512 + 512],
                            in0=pt[:, j * 512:j * 512 + 512],
                            in1=mask5[:, j * 512:j * 512 + 512], op=ALU.add),
                            waits=[et, ev_mask_prev])
                        e_mask_tt_last = em
                    elif j == 0:
                        em = vop(lambda e, pt=pt: e.tensor_tensor(
                            out=pt[:, 0:512], in0=pt[:, 0:512],
                            in1=mpers[:, 0:512], op=ALU.add), waits=[et, e_mp])
                    elif j == 4:
                        em = vop(lambda e, pt=pt: e.tensor_tensor(
                            out=pt[:, 4 * 512:5 * 512], in0=pt[:, 4 * 512:5 * 512],
                            in1=mpers[:, 512:1024], op=ALU.add), waits=[et, e_mp])
                    we_ = [em if em is not None else et, e_ncap]
                    ee = sop(lambda e, j=j, pt=pt: e.activation(
                        out=pt[:, j * 512:j * 512 + 512],
                        in_=pt[:, j * 512:j * 512 + 512],
                        func=AF.Exp, scale=float(CAP), bias=cst_ncap[:, 0:1]), waits=we_)
                    e_exp_prev = ee
                    e_expj[j] = ee
                e_sum_last = None
                e_pv_last = None
                for ji, j in enumerate(JORD):
                    kt = n + j
                    first, last = ji == 0, ji == 4
                    wsum = [e_expj[j]]
                    if first and e_sum_rd[p2] is None:
                        wsum.append(e_ones)
                    if first and e_sum_rd[p2] is not None:
                        wsum.append(e_sum_rd[p2])
                    e_sum_last = top(lambda e, j=j, PS=PS, pt=pt, f=first, l=last: e.matmul(
                        PS[:, 2 * 512:3 * 512], ones_bf[:, :],
                        pt[:, j * 512:j * 512 + 512],
                        start=f, stop=l), waits=wsum)
                    wpv = []
                    if first and e_pv_rd[p2] is not None:
                        wpv.append(e_pv_rd[p2])
                    e_pv_last = top(lambda e, j=j, kt=kt, g=g, PS=PS, pt=pt, f=first,
                                    l=last: e.matmul(
                        PS[:, 3 * 512:4 * 512],
                        vT[:, kt * 512 + g * 128:kt * 512 + g * 128 + 128],
                        pt[:, j * 512:j * 512 + 512],
                        start=f, stop=l), waits=wpv)
                    e_pT_use[(p2, j)] = e_pv_last
                e_attn_mm_last = e_pv_last
                w = [e_sum_last]
                if e_rinv_rd[p2] is not None:
                    w.append(e_rinv_rd[p2])
                el1 = vop(lambda e, PS=PS, p2=p2: e.reciprocal(
                    out=rinv2[p2][:, :], in_=PS[:, 2 * 512:3 * 512]), waits=w)
                e_sum_rd[p2] = el1
                w = [e_pv_last, el1]
                if ev_a3_attn_last is not None:
                    w.append(ev_a3_attn_last)
                if n == 0 and g == 0:
                    w.append(e_a1_alldone)
                if n == 14 and g == 0:
                    w.append(e_prep_sbuf_free)
                ea = vop(lambda e, g=g, n=n, PS=PS, p2=p2: e.tensor_tensor(
                    out=at4f[:, (n * 16 + 4 * g) * 128:(n * 16 + 4 * g + 4) * 128],
                    in0=PS[:, 3 * 512:4 * 512], in1=rinv2[p2][:, :], op=ALU.mult), waits=w)
                ev_a3_attn_last = ea
                e_pv_rd[p2] = ea
                e_rinv_rd[p2] = ea

    e_wot = []
    for kq in range(4):
        w = [e_wobf, e_q_mm_prev]
        if kq == 3:
            w.append(e_attn_mm_last)
        e_wot.append(dma("sync", "d_wo", woT_sb[:, kq * 4:(kq + 1) * 4, :],
                         woT_bf[kq * 512:(kq + 1) * 512, :]
                         .rearrange("(ck p) m -> p ck m", ck=4), waits=w))
    ev_x1 = []
    e_pa_free = None
    e_sq_rd = None
    e_ecp2_prev = None
    e_wo_mm = {}
    e_xn2_rd = None
    for n in range(T_OWN):
        mm = []
        for ck in range(16):
            for fq in range(4):
                w = []
                if ck == 0 and fq == 0:
                    w = [e_wot[-1], ev_a3_attn_last]
                    if e_pa_free is not None:
                        w.append(e_pa_free)
                mm.append(top(lambda e, ck=ck, fq=fq, n=n: e.matmul(
                    pA[:, fq * 512:(fq + 1) * 512],
                    at4[:, n, ck, :],
                    woT_sb[:, ck, fq * 512:(fq + 1) * 512],
                    start=(ck == 0), stop=(ck == 15)), waits=w))
        e_wo_mm[n] = mm[-1]
        o1 = o1b[n % 2]
        w = [mm[-1]]
        if n < 2:
            w.append(e_attn_mm_last)
        if n >= 2:
            w.append(ev_x1[n - 2][1])
        eo = sop(lambda e, o=o1: e.copy(out=o[:, :], in_=pA[:, :]), waits=w)
        e_pa_free = eo
        es1 = vop(lambda e, o=o1, n=n: e.reduce_sum(out=w1[:, n:n + 1], in_=o[:, :],
                                                    axis=AX.X), waits=[eo])
        wsq = [eo]
        if e_sq_rd is not None:
            wsq.append(e_sq_rd)
        if n == 0:
            wsq.append(e_attn_mm_last)
        esq = sop(lambda e, o=o1: e.activation(out=sq2[:, :], in_=o[:, :], func=AF.Square),
                  waits=wsq)
        es2 = vop(lambda e, n=n: e.reduce_sum(out=w2[:, n:n + 1], in_=sq2[:, :], axis=AX.X),
                  waits=[esq])
        e_sq_rd = es2
        em_ = vop(lambda e, n=n: e.tensor_scalar(
            out=wm[:, n:n + 1], in0=w1[:, n:n + 1], scalar1=float(1.0 / D), scalar2=None,
            op0=ALU.mult), waits=[es1])
        eq_ = vop(lambda e, n=n: e.tensor_tensor(
            out=wt[:, n:n + 1], in0=wm[:, n:n + 1], in1=wm[:, n:n + 1], op=ALU.mult),
            waits=[em_])
        ee_ = vop(lambda e, n=n: e.tensor_scalar(
            out=w2[:, n:n + 1], in0=w2[:, n:n + 1], scalar1=float(1.0 / D), scalar2=None,
            op0=ALU.mult), waits=[es2])
        ev_ = vop(lambda e, n=n: e.tensor_tensor(
            out=w2[:, n:n + 1], in0=w2[:, n:n + 1], in1=wt[:, n:n + 1], op=ALU.subtract),
            waits=[ee_, eq_])
        el_ = sop(lambda e, n=n: e.activation(out=wr[:, n:n + 1], in_=w2[:, n:n + 1],
                                              func=AF.Ln, bias=cst_eps[:, 0:1]), waits=[ev_])
        er_ = sop(lambda e, n=n: e.activation(out=wr[:, n:n + 1], in_=wr[:, n:n + 1],
                                              func=AF.Exp, scale=-0.5), waits=[el_])
        xb = a4x[n % 2]
        wx = [e_attn_mm_last] if n < 2 else [ev_x1[n - 2][3], ev_x1[n - 2][5]]
        ex = dma("sync", f"d_x{n % 2}", xb[:, :], x_in[512 + n * 128:512 + (n + 1) * 128, :],
                 waits=wx)
        et_ = vop(lambda e, o=o1, n=n: e.tensor_scalar(
            out=sq2[:, :], in0=o[:, :], scalar1=wm[:, n:n + 1], scalar2=wr[:, n:n + 1],
            op0=ALU.subtract, op1=ALU.mult), waits=[er_, es2, eo])
        ex1 = vop(lambda e, xb=xb: e.tensor_tensor(
            out=xb[:, :], in0=sq2[:, :], in1=xb[:, :], op=ALU.add), waits=[et_, ex, es1])
        es_ = dma("sync", f"st_x1{n % 2}", x1_scr[n * 128:(n + 1) * 128, :], xb[:, :],
                  waits=[ex1])
        eb1 = vop(lambda e, xb=xb, n=n: e.reduce_sum(
            out=m2a[:, n:n + 1], in_=xb[:, :], axis=AX.X), waits=[ex1])
        esqb = sop(lambda e, xb=xb: e.activation(out=sq2[:, :], in_=xb[:, :],
                                                 func=AF.Square), waits=[ex1])
        eb2 = vop(lambda e, n=n: e.reduce_sum(
            out=m2b[:, n:n + 1], in_=sq2[:, :], axis=AX.X), waits=[esqb])
        e_sq_rd = eb2
        eb3 = vop(lambda e, n=n: e.tensor_scalar(
            out=m2m[:, n:n + 1], in0=m2a[:, n:n + 1], scalar1=float(1.0 / D),
            scalar2=None, op0=ALU.mult), waits=[eb1])
        eb4 = vop(lambda e, n=n: e.tensor_tensor(
            out=m2t[:, n:n + 1], in0=m2m[:, n:n + 1], in1=m2m[:, n:n + 1], op=ALU.mult),
            waits=[eb3])
        eb5 = vop(lambda e, n=n: e.tensor_scalar(
            out=m2b[:, n:n + 1], in0=m2b[:, n:n + 1], scalar1=float(1.0 / D),
            scalar2=None, op0=ALU.mult), waits=[eb2])
        eb6 = vop(lambda e, n=n: e.tensor_tensor(
            out=m2b[:, n:n + 1], in0=m2b[:, n:n + 1], in1=m2t[:, n:n + 1],
            op=ALU.subtract), waits=[eb4, eb5])
        eb7 = sop(lambda e, n=n: e.activation(out=m2r[:, n:n + 1], in_=m2b[:, n:n + 1],
                                              func=AF.Ln, bias=cst_eps[:, 0:1]), waits=[eb6])
        eb8 = sop(lambda e, n=n: e.activation(out=m2r[:, n:n + 1], in_=m2r[:, n:n + 1],
                                              func=AF.Exp, scale=-0.5), waits=[eb7])
        wxn2 = [eb8, ex1, eb2]
        if e_xn2_rd is not None:
            wxn2.append(e_xn2_rd)
        if n == 0:
            wxn2.append(e_a1_alldone)
        exn2 = vop(lambda e, xb=xb, n=n: e.tensor_scalar(
            out=xn2_tile[:, :], in0=xb[:, :], scalar1=m2m[:, n:n + 1],
            scalar2=m2r[:, n:n + 1], op0=ALU.subtract, op1=ALU.mult), waits=wxn2)
        ea2 = vop(lambda e, n=n: e.tensor_reduce(
            out=am_x[:, n:n + 1], in_=xn2_tile[:, :], axis=AX.X, op=ALU.max,
            apply_absolute_value=True), waits=[exn2])
        wtp = [exn2]
        if n == 0:
            wtp.append(ev_a3_attn_last)
        else:
            wtp.append(e_ecp2_prev)
        etp = None
        for cc in range(16):
            etp = top(lambda e, cc=cc: e.matmul(
                pB[:, cc * 128:(cc + 1) * 128], xn2_tile[:, cc * 128:(cc + 1) * 128],
                identf[:, :], start=True, stop=True, is_transpose=True),
                waits=(wtp if cc == 0 else []))
        ecp2 = sop(lambda e, n=n: e.copy(
            out=at4[:, n, :, :],
            in_=pB[:, :].rearrange("p (cc m) -> p cc m", cc=16)),
            waits=[etp, e_wo_mm[n]])
        e_ecp2_prev = ecp2
        e_xn2_rd = ecp2
        ev_x1.append((ex, eo, ex1, es_, ea2, exn2))
    e_wo_last = e_wo_mm[T_OWN - 1]

    eax2 = vop(lambda e: e.tensor_reduce(
        out=am_x[:, 0:1], in_=am_x[:, 0:16], axis=AX.X, op=ALU.max),
        waits=[ev_x1[-1][4], ev_x1[-2][4]])
    e_gx = part_max(am_x[:, 0:1], 8, [eax2])
    e_arx = allred_max(1, 8, 9, [e_gx])
    e_sx = vop(lambda e: e.tensor_scalar(
        out=scp[0:1, 10:11], in0=scp[0:1, 9:10], scalar1=float(1.0 / FMAX), scalar2=1e-6,
        op0=ALU.mult, op1=ALU.max), waits=[e_arx])
    e_kx0 = vop(lambda e: e.reciprocal(out=scp[0:1, 11:12], in_=scp[0:1, 10:11]), waits=[e_sx])
    e_kx = vop(lambda e: e.tensor_scalar(
        out=scp[0:1, 11:12], in0=scp[0:1, 11:12], scalar1=0.5, scalar2=None, op0=ALU.mult),
        waits=[e_kx0])
    e_bkx = bcast(11, 0, [e_kx])
    e_s1a_ = vop(lambda e: e.tensor_tensor(
        out=scp[0:1, 12:13], in0=scp[0:1, 10:11], in1=scp[0:1, 2:3], op=ALU.mult),
        waits=[e_sx, e_swf])
    e_s1b_ = vop(lambda e: e.tensor_scalar(
        out=scp[0:1, 12:13], in0=scp[0:1, 12:13], scalar1=4.0, scalar2=None, op0=ALU.mult),
        waits=[e_s1a_])
    e_bs1 = bcast(12, 1, [e_s1b_])
    ev_qz = []
    for cc in range(16):
        w = [e_bkx, e_ecp2_prev]
        if cc < 2:
            w.append(e_wo_last)
        eqz = sop(lambda e, cc=cc: e.activation(
            out=xq8[:, cc, :].rearrange("p (n m) -> p n m", n=16),
            in_=at4[:, :, cc, :], func=AF.Copy, scale=bc[:, 0:1]), waits=w)
        ev_qz.append(eqz)
    e_qz_all = ev_qz[-1]

    xq8v = xq8[:, :, :]
    ev_f1 = []
    e_bank_free = {0: e_pa_free, 1: e_ecp2_prev}
    e_r_prev = {0: None, 1: None}
    e_amr0 = vop(lambda e: e.memset(am_r[:, :], 0.0), waits=[e_qz_all])
    e_amr_prev = e_amr0
    ev_wl = []
    e_f1_mm = {}
    for mc in range(64):
        w = [e_ag1]
        if mc < 2:
            w.append(e_wo_last)
        if mc >= 2:
            w.append(e_f1_mm[mc - 2])
        ewl = dma("sync", "d_wfc", f1w[mc % 2][:, :], wfc8[mc * 128:(mc + 1) * 128, :],
                  waits=w)
        ev_wl.append(ewl)
        ps = pA if (mc % 2 == 0) else pB
        emm = None
        for cc2 in range(8):
            lhs = f1w[mc % 2][:, cc2 * 256:(cc2 + 1) * 256].rearrange(
                "p (kk m) -> p kk m", kk=2)
            for tq in range(4):
                w2_ = []
                if cc2 == 0 and tq == 0:
                    w2_ = [ewl, e_qz_all]
                    if e_bank_free[mc % 2] is not None:
                        w2_.append(e_bank_free[mc % 2])
                emm = top(lambda e, ps=ps, lhs=lhs, cc2=cc2, tq=tq: e.matmul(
                    ps[:, tq * 512:(tq + 1) * 512], lhs,
                    xq8v[:, 2 * cc2:2 * cc2 + 2, tq * 512:(tq + 1) * 512],
                    start=(cc2 == 0), stop=(cc2 == 7), perf_mode=DR), waits=w2_)
        e_f1_mm[mc] = emm
        rb = f1r[mc % 2]
        w = [emm, e_bs1]
        if mc < 2:
            w.append(ev_x1[-1][3])
            w.append(ev_x1[-2][3])
            w.append(ev_x1[-1][5])
            w.append(ev_x1[-2][5])
        if mc == 0:
            w.append(e_p8_free)
        if e_r_prev[mc % 2] is not None:
            w.append(e_r_prev[mc % 2])
        ee = vop(lambda e, ps=ps, rb=rb: e.tensor_scalar(
            out=rb[:, :], in0=ps[:, :], scalar1=0.0, scalar2=bc[:, 1:2],
            op0=ALU.max, op1=ALU.mult), waits=w)
        e_bank_free[mc % 2] = ee
        ea = vop(lambda e, rb=rb: e.tensor_reduce(
            out=pam[:, 34:35], in_=rb[:, :], axis=AX.X, op=ALU.max),
            waits=[ee, e_amr_prev])
        eam = vop(lambda e: e.tensor_tensor(
            out=am_r[:, :], in0=am_r[:, :], in1=pam[:, 34:35], op=ALU.max), waits=[ea])
        e_amr_prev = eam
        es = dma("scalar", f"st_r{mc % 2}", r_spill[mc * 128:(mc + 1) * 128, :], rb[:, :],
                 waits=[ee])
        e_r_prev[mc % 2] = es
        ev_f1.append((emm, es, eam))
    e_f1_last = e_f1_mm[63]

    e_gr = part_max(am_r[:, 0:1], 16, [e_amr_prev])
    e_arr = allred_max(2, 16, 17, [e_gr])
    e_mh = vop(lambda e: e.tensor_tensor(
        out=scp[0:1, 18:19], in0=scp[0:1, 17:18], in1=scp[0:1, 17:18], op=ALU.mult),
        waits=[e_arr])
    e_sh = vop(lambda e: e.tensor_scalar(
        out=scp[0:1, 18:19], in0=scp[0:1, 18:19], scalar1=float(1.0 / FMAX), scalar2=1e-6,
        op0=ALU.mult, op1=ALU.max), waits=[e_mh])
    e_kh0 = vop(lambda e: e.reciprocal(out=scp[0:1, 19:20], in_=scp[0:1, 18:19]), waits=[e_sh])
    e_kh = vop(lambda e: e.tensor_scalar(
        out=scp[0:1, 19:20], in0=scp[0:1, 19:20], scalar1=0.5, scalar2=None, op0=ALU.mult),
        waits=[e_kh0])
    e_lk = sop(lambda e: e.activation(out=scp[0:1, 20:21], in_=scp[0:1, 19:20], func=AF.Ln),
               waits=[e_kh])
    e_sk = sop(lambda e: e.activation(out=scp[0:1, 20:21], in_=scp[0:1, 20:21], func=AF.Exp,
                                      scale=0.5), waits=[e_lk])
    e_bsk = bcast(20, 4, [e_sk])
    e_bk6 = bcast(19, 6, [e_kh, e_bsk])
    e_s2a = vop(lambda e: e.tensor_tensor(
        out=scp[0:1, 21:22], in0=scp[0:1, 18:19], in1=scp[0:1, 4:5], op=ALU.mult),
        waits=[e_sh, e_swp])
    e_s2b = vop(lambda e: e.tensor_scalar(
        out=scp[0:1, 21:22], in0=scp[0:1, 21:22], scalar1=4.0, scalar2=None, op0=ALU.mult),
        waits=[e_s2a])
    e_bs2 = bcast(21, 5, [e_s2b])

    wp8v = wp8[:, :].rearrange("(kc q p) n -> kc q p n", kc=32, q=4)
    h8v = h8[:, :, :, :]
    ev_y = []
    e_h8_mm_done = None
    e_o2_free = {}
    e_pq_free = {t: None for t in range(8)}
    e_sq2_rd = None
    e_f2t_rd = None
    e_eq8 = {}
    e_wp_mm = {}
    e_r_ld = {0: None, 1: None}
    for blk in range(2):
        for mc in range(64):
            w = [ev_f1[mc][1]]
            if e_r_ld[mc % 2] is not None:
                w.append(e_r_ld[mc % 2])
            elr = dma("sync", f"d_r{mc % 2}", f2r[mc % 2][:, :],
                      r_spill[mc * 128:(mc + 1) * 128, blk * 1024:(blk + 1) * 1024],
                      waits=w)
            wq_ = [elr, e_bsk, e_bk6]
            if mc < 2:
                wq_.append(e_qz_all if blk == 0 else e_h8_mm_done)
            if mc % 2 == 0:
                eq8 = sop(lambda e, mc=mc: e.activation(
                    out=h8v[:, mc // 2, mc % 2, :], in_=f2r[mc % 2][:, :], func=AF.Square,
                    scale=bc[:, 4:5]), waits=wq_)
            else:
                eq8 = vop(lambda e, mc=mc: e.scalar_tensor_tensor(
                    out=h8v[:, mc // 2, mc % 2, :], in0=f2r[mc % 2][:, :],
                    scalar=bc[:, 6:7], in1=f2r[mc % 2][:, :],
                    op0=ALU.mult, op1=ALU.mult), waits=wq_)
            e_r_ld[mc % 2] = eq8
            e_eq8[mc] = eq8
        ev_o2 = {}
        for q in range(4):
            for kc2 in range(32):
                w = [e_ag2]
                if blk == 0 and q == 0 and kc2 < 2:
                    w.append(e_qz_all)
                    w.append(ev_x1[-1][3])
                    w.append(ev_x1[-1][5])
                prev = e_wp_mm.get((q, kc2 - 2)) if kc2 >= 2 else (
                    e_wp_mm.get((q - 1, 30 + kc2)) if q > 0 else
                    (e_wp_mm.get((3, 30 + kc2)) if blk > 0 else None))
                if prev is not None:
                    w.append(prev)
                ewp = dma("sync", f"d_wp{kc2 % 2}", f2w[kc2 % 2][:, :],
                          wp8v[kc2, q, :, :], waits=w)
                rhs = f2w[kc2 % 2][:, :].rearrange("p (kk n) -> p kk n", kk=2)
                emm = None
                for t in range(8):
                    ps = pA if t < 4 else pB
                    w2_ = []
                    if t == 0:
                        w2_ = [ewp, e_eq8[2 * kc2 + 1], e_eq8[min(2 * kc2, 63)]]
                    if kc2 == 0:
                        if q == 0 and blk == 0:
                            if t < 4:
                                w2_.append(e_bank_free[0])
                            else:
                                w2_.append(e_bank_free[1])
                            w2_.append(e_f1_last)
                        elif e_pq_free[t] is not None:
                            w2_.append(e_pq_free[t])
                    emm = top(lambda e, ps=ps, t=t, kc2=kc2, rhs=rhs: e.matmul(
                        ps[:, (t % 4) * 512:(t % 4 + 1) * 512],
                        h8v[:, kc2, :, t * 128:(t + 1) * 128], rhs,
                        start=(kc2 == 0), stop=(kc2 == 31), perf_mode=DR), waits=w2_)
                    e_wp_mm[(q, kc2)] = emm
            for t in range(8):
                ps = pA if t < 4 else pB
                w = [e_wp_mm[(q, 31)], e_bs2]
                if q == 0:
                    w.append(e_f1_last)
                    if blk > 0:
                        w.append(e_o2_free[t])
                ev_o2[(q, t)] = vop(lambda e, ps=ps, t=t, q=q: e.tensor_scalar(
                    out=o2[:, t, q * 512:(q + 1) * 512],
                    in0=ps[:, (t % 4) * 512:(t % 4 + 1) * 512], scalar1=bc[:, 5:6],
                    scalar2=None, op0=ALU.mult), waits=w)
                e_pq_free[t] = ev_o2[(q, t)]
        e_h8_mm_done = e_wp_mm[(3, 31)]
        for t in range(8):
            n = blk * 8 + t
            strip = o2[:, t, :]
            w = [ev_o2[(q, t)] for q in range(4)]
            es1_ = vop(lambda e, s=strip, n=n: e.reduce_sum(
                out=f3a[:, n:n + 1], in_=s, axis=AX.X), waits=w)
            wsq = list(w)
            if e_sq2_rd is not None:
                wsq.append(e_sq2_rd)
            if e_f2t_rd is not None:
                wsq.append(e_f2t_rd)
            if blk == 0 and t == 0:
                wsq.append(e_qz_all)
                wsq.append(ev_f1[-1][2])
            esq_ = sop(lambda e, s=strip: e.activation(out=f2t[:, :], in_=s, func=AF.Square),
                       waits=wsq)
            es2_ = vop(lambda e, n=n: e.reduce_sum(
                out=f3b[:, n:n + 1], in_=f2t[:, :], axis=AX.X), waits=[esq_])
            e_sq2_rd = es2_
            em_ = vop(lambda e, n=n: e.tensor_scalar(
                out=f3m[:, n:n + 1], in0=f3a[:, n:n + 1], scalar1=float(1.0 / D),
                scalar2=None, op0=ALU.mult), waits=[es1_])
            eqq = vop(lambda e, n=n: e.tensor_tensor(
                out=f3t[:, n:n + 1], in0=f3m[:, n:n + 1], in1=f3m[:, n:n + 1], op=ALU.mult),
                waits=[em_])
            eee = vop(lambda e, n=n: e.tensor_scalar(
                out=f3b[:, n:n + 1], in0=f3b[:, n:n + 1], scalar1=float(1.0 / D),
                scalar2=None, op0=ALU.mult), waits=[es2_])
            evv = vop(lambda e, n=n: e.tensor_tensor(
                out=f3b[:, n:n + 1], in0=f3b[:, n:n + 1], in1=f3t[:, n:n + 1],
                op=ALU.subtract), waits=[eqq, eee])
            ell = sop(lambda e, n=n: e.activation(out=f3r[:, n:n + 1], in_=f3b[:, n:n + 1],
                                                  func=AF.Ln, bias=cst_eps[:, 0:1]),
                      waits=[evv])
            err = sop(lambda e, n=n: e.activation(out=f3r[:, n:n + 1], in_=f3r[:, n:n + 1],
                                                  func=AF.Exp, scale=-0.5), waits=[ell])
            xb = f2x[t % 2]
            wx = [ev_x1[n][3], e_qz_all]
            if len(ev_y) >= 2:
                wx.append(ev_y[-2])
            ex = dma("sync", f"d_x{t % 2}", xb[:, :], x1_scr[n * 128:(n + 1) * 128, :],
                     waits=wx)
            wty = [err, es2_]
            if e_f2t_rd is not None:
                wty.append(e_f2t_rd)
            ety = vop(lambda e, s=strip, n=n: e.tensor_scalar(
                out=f2t[:, :], in0=s, scalar1=f3m[:, n:n + 1], scalar2=f3r[:, n:n + 1],
                op0=ALU.subtract, op1=ALU.mult), waits=wty)
            e_o2_free[t] = ety
            eyy = vop(lambda e, xb=xb: e.tensor_tensor(
                out=xb[:, :], in0=f2t[:, :], in1=xb[:, :], op=ALU.add), waits=[ety, ex])
            e_f2t_rd = eyy
            est = dma("sync", f"st_y{t % 2}", y_out[n * 128:(n + 1) * 128, :], xb[:, :],
                      waits=[eyy])
            ev_y.append(est)

    v0, v1 = sch.cnt["st_y0"], sch.cnt["st_y1"]
    sch.do("sync", lambda e, a=v0, b=v1: (e.wait_ge("st_y0", a), e.wait_ge("st_y1", b)),
           waits=(), inc=None)
    return nc, sch


class _Wrap:
    def __init__(self, eng, sems):
        self._e = eng
        self._s = sems

    def __getattr__(self, k):
        return getattr(self._e, k)

    def wait_ge(self, s, v):
        if isinstance(s, str):
            s = self._s[s]
        return self._e.wait_ge(s, v)


def _emit(nc, sch):
    with contextlib.ExitStack() as es:
        sems = {name: es.enter_context(nc.semaphore(name)) for name in sch.cnt}
        block = es.enter_context(nc.Block())

        def make(eng_name):
            ops = sch.seq[eng_name]

            def _f(e):
                waited = {}
                wrap = _Wrap(e, sems)
                for w, fn, inc, amt in ops:
                    for s, v in w.items():
                        if waited.get(s, 0) < v:
                            e.wait_ge(sems[s], v)
                            waited[s] = v
                    ins = fn(wrap)
                    if inc is not None:
                        if amt != 1:
                            ins.then_inc(sems[inc], amt)
                        else:
                            ins.then_inc(sems[inc])
            return _f

        block.sync(make("sync"))
        block.scalar(make("scalar"))
        block.vector(make("vector"))
        block.tensor(make("tensor"))
        block.gpsimd(make("gpsimd"))
    return nc


def _host_tables(s0):
    inv = (1.0 / (np.float32(THETA) ** (np.arange(0, HD, 2, dtype=np.float32)
                                        / np.float32(HD)))).astype(np.float32)
    pos = (np.arange(N_TOK, dtype=np.float32) + np.float32(s0 - HALO))
    freqs = pos[None, :] * inv[:, None]
    ang = np.concatenate([freqs, freqs], axis=0)
    cos = np.cos(ang).astype(np.float32)
    sin = np.sin(ang).astype(np.float32)
    sin[0:64, :] *= np.float32(-1.0)
    return np.ascontiguousarray(cos), np.ascontiguousarray(sin)


def _host_masks(s0):
    out = np.zeros((22, HD, 512), dtype=np.float32)
    cc = np.arange(128)[:, None]
    r = np.arange(128)[None, :]
    m0 = np.where(cc > r, 0.0, NEG)
    m4 = np.where(cc <= r, 0.0, NEG)
    out[20] = np.tile(m0, (1, 4))
    out[21] = np.tile(m4, (1, 4))
    for n in range(4):
        for j in range(5):
            i = s0 + 128 * n + r
            jg = s0 - HALO + 128 * (n + j) + cc
            allowed = (jg <= i) & (jg > i - WIN) & (jg >= 0)
            out[n * 5 + j] = np.tile(np.where(allowed, 0.0, NEG), (1, 4))
    return out.astype(ml_dtypes.bfloat16)


_NC_CACHE = None


def kernel(L0_attn_hidden_states, L0_attn_wq, L0_attn_wk, L0_attn_wv, L0_attn_wo,
           L0_ffn_w_fc, L0_ffn_w_proj):
    global _NC_CACHE
    x = np.ascontiguousarray(L0_attn_hidden_states, dtype=np.float32)
    wq = np.asarray(L0_attn_wq, dtype=np.float32)
    wk = np.asarray(L0_attn_wk, dtype=np.float32)
    wv = np.asarray(L0_attn_wv, dtype=np.float32)
    wo = np.asarray(L0_attn_wo, dtype=np.float32)
    wfc = np.asarray(L0_ffn_w_fc, dtype=np.float32)
    wproj = np.asarray(L0_ffn_w_proj, dtype=np.float32)

    wqT = np.ascontiguousarray(wq.T)
    wkT = np.ascontiguousarray(wk.T)
    wvT = np.ascontiguousarray(wv.T)
    woT = np.ascontiguousarray(wo.T)
    wfc_perm = np.ascontiguousarray(
        wfc.reshape(64, 128, 8, 2, 128).transpose(0, 4, 2, 3, 1).reshape(64, 128, D))
    wp_perm = np.ascontiguousarray(
        wproj.reshape(4, 512, 32, 2, 128).transpose(2, 0, 4, 3, 1).reshape(32, 4, 128, 1024))

    if _NC_CACHE is None:
        nc, sch = build_nc()
        _emit(nc, sch)
        _NC_CACHE = nc
    nc = _NC_CACHE

    ones_bf = np.ones((HD, HD), dtype=ml_dtypes.bfloat16)
    ident_bf = np.eye(HD, dtype=ml_dtypes.bfloat16)
    in_maps = []
    for c in range(8):
        b, half = c // 2, c % 2
        s0 = half * N_OWN
        xp = np.zeros((N_TOK, D), dtype=np.float32)
        lo = max(0, s0 - HALO)
        xp[HALO - (s0 - lo):, :] = x[b, lo:s0 + N_OWN, :]
        cos, sin = _host_tables(s0)
        in_maps.append({
            "x": xp,
            "cosT": cos, "sinT": sin,
            "masks": _host_masks(s0),
            "ones_in": ones_bf, "ident_in": ident_bf,
            "wqT": wqT, "wkT": wkT, "wvT": wvT, "woT": woT,
            "wfc_p": np.ascontiguousarray(wfc_perm[c * 8:(c + 1) * 8]),
            "wp_p": np.ascontiguousarray(wp_perm[c * 4:(c + 1) * 4]),
        })
    res = run_bass_kernel_spmd(nc, in_maps, core_ids=list(range(8)))
    out = np.empty((B, S, D), dtype=np.float32)
    for c in range(8):
        b, half = c // 2, c % 2
        out[b, half * N_OWN:(half + 1) * N_OWN, :] = res.results[c]["y"]
    return out


# version 12
# speedup vs baseline: 1.0096x; 1.0096x over previous
import numpy as np
import ml_dtypes
import contextlib
import concourse.bass as bass
import concourse.mybir as mybir
from concourse.bass_utils import run_bass_kernel_spmd

F32 = mybir.dt.float32
BF16 = mybir.dt.bfloat16
FP8 = mybir.dt.float8e4
AF = mybir.ActivationFunctionType
ALU = mybir.AluOpType
AX = mybir.AxisListType
DR = mybir.MatmulPerfMode.DoubleRow

B, S, D = 4, 4096, 2048
HQ, HKV, HD = 16, 4, 128
WIN = 512
CAP = 50.0
THETA = 10000.0
EPS = 1e-5
N_OWN, HALO, N_TOK = 2048, 512, 2560
T_OWN, T_ALL = 16, 20
NCH = 4
FMAX = 448.0
NEG = -1e30

ENGINES = ("sync", "scalar", "vector", "tensor", "gpsimd")


class Sched:
    def __init__(self):
        self.seq = {e: [] for e in ENGINES}
        self.cnt = {}
        self.known = set()

    def new_sem(self, name):
        self.cnt[name] = 0

    def do(self, eng, fn, waits=(), inc=None, amt=1):
        w = {}
        for e in waits:
            if e is None:
                continue
            s, v = e
            assert (s, v) in self.known, f"wait on future/unknown event {s}:{v}"
            w[s] = max(w.get(s, 0), v)
        self.seq[eng].append((w, fn, inc, amt))
        if inc is not None:
            self.cnt[inc] += amt
            e = (inc, self.cnt[inc])
            self.known.add(e)
            return e
        return None


def build_nc():
    nc = bass.Bass("TRN2", target_bir_lowering=False)
    sch = Sched()

    x_in = nc.dram_tensor("x", [N_TOK, D], F32, kind="ExternalInput")
    cosT_d = nc.dram_tensor("cosT", [HD, N_TOK], F32, kind="ExternalInput")
    sinT_d = nc.dram_tensor("sinT", [HD, N_TOK], F32, kind="ExternalInput")
    masks_d = nc.dram_tensor("masks", [22, HD, 512], BF16, kind="ExternalInput")
    ones_d = nc.dram_tensor("ones_in", [HD, HD], BF16, kind="ExternalInput")
    ident_d = nc.dram_tensor("ident_in", [HD, HD], BF16, kind="ExternalInput")
    wqT_d = nc.dram_tensor("wqT", [D, D], F32, kind="ExternalInput")
    wkT_d = nc.dram_tensor("wkT", [D, 512], F32, kind="ExternalInput")
    wvT_d = nc.dram_tensor("wvT", [D, 512], F32, kind="ExternalInput")
    woT_d = nc.dram_tensor("woT", [D, D], F32, kind="ExternalInput")
    wfc_p = nc.dram_tensor("wfc_p", [8, 128, D], F32, kind="ExternalInput")
    wp_p = nc.dram_tensor("wp_p", [4, 4, 128, 1024], F32, kind="ExternalInput")
    y_out = nc.dram_tensor("y", [N_OWN, D], F32, kind="ExternalOutput")

    wqT_bf = nc.dram_tensor("wqT_bf", [D, D], BF16)
    wkT_bf = nc.dram_tensor("wkT_bf", [D, 512], BF16)
    wvT_bf = nc.dram_tensor("wvT_bf", [D, 512], BF16)
    woT_bf = nc.dram_tensor("woT_bf", [D, D], BF16)
    wfc8_ag = nc.dram_tensor("wfc8_ag", [8 * 128, D], FP8)
    wfc8 = nc.dram_tensor("wfc8", [64 * 128, D], FP8)
    wp8_ag = nc.dram_tensor("wp8_ag", [16 * 128, 1024], FP8)
    wp8 = nc.dram_tensor("wp8", [128 * 128, 1024], FP8)
    xnT_scr = nc.dram_tensor("xnT_scr", [16, 128, N_TOK], BF16)
    x1_scr = nc.dram_tensor("x1_scr", [N_OWN, D], F32)
    r_spill = nc.dram_tensor("r_spill", [4 * D, N_OWN], BF16)
    cc_in = [nc.dram_tensor(f"cc{i}_in", [1, 8], F32) for i in range(3)]
    cc_out = [nc.dram_tensor(f"cc{i}_out", [1, 8], F32) for i in range(3)]
    col_scr = nc.dram_tensor("col_scr", [HD, 1], F32)
    row_scr = nc.dram_tensor("row_scr", [HD, 1], F32)

    SB_BASE = 16512
    def sb(name, shape, dtype, off_kb):
        return nc.alloc_sbuf_tensor_at(name, shape, dtype, offset=SB_BASE + int(off_kb * 1024))

    _cur = [SB_BASE]

    def sbp(name, shape, dtype):
        ds = {F32: 4, BF16: 2, FP8: 1}[dtype]
        n = 1
        for s_ in shape[1:]:
            n *= s_
        t = nc.alloc_sbuf_tensor_at(name, shape, dtype, offset=_cur[0])
        _cur[0] += -(-n * ds // 32) * 32
        assert _cur[0] <= SB_BASE + 6 * 1024
        return t

    ones_bf = sbp("ones_bf", [HD, HD], BF16)
    ident = sbp("ident", [HD, HD], BF16)
    identf = sbp("identf", [HD, HD], F32)
    onesrow = sbp("onesrow", [1, 128], F32)
    scp = sbp("scp", [1, 64], F32)
    bc = sbp("bc", [HD, 8], F32)
    grow = sbp("grow", [1, 128], F32)
    pam = sbp("pam", [HD, 40], F32)
    s1a = sbp("s1a", [HD, 24], F32)
    s1b = sbp("s1b", [HD, 24], F32)
    s1m = sbp("s1m", [HD, 24], F32)
    s1r = sbp("s1r", [HD, 24], F32)
    s1t = sbp("s1t", [HD, 24], F32)
    w1 = sbp("wst1", [HD, 16], F32)
    w2 = sbp("wst2", [HD, 16], F32)
    wm = sbp("wstm", [HD, 16], F32)
    wr = sbp("wstr", [HD, 16], F32)
    wt = sbp("wstt", [HD, 16], F32)
    m2a = sbp("m2a", [HD, 16], F32)
    m2b = sbp("m2b", [HD, 16], F32)
    m2m = sbp("m2m", [HD, 16], F32)
    m2r = sbp("m2r", [HD, 16], F32)
    m2t = sbp("m2t", [HD, 16], F32)
    ntb = sbp("ntb", [HD, 16], F32)
    ntb2 = sbp("ntb2", [HD, 16], F32)
    am_x = sbp("am_x", [HD, 16], F32)
    am_r = sbp("am_r", [HD, 1], F32)
    f3a = sbp("f3a", [HD, 16], F32)
    f3b = sbp("f3b", [HD, 16], F32)
    f3m = sbp("f3m", [HD, 16], F32)
    f3r = sbp("f3r", [HD, 16], F32)
    f3t = sbp("f3t", [HD, 16], F32)
    cst_eps = sbp("cst_eps", [HD, 1], F32)
    cst_ncap = sbp("cst_ncap", [HD, 1], F32)

    ax = [sb("ax0", [HD, D], F32, 6), sb("ax1", [HD, D], F32, 14)]
    xnc = [sb("xnc0", [HD, 16, 512], BF16, 22), sb("xnc1", [HD, 16, 512], BF16, 38)]
    xn_tile = sb("xn_tile", [HD, D], F32, 54)
    at3 = sb("at3", [HD, 16, 16, 128], BF16, 6)
    h8 = sb("h8", [HD, 32, 2, 1024], FP8, 6)
    pwb = [sb("pw0", [HD, 1024], F32, 62), sb("pw1", [HD, 1024], F32, 66)]
    p8 = sb("p8", [HD, 1024], FP8, 134)
    wqs = sb("wqs", [HD, 2, 16, 512], BF16, 70)
    xnT_sb = sb("xnT_sb", [HD, 16, 512], BF16, 102)
    qc = sb("qc", [HD, 16 * 512], BF16, 118)
    xn_tile_b = sb("xn_tile_b", [HD, D], F32, 118)
    woT_sb = sb("woT_sb", [HD, 16, D], BF16, 70)
    xq8 = sb("xq8", [HD, 16, N_OWN], FP8, 70)
    f1w = [sb("f1w0", [HD, D], FP8, 102), sb("f1w1", [HD, D], FP8, 104)]
    o2 = sb("o2", [HD, 8, D], F32, 70)
    kT = sb("kT", [HD, HKV * N_TOK], BF16, 134)
    vT = sb("vT", [HD, T_ALL * 512], BF16, 154)
    a4x = [sb("a4x0", [HD, D], F32, 134), sb("a4x1", [HD, D], F32, 142)]
    sq2 = sb("sq2", [HD, D], F32, 150)
    o1b = [sb("o10", [HD, D], F32, 158), sb("o11", [HD, D], F32, 166)]
    f1r = [sb("f1r0", [HD, D], BF16, 134), sb("f1r1", [HD, D], BF16, 138)]
    f2w = [sb("f2w0", [HD, 1024], FP8, 142), sb("f2w1", [HD, 1024], FP8, 143)]
    f2r = [sb("f2r0", [HD, 1024], BF16, 144), sb("f2r1", [HD, 1024], BF16, 146)]
    f2x = [sb("f2x0", [HD, D], F32, 148), sb("f2x1", [HD, D], F32, 156)]
    f2t = sb("f2t", [HD, D], F32, 164)
    tabc = sb("tabc", [HD, 2 * 1024], F32, 174)
    sq1 = sb("sq1", [HD, D], F32, 182)
    mask5 = sb("mask5", [HD, 5 * 512], BF16, 182)
    mpers = sb("mpers", [HD, 2 * 512], BF16, 187)
    pTs = sb("pTs", [HD, 5 * 512], BF16, 189)
    ropa = sb("ropa", [HD, 512], F32, 194)
    ropb = sb("ropb", [HD, 512], F32, 196)
    rinv2 = [sb("rinv", [HD, 512], F32, 198), sb("rinvB", [HD, 512], F32, 205)]
    pTs2 = sb("pTsB", [HD, 5 * 512], BF16, 200)
    xn2_tile = sb("xn2_tile", [HD, D], F32, 182)

    pA = nc.alloc_psum_tensor("pA", [HD, D], F32)
    pB = nc.alloc_psum_tensor("pB", [HD, D], F32)

    SEM_NAMES = ("g_d", "s_d", "cc", "te", "ve", "se", "ge",
                 "d_x0", "d_x1", "st_xnt", "d_xnt", "d_wkv",
                 "d_wq0", "d_wq1", "d_wo", "d_wfc", "d_wp0", "d_wp1",
                 "d_r0", "d_r1", "st_r0", "st_r1", "st_x10", "st_x11",
                 "st_y0", "st_y1", "d_tab0", "d_tab1", "d_mask")
    CHAINQ = {"gpsimd": "g_d", "scalar": "s_d"}
    _chain_last = {}
    for s in SEM_NAMES:
        sch.new_sem(s)

    def dma(q, sem, out, in_, waits=()):
        waits = [w for w in waits if w is not None]
        if q in CHAINQ:
            sem = CHAINQ[q]
            if sem in _chain_last:
                waits = waits + [_chain_last[sem]]
        ev = sch.do(q, lambda e, o=out, i=in_: e.dma_start(out=o, in_=i),
                    waits=waits, inc=sem, amt=16)
        if q in CHAINQ:
            _chain_last[sem] = ev
        return ev

    def vop(fn, waits=()):
        return sch.do("vector", fn, waits=waits, inc="ve")

    def sop(fn, waits=()):
        return sch.do("scalar", fn, waits=waits, inc="se")

    def top(fn, waits=()):
        return sch.do("tensor", fn, waits=waits, inc="te")

    def gop(fn, waits=()):
        return sch.do("gpsimd", fn, waits=waits, inc="ge")

    def coll(kind, op, i_, o_, waits):
        ev = sch.do("gpsimd", lambda e, k=kind, o=op, a=i_, b=o_: e.collective_compute(
            k, o, replica_groups=[list(range(8))], ins=[a], outs=[b]),
            waits=waits + ([_chain_last["g_d"]] if "g_d" in _chain_last else []),
            inc="cc", amt=1)
        return ev

    e_grow = [None]

    def part_max(src_col, slot, waits):
        w = list(waits) + ([e_grow[0]] if e_grow[0] else [])
        e1 = dma("gpsimd", "g_d", col_scr[:, :], src_col, waits=w)
        e2 = dma("gpsimd", "g_d", grow[0:1, 0:128], col_scr[:, :], waits=[e1])
        e3 = vop(lambda e, sl=slot: e.tensor_reduce(
            out=scp[0:1, sl:sl + 1], in_=grow[0:1, 0:128], axis=AX.X, op=ALU.max),
            waits=[e2])
        e_grow[0] = e3
        return e3

    def bcast(slot, col, waits):
        w = list(waits) + ([e_grow[0]] if e_grow[0] else [])
        e1 = vop(lambda e, sl=slot: e.tensor_scalar(
            out=grow[0:1, 0:128], in0=onesrow[0:1, 0:128],
            scalar1=scp[0:1, sl:sl + 1], scalar2=None, op0=ALU.mult), waits=w)
        e2 = dma("gpsimd", "g_d", row_scr[:, :], grow[0:1, 0:128], waits=[e1])
        e3 = dma("gpsimd", "g_d", bc[:, col:col + 1], row_scr[:, :], waits=[e2])
        e_grow[0] = e3
        return e3

    def allred_max(idx, slot_in, slot_out, waits):
        w = list(waits) + ([e_grow[0]] if e_grow[0] else [])
        e1 = vop(lambda e, sl=slot_in: e.tensor_scalar(
            out=grow[0:1, 0:8], in0=onesrow[0:1, 0:8],
            scalar1=scp[0:1, sl:sl + 1], scalar2=None, op0=ALU.mult), waits=w)
        e2 = dma("gpsimd", "g_d", cc_in[idx][:, :], grow[0:1, 0:8], waits=[e1])
        e3 = coll("AllReduce", ALU.max, cc_in[idx][:, :], cc_out[idx][:, :], waits=[e2])
        e4 = dma("gpsimd", "g_d", grow[0:1, 0:8], cc_out[idx][:, :], waits=[e3])
        e5 = vop(lambda e, sl=slot_out: e.tensor_copy(
            out=scp[0:1, sl:sl + 1], in_=grow[0:1, 0:1]), waits=[e4])
        e_grow[0] = e5
        return e5

    e_ones = dma("gpsimd", "g_d", ones_bf[:, :], ones_d[:, :])
    e_id0 = dma("gpsimd", "g_d", ident[:, :], ident_d[:, :])
    e_ident = vop(lambda e: e.tensor_copy(out=identf[:, :], in_=ident[:, :]),
                  waits=[e_id0])
    e_or = gop(lambda e: e.memset(onesrow[0:1, 0:128], 1.0))
    e_ceps = gop(lambda e: e.memset(cst_eps[:, :], float(EPS)), waits=[e_or])
    e_ncap = gop(lambda e: e.memset(cst_ncap[:, :], float(-8.0)), waits=[e_ceps])

    e_wkbf = dma("gpsimd", "g_d", wkT_bf[:, :], wkT_d[:, :])
    e_wvbf = dma("gpsimd", "g_d", wvT_bf[:, :], wvT_d[:, :])
    e_wqbf = []
    for hq in range(4):
        e_wqbf.append(dma("gpsimd", "g_d", wqT_bf[:, hq * 512:(hq + 1) * 512],
                          wqT_d[:, hq * 512:(hq + 1) * 512]))
    e_wobf = dma("gpsimd", "g_d", woT_bf[:, :], woT_d[:, :])

    def prep_src(j):
        if j < 16:
            return wfc_p[j // 2, :, (j % 2) * 1024:(j % 2 + 1) * 1024]
        i, q = (j - 16) // 4, (j - 16) % 4
        return wp_p[i, q, :, :]

    def prep_dst(j):
        if j < 16:
            return wfc8_ag[(j // 2) * 128:(j // 2 + 1) * 128,
                           (j % 2) * 1024:(j % 2 + 1) * 1024]
        i, q = (j - 16) // 4, (j - 16) % 4
        base = (i * 4 + q) * 128
        return wp8_ag[base:base + 128, :]

    prep_state = {"red": {}, "last_red": None}

    def emit_amax(j):
        w = []
        if j >= 2:
            w.append(prep_state["red"][j - 2])
        el = dma("gpsimd", "g_d", pwb[j % 2][:, :], prep_src(j), waits=w)
        er = vop(lambda e, j=j: e.tensor_reduce(
            out=pam[:, j:j + 1], in_=pwb[j % 2][:, :], axis=AX.X, op=ALU.max,
            apply_absolute_value=True), waits=[el])
        prep_state["red"][j] = er
        prep_state["last_red"] = er

    at4 = at3[:, :, :, :]
    wkv3 = wqs[:, :, :, :].rearrange("p a c m -> p (a c) m")
    e_wkv = dma("sync", "d_wkv", wkv3[:, 0:16, :],
                wkT_bf[:, :].rearrange("(cc p) m -> p cc m", cc=16), waits=[e_wkbf])
    e_wvv = dma("sync", "d_wkv", wkv3[:, 16:32, :],
                wvT_bf[:, :].rearrange("(cc p) m -> p cc m", cc=16), waits=[e_wvbf])

    def rope_evac(pt, pcol, tb, out_ap, waits):
        ea = vop(lambda e: e.tensor_tensor(
            out=ropa[:, :], in0=pt[0:128, pcol:pcol + 512],
            in1=tabc[0:128, tb:tb + 512], op=ALU.mult), waits=waits)
        eb = vop(lambda e: e.tensor_tensor(
            out=ropb[0:64, :], in0=pt[64:128, pcol:pcol + 512],
            in1=tabc[0:64, tb + 512:tb + 1024], op=ALU.mult), waits=[ea])
        ec = vop(lambda e: e.tensor_tensor(
            out=ropb[64:128, :], in0=pt[0:64, pcol:pcol + 512],
            in1=tabc[64:128, tb + 512:tb + 1024], op=ALU.mult), waits=[eb])
        ed = vop(lambda e: e.tensor_tensor(
            out=out_ap, in0=ropa[:, :], in1=ropb[:, :], op=ALU.add), waits=[ec])
        return ed

    ev_tab_prev = None
    e_prev_kmm = None
    e_prev_rope = None
    e_prev_vcopy = None
    ev_xload = {}
    ev_xn_ecp = {}
    ev_xnc_store = {}
    e_exn = {}
    ev_sq = {}
    e_tp_evac = {"A": None, "B": None}
    e_sq1_rd = None
    e_ln_prev = None
    for c in range(5):
        c0 = c * 4
        for tt in range(4):
            t = c0 + tt
            w = []
            if t >= 2:
                w.append(e_exn.get(t - 2))
                w.append(ev_sq.get(t - 2))
            el = dma("sync", f"d_x{t % 2}", ax[t % 2][:, :], x_in[t * 128:(t + 1) * 128, :],
                     waits=w)
            ev_xload[t] = el
            e1 = vop(lambda e, b=ax[t % 2], t=t: e.reduce_sum(
                out=s1a[:, t:t + 1], in_=b[:, :], axis=AX.X), waits=[el])
            wsq = [el]
            if e_sq1_rd is not None:
                wsq.append(e_sq1_rd)
            esq = sop(lambda e, b=ax[t % 2]: e.activation(out=sq1[:, :], in_=b[:, :],
                                                          func=AF.Square), waits=wsq)
            e2 = vop(lambda e, t=t: e.reduce_sum(out=s1b[:, t:t + 1], in_=sq1[:, :],
                                                 axis=AX.X), waits=[esq])
            e_sq1_rd = e2
            ev_sq[t] = e2
            eM = vop(lambda e, t=t: e.tensor_scalar(
                out=s1m[:, t:t + 1], in0=s1a[:, t:t + 1], scalar1=float(1.0 / D),
                scalar2=None, op0=ALU.mult), waits=[e1])
            eE = vop(lambda e, t=t: e.tensor_scalar(
                out=s1b[:, t:t + 1], in0=s1b[:, t:t + 1], scalar1=float(1.0 / D),
                scalar2=None, op0=ALU.mult), waits=[e2])
            eQ = vop(lambda e, t=t: e.tensor_tensor(
                out=s1t[:, t:t + 1], in0=s1m[:, t:t + 1], in1=s1m[:, t:t + 1],
                op=ALU.mult), waits=[eM])
            eV = vop(lambda e, t=t: e.tensor_tensor(
                out=s1b[:, t:t + 1], in0=s1b[:, t:t + 1], in1=s1t[:, t:t + 1],
                op=ALU.subtract), waits=[eE, eQ])
            wl_ = [eV, e_ceps]
            if e_ln_prev is not None:
                wl_.append(e_ln_prev)
            eL = sop(lambda e, t=t: e.activation(out=s1r[:, t:t + 1], in_=s1b[:, t:t + 1],
                                                 func=AF.Ln, bias=cst_eps[:, 0:1]),
                     waits=wl_)
            eR = sop(lambda e, t=t: e.activation(out=s1r[:, t:t + 1], in_=s1r[:, t:t + 1],
                                                 func=AF.Exp, scale=-0.5), waits=[eL])
            e_ln_prev = eR
            xnb = xn_tile if (t % 2 == 0) else xn_tile_b
            wn = [eR, el]
            if t >= 2:
                wn.append(ev_xn_ecp.get(t - 2))
            exn = vop(lambda e, b=ax[t % 2], o=xnb, t=t: e.tensor_scalar(
                out=o[:, :], in0=b[:, :], scalar1=s1m[:, t:t + 1],
                scalar2=s1r[:, t:t + 1], op0=ALU.subtract, op1=ALU.mult), waits=wn)
            e_exn[t] = exn
            ps = pA if (tt % 2 == 0) else pB
            key = "A" if (tt % 2 == 0) else "B"
            wtp = [exn, e_ident]
            if e_tp_evac[key] is not None:
                wtp.append(e_tp_evac[key])
            if c > 0:
                if key == "A" and e_prev_rope is not None:
                    wtp.append(e_prev_rope)
                if key == "B" and e_prev_vcopy is not None:
                    wtp.append(e_prev_vcopy)
            etp = None
            for cc in range(16):
                etp = top(lambda e, ps=ps, o=xnb, cc=cc: e.matmul(
                    ps[:, cc * 128:(cc + 1) * 128], o[:, cc * 128:(cc + 1) * 128],
                    identf[:, :], start=True, stop=True, is_transpose=True),
                    waits=(wtp if cc == 0 else []))
            wev = [etp]
            if c >= 2:
                wev.append(e_prev_kmm)
                wev.append(ev_xnc_store.get(c - 2))
            ecp = sop(lambda e, ps=ps, c=c, tt=tt: e.copy(
                out=xnc[c % 2][:, :, tt * 128:(tt + 1) * 128],
                in_=ps[:, :].rearrange("p (cc m) -> p cc m", cc=16)), waits=wev)
            ev_xn_ecp[t] = ecp
            e_tp_evac[key] = ecp
        ev_xnc_store[c] = dma("scalar", "st_xnt",
                              xnT_scr[:, :, c * 512:(c + 1) * 512]
                              .rearrange("cc p t -> p cc t"),
                              xnc[c % 2][:, :, :], waits=[ev_xn_ecp[c0 + 3]])
        tb = (c % 2) * 1024
        wtab = []
        if ev_tab_prev:
            wtab.append(ev_tab_prev)
        if c >= 2:
            wtab.append(e_prev_rope)
        etc = dma("sync", f"d_tab{c % 2}", tabc[:, tb:tb + 512],
                  cosT_d[:, c * 512:(c + 1) * 512], waits=wtab)
        ets = dma("sync", f"d_tab{c % 2}", tabc[:, tb + 512:tb + 1024],
                  sinT_d[:, c * 512:(c + 1) * 512], waits=[etc])
        ev_tab_prev = ets
        mmw = [ev_xn_ecp[c0 + 3], e_wkv, e_wvv]
        if e_prev_rope is not None:
            mmw += [e_prev_rope, e_prev_vcopy]
        ek = []
        for hh in range(HKV):
            for cc in range(16):
                ek.append(top(lambda e, hh=hh, cc=cc, c=c: e.matmul(
                    pA[:, hh * 512:(hh + 1) * 512],
                    wkv3[:, cc, hh * 128:(hh + 1) * 128],
                    xnc[c % 2][:, cc, :],
                    start=(cc == 0), stop=(cc == 15)),
                    waits=(mmw if (hh == 0 and cc == 0) else [])))
        evm = []
        for i in range(4):
            for cc in range(16):
                evm.append(top(lambda e, i=i, cc=cc, c=c: e.matmul(
                    pB[:, i * 512:(i + 1) * 512],
                    xnc[c % 2][:, cc, i * 128:(i + 1) * 128],
                    wkv3[:, 16 + cc, :],
                    start=(cc == 0), stop=(cc == 15)), waits=[]))
        e_prev_kmm = evm[-1]
        for hh in range(HKV):
            w = [ek[hh * 16 + 15], ets]
            if e_prev_rope is not None:
                w.append(e_prev_rope)
            e_prev_rope = rope_evac(
                pA, hh * 512, tb,
                kT[:, hh * N_TOK + c * 512:hh * N_TOK + c * 512 + 512], w)
        for i in range(4):
            w = [evm[i * 16 + 15]]
            if e_prev_vcopy is not None:
                w.append(e_prev_vcopy)
            e_prev_vcopy = sop(lambda e, i=i, c=c: e.copy(
                out=vT[:, (c * 4 + i) * 512:(c * 4 + i) * 512 + 512],
                in_=pB[:, i * 512:(i + 1) * 512]), waits=w)
        for j in range(c * 7, min(32, c * 7 + 7)):
            emit_amax(j)
    e_a1_alldone = ev_xn_ecp[19]
    e_fam = vop(lambda e: e.tensor_reduce(
        out=pam[:, 32:33], in_=pam[:, 0:16], axis=AX.X, op=ALU.max),
        waits=[prep_state["last_red"]])
    e_pam = vop(lambda e: e.tensor_reduce(
        out=pam[:, 33:34], in_=pam[:, 16:32], axis=AX.X, op=ALU.max), waits=[e_fam])
    e_fm = part_max(pam[:, 32:33], 0, [e_fam])
    e_pm = part_max(pam[:, 33:34], 1, [e_pam])
    e_r0 = vop(lambda e: e.tensor_scalar(
        out=grow[0:1, 0:8], in0=onesrow[0:1, 0:8], scalar1=scp[0:1, 0:1],
        scalar2=None, op0=ALU.mult), waits=[e_or, e_fm, e_pm, e_grow[0]])
    e_r1 = vop(lambda e: e.tensor_scalar(
        out=grow[0:1, 1:2], in0=onesrow[0:1, 1:2], scalar1=scp[0:1, 1:2],
        scalar2=None, op0=ALU.mult), waits=[e_r0])
    e_grow[0] = e_r1
    e_ci = dma("gpsimd", "g_d", cc_in[0][:, :], grow[0:1, 0:8], waits=[e_r1])
    e_ar1 = coll("AllReduce", ALU.max, cc_in[0][:, :], cc_out[0][:, :], waits=[e_ci])
    e_mp0 = dma("sync", "d_mask", mpers[:, 0:512], masks_d[20, :, :],
                waits=[e_a1_alldone])
    e_mp = dma("sync", "d_mask", mpers[:, 512:1024], masks_d[21, :, :], waits=[e_mp0])

    qc3 = qc[:, :].rearrange("p (h t) -> p h t", h=HQ)
    at4f = at3[:, :, :, :].rearrange("p a b c -> p (a b c)")
    ev_a3_attn_last = None
    e_prev_qrope = e_prev_rope
    e_attn_mm_last = None
    ev_mask_prev = None
    e_tanh = {}
    e_exp_prev = None
    e_pT_use = {}
    e_sum_rd = {0: None, 1: None}
    e_pv_rd = {0: None, 1: None}
    e_rinv_rd = {0: None, 1: None}
    e_first_score = True
    e_mask_tt_last = None
    e_xnt_rd = None
    prev_chunk_qmm = None
    e_q_mm_prev = None
    e_prep_sbuf_free = None
    e_p8_free = None
    for c in range(NCH):
        if c == 2:
            e_c1l = dma("gpsimd", "g_d", grow[0:1, 0:8], cc_out[0][:, :],
                        waits=[e_ar1, e_grow[0]])
            e_grow[0] = e_c1l
            e_swf = vop(lambda e: e.tensor_scalar(
                out=scp[0:1, 2:3], in0=grow[0:1, 0:1], scalar1=float(1.0 / FMAX),
                scalar2=1e-6, op0=ALU.mult, op1=ALU.max), waits=[e_c1l])
            e_swp = vop(lambda e: e.tensor_scalar(
                out=scp[0:1, 4:5], in0=grow[0:1, 1:2], scalar1=float(1.0 / FMAX),
                scalar2=1e-6, op0=ALU.mult, op1=ALU.max), waits=[e_swf])
            e_grow[0] = e_swp
            e_k0 = vop(lambda e: e.reciprocal(out=scp[0:1, 3:4], in_=scp[0:1, 2:3]),
                       waits=[e_swf])
            e_kwf = vop(lambda e: e.tensor_scalar(
                out=scp[0:1, 3:4], in0=scp[0:1, 3:4], scalar1=0.5, scalar2=None,
                op0=ALU.mult), waits=[e_k0])
            e_k1 = vop(lambda e: e.reciprocal(out=scp[0:1, 5:6], in_=scp[0:1, 4:5]),
                       waits=[e_swp])
            e_kwp = vop(lambda e: e.tensor_scalar(
                out=scp[0:1, 5:6], in0=scp[0:1, 5:6], scalar1=0.5, scalar2=None,
                op0=ALU.mult), waits=[e_k1])
            e_bwf = bcast(3, 2, [e_kwf])
            e_bwp = bcast(5, 3, [e_kwp])
            e_q_prev = {}
            e_st_prev = None
            for j in range(32):
                w = [e_bwf if j < 16 else e_bwp]
                if j >= 2:
                    w.append(e_q_prev[j - 2])
                el = dma("gpsimd", "g_d", pwb[j % 2][:, :], prep_src(j), waits=w)
                eq = vop(lambda e, j=j: e.tensor_scalar(
                    out=p8[:, :], in0=pwb[j % 2][:, :],
                    scalar1=bc[:, 2:3] if j < 16 else bc[:, 3:4],
                    scalar2=None, op0=ALU.mult), waits=[el, e_bwf, e_bwp] +
                    ([e_st_prev] if e_st_prev else []))
                e_q_prev[j] = eq
                e_st_prev = dma("gpsimd", "g_d", prep_dst(j), p8[:, :], waits=[eq])
            e_prep_sbuf_free = e_q_prev[31]
            e_p8_free = e_st_prev
            e_ag1 = coll("AllGather", ALU.bypass, wfc8_ag[:, :], wfc8[:, :],
                         waits=[e_st_prev])
            e_ag2 = coll("AllGather", ALU.bypass, wp8_ag[:, :], wp8[:, :], waits=[e_ag1])
        tb = (c % 2) * 1024
        wtab = [ev_tab_prev]
        wtab.append(e_prev_qrope if c >= 1 else e_prev_rope)
        etc = dma("sync", f"d_tab{c % 2}", tabc[:, tb:tb + 512],
                  cosT_d[:, 512 + c * 512:512 + (c + 1) * 512], waits=wtab)
        ets = dma("sync", f"d_tab{c % 2}", tabc[:, tb + 512:tb + 1024],
                  sinT_d[:, 512 + c * 512:512 + (c + 1) * 512], waits=[etc])
        ev_tab_prev = ets
        wxl = [ev_xnc_store[c + 1]]
        if e_xnt_rd is not None:
            wxl.append(e_xnt_rd)
        exl = dma("sync", "d_xnt", xnT_sb[:, :, :],
                  xnT_scr[:, :, 512 + c * 512:512 + (c + 1) * 512]
                  .rearrange("cc p t -> p cc t"), waits=wxl)
        eq_mm_done = []
        for hq in range(4):
            wb = hq % 2
            w = [e_wqbf[hq]]
            if c == 0 and hq < 2:
                w.append(e_prev_kmm)
            if hq >= 2:
                w.append(eq_mm_done[hq - 2])
            elif c > 0:
                w.append(prev_chunk_qmm[hq - 2])
            ewq = dma("sync", f"d_wq{hq % 2}", wqs[:, wb, :, :],
                      wqT_bf[:, hq * 512:(hq + 1) * 512]
                      .rearrange("(cc p) m -> p cc m", cc=16), waits=w)
            mmw = [exl, ewq]
            if e_prev_qrope is not None:
                mmw.append(e_prev_qrope)
            if c == 0 and hq == 0:
                mmw.append(e_prev_kmm)
                mmw.append(e_prev_rope)
            if c > 0 and hq == 0:
                mmw.append(e_exp_prev)
                mmw.append(ev_a3_attn_last)
            emm = []
            for cc in range(16):
                for hh in range(4):
                    emm.append(top(lambda e, cc=cc, hh=hh, wb=wb: e.matmul(
                        pA[:, hh * 512:(hh + 1) * 512],
                        wqs[:, wb, cc, hh * 128:(hh + 1) * 128],
                        xnT_sb[:, cc, :],
                        start=(cc == 0), stop=(cc == 15)),
                        waits=(mmw if (cc == 0 and hh == 0) else [])))
            eq_mm_done.append(emm[-1])
            e_xnt_rd = emm[-1]
            for hh in range(4):
                hg = hq * 4 + hh
                w = [emm[15 * 4 + hh], ets]
                if e_prev_qrope is not None:
                    w.append(e_prev_qrope)
                e_prev_qrope = rope_evac(pA, hh * 512, tb,
                                         qc[:, hg * 512:hg * 512 + 512], w)
        prev_chunk_qmm = eq_mm_done
        e_q_mm_prev = eq_mm_done[-1]
        for i in range(4):
            n = 4 * c + i
            if n < 4:
                ml = []
                for j in range(5):
                    if j == 0:
                        w0 = [e_mp, e_a1_alldone]
                        if e_mask_tt_last is not None:
                            w0 = [e_mask_tt_last]
                    else:
                        w0 = [ml[-1]]
                    ml.append(dma("sync", "d_mask", mask5[:, j * 512:j * 512 + 512],
                                  masks_d[n * 5 + j, :, :], waits=w0))
                ev_mask_prev = ml[-1]
            for g in range(HKV):
                rhs_q = qc3[:, 4 * g:4 * g + 4, i * 128:(i + 1) * 128]
                p2 = (n * 4 + g) % 2
                e_expj = {}
                PS = pB if p2 == 0 else pA
                pt = pTs if p2 == 0 else pTs2
                JORD = (1, 2, 3, 0, 4) if n >= 4 else (0, 1, 2, 3, 4)
                for j in JORD:
                    kt = n + j
                    bk = (j % 2)
                    w = [e_prev_qrope]
                    if e_first_score:
                        w.append(e_prev_vcopy)
                        e_first_score = False
                    if e_tanh.get((p2, bk)) is not None:
                        w.append(e_tanh[(p2, bk)])
                    emm = top(lambda e, kt=kt, g=g, bk=bk, r=rhs_q, PS=PS: e.matmul(
                        PS[:, bk * 512:bk * 512 + 512],
                        kT[:, g * N_TOK + kt * 128:g * N_TOK + kt * 128 + 128], r,
                        start=True, stop=True), waits=w)
                    wt_ = [emm]
                    if n == 0 and g == 0 and j == 0:
                        wt_.append(e_a1_alldone)
                    if e_pT_use.get((p2, j)) is not None:
                        wt_.append(e_pT_use[(p2, j)])
                    et = sop(lambda e, bk=bk, j=j, PS=PS, pt=pt: e.activation(
                        out=pt[:, j * 512:j * 512 + 512],
                        in_=PS[:, bk * 512:bk * 512 + 512], func=AF.Tanh,
                        scale=float(1.0 / (CAP * float(np.sqrt(HD))))), waits=wt_)
                    e_tanh[(p2, bk)] = et
                    em = None
                    if n < 4:
                        em = vop(lambda e, j=j, pt=pt: e.tensor_tensor(
                            out=pt[:, j * 512:j * 512 + 512],
                            in0=pt[:, j * 512:j * 512 + 512],
                            in1=mask5[:, j * 512:j * 512 + 512], op=ALU.add),
                            waits=[et, ev_mask_prev])
                        e_mask_tt_last = em
                    elif j == 0:
                        em = vop(lambda e, pt=pt: e.tensor_tensor(
                            out=pt[:, 0:512], in0=pt[:, 0:512],
                            in1=mpers[:, 0:512], op=ALU.add), waits=[et, e_mp])
                    elif j == 4:
                        em = vop(lambda e, pt=pt: e.tensor_tensor(
                            out=pt[:, 4 * 512:5 * 512], in0=pt[:, 4 * 512:5 * 512],
                            in1=mpers[:, 512:1024], op=ALU.add), waits=[et, e_mp])
                    we_ = [em if em is not None else et, e_ncap]
                    ee = sop(lambda e, j=j, pt=pt: e.activation(
                        out=pt[:, j * 512:j * 512 + 512],
                        in_=pt[:, j * 512:j * 512 + 512],
                        func=AF.Exp, scale=float(CAP), bias=cst_ncap[:, 0:1]), waits=we_)
                    e_exp_prev = ee
                    e_expj[j] = ee
                e_sum_last = None
                e_pv_last = None
                for ji, j in enumerate(JORD):
                    kt = n + j
                    first, last = ji == 0, ji == 4
                    wsum = [e_expj[j]]
                    if first and e_sum_rd[p2] is None:
                        wsum.append(e_ones)
                    if first and e_sum_rd[p2] is not None:
                        wsum.append(e_sum_rd[p2])
                    e_sum_last = top(lambda e, j=j, PS=PS, pt=pt, f=first, l=last: e.matmul(
                        PS[:, 2 * 512:3 * 512], ones_bf[:, :],
                        pt[:, j * 512:j * 512 + 512],
                        start=f, stop=l), waits=wsum)
                    wpv = []
                    if first and e_pv_rd[p2] is not None:
                        wpv.append(e_pv_rd[p2])
                    e_pv_last = top(lambda e, j=j, kt=kt, g=g, PS=PS, pt=pt, f=first,
                                    l=last: e.matmul(
                        PS[:, 3 * 512:4 * 512],
                        vT[:, kt * 512 + g * 128:kt * 512 + g * 128 + 128],
                        pt[:, j * 512:j * 512 + 512],
                        start=f, stop=l), waits=wpv)
                    e_pT_use[(p2, j)] = e_pv_last
                e_attn_mm_last = e_pv_last
                w = [e_sum_last]
                if e_rinv_rd[p2] is not None:
                    w.append(e_rinv_rd[p2])
                el1 = vop(lambda e, PS=PS, p2=p2: e.reciprocal(
                    out=rinv2[p2][:, :], in_=PS[:, 2 * 512:3 * 512]), waits=w)
                e_sum_rd[p2] = el1
                w = [e_pv_last, el1]
                if ev_a3_attn_last is not None:
                    w.append(ev_a3_attn_last)
                if n == 0 and g == 0:
                    w.append(e_a1_alldone)
                if n == 14 and g == 0:
                    w.append(e_prep_sbuf_free)
                ea = vop(lambda e, g=g, n=n, PS=PS, p2=p2: e.tensor_tensor(
                    out=at4f[:, (n * 16 + 4 * g) * 128:(n * 16 + 4 * g + 4) * 128],
                    in0=PS[:, 3 * 512:4 * 512], in1=rinv2[p2][:, :], op=ALU.mult), waits=w)
                ev_a3_attn_last = ea
                e_pv_rd[p2] = ea
                e_rinv_rd[p2] = ea

    e_wot = []
    for kq in range(4):
        w = [e_wobf, e_q_mm_prev]
        if kq == 3:
            w.append(e_attn_mm_last)
        e_wot.append(dma("sync", "d_wo", woT_sb[:, kq * 4:(kq + 1) * 4, :],
                         woT_bf[kq * 512:(kq + 1) * 512, :]
                         .rearrange("(ck p) m -> p ck m", ck=4), waits=w))
    ev_x1 = []
    e_pa_free = None
    e_sq_rd = None
    e_ecp2_prev = None
    e_wo_mm = {}
    e_xn2_rd = None
    for n in range(T_OWN):
        mm = []
        for ck in range(16):
            for fq in range(4):
                w = []
                if ck == 0 and fq == 0:
                    w = [e_wot[-1], ev_a3_attn_last]
                    if e_pa_free is not None:
                        w.append(e_pa_free)
                mm.append(top(lambda e, ck=ck, fq=fq, n=n: e.matmul(
                    pA[:, fq * 512:(fq + 1) * 512],
                    at4[:, n, ck, :],
                    woT_sb[:, ck, fq * 512:(fq + 1) * 512],
                    start=(ck == 0), stop=(ck == 15)), waits=w))
        e_wo_mm[n] = mm[-1]
        o1 = o1b[n % 2]
        w = [mm[-1]]
        if n < 2:
            w.append(e_attn_mm_last)
        if n >= 2:
            w.append(ev_x1[n - 2][1])
        eo = sop(lambda e, o=o1: e.copy(out=o[:, :], in_=pA[:, :]), waits=w)
        e_pa_free = eo
        es1 = vop(lambda e, o=o1, n=n: e.reduce_sum(out=w1[:, n:n + 1], in_=o[:, :],
                                                    axis=AX.X), waits=[eo])
        wsq = [eo]
        if e_sq_rd is not None:
            wsq.append(e_sq_rd)
        if n == 0:
            wsq.append(e_attn_mm_last)
        esq = sop(lambda e, o=o1: e.activation(out=sq2[:, :], in_=o[:, :], func=AF.Square),
                  waits=wsq)
        es2 = vop(lambda e, n=n: e.reduce_sum(out=w2[:, n:n + 1], in_=sq2[:, :], axis=AX.X),
                  waits=[esq])
        e_sq_rd = es2
        em_ = vop(lambda e, n=n: e.tensor_scalar(
            out=wm[:, n:n + 1], in0=w1[:, n:n + 1], scalar1=float(1.0 / D), scalar2=None,
            op0=ALU.mult), waits=[es1])
        eq_ = vop(lambda e, n=n: e.tensor_tensor(
            out=wt[:, n:n + 1], in0=wm[:, n:n + 1], in1=wm[:, n:n + 1], op=ALU.mult),
            waits=[em_])
        ee_ = vop(lambda e, n=n: e.tensor_scalar(
            out=w2[:, n:n + 1], in0=w2[:, n:n + 1], scalar1=float(1.0 / D), scalar2=None,
            op0=ALU.mult), waits=[es2])
        ev_ = vop(lambda e, n=n: e.tensor_tensor(
            out=w2[:, n:n + 1], in0=w2[:, n:n + 1], in1=wt[:, n:n + 1], op=ALU.subtract),
            waits=[ee_, eq_])
        el_ = sop(lambda e, n=n: e.activation(out=wr[:, n:n + 1], in_=w2[:, n:n + 1],
                                              func=AF.Ln, bias=cst_eps[:, 0:1]), waits=[ev_])
        er_ = sop(lambda e, n=n: e.activation(out=wr[:, n:n + 1], in_=wr[:, n:n + 1],
                                              func=AF.Exp, scale=-0.5), waits=[el_])
        xb = a4x[n % 2]
        wx = [e_attn_mm_last] if n < 2 else [ev_x1[n - 2][3], ev_x1[n - 2][5]]
        ex = dma("sync", f"d_x{n % 2}", xb[:, :], x_in[512 + n * 128:512 + (n + 1) * 128, :],
                 waits=wx)
        et_ = vop(lambda e, o=o1, n=n: e.tensor_scalar(
            out=sq2[:, :], in0=o[:, :], scalar1=wm[:, n:n + 1], scalar2=wr[:, n:n + 1],
            op0=ALU.subtract, op1=ALU.mult), waits=[er_, es2, eo])
        ex1 = vop(lambda e, xb=xb: e.tensor_tensor(
            out=xb[:, :], in0=sq2[:, :], in1=xb[:, :], op=ALU.add), waits=[et_, ex, es1])
        es_ = dma("sync", f"st_x1{n % 2}", x1_scr[n * 128:(n + 1) * 128, :], xb[:, :],
                  waits=[ex1])
        eb1 = vop(lambda e, xb=xb, n=n: e.reduce_sum(
            out=m2a[:, n:n + 1], in_=xb[:, :], axis=AX.X), waits=[ex1])
        esqb = sop(lambda e, xb=xb: e.activation(out=sq2[:, :], in_=xb[:, :],
                                                 func=AF.Square), waits=[ex1])
        eb2 = vop(lambda e, n=n: e.reduce_sum(
            out=m2b[:, n:n + 1], in_=sq2[:, :], axis=AX.X), waits=[esqb])
        e_sq_rd = eb2
        eb3 = vop(lambda e, n=n: e.tensor_scalar(
            out=m2m[:, n:n + 1], in0=m2a[:, n:n + 1], scalar1=float(1.0 / D),
            scalar2=None, op0=ALU.mult), waits=[eb1])
        eb4 = vop(lambda e, n=n: e.tensor_tensor(
            out=m2t[:, n:n + 1], in0=m2m[:, n:n + 1], in1=m2m[:, n:n + 1], op=ALU.mult),
            waits=[eb3])
        eb5 = vop(lambda e, n=n: e.tensor_scalar(
            out=m2b[:, n:n + 1], in0=m2b[:, n:n + 1], scalar1=float(1.0 / D),
            scalar2=None, op0=ALU.mult), waits=[eb2])
        eb6 = vop(lambda e, n=n: e.tensor_tensor(
            out=m2b[:, n:n + 1], in0=m2b[:, n:n + 1], in1=m2t[:, n:n + 1],
            op=ALU.subtract), waits=[eb4, eb5])
        eb7 = sop(lambda e, n=n: e.activation(out=m2r[:, n:n + 1], in_=m2b[:, n:n + 1],
                                              func=AF.Ln, bias=cst_eps[:, 0:1]), waits=[eb6])
        eb8 = sop(lambda e, n=n: e.activation(out=m2r[:, n:n + 1], in_=m2r[:, n:n + 1],
                                              func=AF.Exp, scale=-0.5), waits=[eb7])
        wxn2 = [eb8, ex1, eb2]
        if e_xn2_rd is not None:
            wxn2.append(e_xn2_rd)
        if n == 0:
            wxn2.append(e_a1_alldone)
        exn2 = vop(lambda e, xb=xb, n=n: e.tensor_scalar(
            out=xn2_tile[:, :], in0=xb[:, :], scalar1=m2m[:, n:n + 1],
            scalar2=m2r[:, n:n + 1], op0=ALU.subtract, op1=ALU.mult), waits=wxn2)
        ea2 = vop(lambda e, n=n: e.tensor_reduce(
            out=am_x[:, n:n + 1], in_=xn2_tile[:, :], axis=AX.X, op=ALU.max,
            apply_absolute_value=True), waits=[exn2])
        wtp = [exn2]
        if n == 0:
            wtp.append(ev_a3_attn_last)
        else:
            wtp.append(e_ecp2_prev)
        etp = None
        for cc in range(16):
            etp = top(lambda e, cc=cc: e.matmul(
                pB[:, cc * 128:(cc + 1) * 128], xn2_tile[:, cc * 128:(cc + 1) * 128],
                identf[:, :], start=True, stop=True, is_transpose=True),
                waits=(wtp if cc == 0 else []))
        ecp2 = sop(lambda e, n=n: e.copy(
            out=at4[:, n, :, :],
            in_=pB[:, :].rearrange("p (cc m) -> p cc m", cc=16)),
            waits=[etp, e_wo_mm[n]])
        e_ecp2_prev = ecp2
        e_xn2_rd = ecp2
        ev_x1.append((ex, eo, ex1, es_, ea2, exn2))
    e_wo_last = e_wo_mm[T_OWN - 1]

    eax2 = vop(lambda e: e.tensor_reduce(
        out=am_x[:, 0:1], in_=am_x[:, 0:16], axis=AX.X, op=ALU.max),
        waits=[ev_x1[-1][4], ev_x1[-2][4]])
    e_gx = part_max(am_x[:, 0:1], 8, [eax2])
    e_arx = allred_max(1, 8, 9, [e_gx])
    e_sx = vop(lambda e: e.tensor_scalar(
        out=scp[0:1, 10:11], in0=scp[0:1, 9:10], scalar1=float(1.0 / FMAX), scalar2=1e-6,
        op0=ALU.mult, op1=ALU.max), waits=[e_arx])
    e_kx0 = vop(lambda e: e.reciprocal(out=scp[0:1, 11:12], in_=scp[0:1, 10:11]), waits=[e_sx])
    e_kx = vop(lambda e: e.tensor_scalar(
        out=scp[0:1, 11:12], in0=scp[0:1, 11:12], scalar1=0.5, scalar2=None, op0=ALU.mult),
        waits=[e_kx0])
    e_bkx = bcast(11, 0, [e_kx])
    e_s1a_ = vop(lambda e: e.tensor_tensor(
        out=scp[0:1, 12:13], in0=scp[0:1, 10:11], in1=scp[0:1, 2:3], op=ALU.mult),
        waits=[e_sx, e_swf])
    e_s1b_ = vop(lambda e: e.tensor_scalar(
        out=scp[0:1, 12:13], in0=scp[0:1, 12:13], scalar1=4.0, scalar2=None, op0=ALU.mult),
        waits=[e_s1a_])
    e_bs1 = bcast(12, 1, [e_s1b_])
    ev_qz = []
    for cc in range(16):
        w = [e_bkx, e_ecp2_prev]
        if cc < 2:
            w.append(e_wo_last)
        if cc % 2 == 0:
            eqz = sop(lambda e, cc=cc: e.activation(
                out=xq8[:, cc, :].rearrange("p (n m) -> p n m", n=16),
                in_=at4[:, :, cc, :], func=AF.Copy, scale=bc[:, 0:1]), waits=w)
        else:
            eqz = vop(lambda e, cc=cc: e.tensor_scalar(
                out=xq8[:, cc, :].rearrange("p (n m) -> p n m", n=16),
                in0=at4[:, :, cc, :], scalar1=bc[:, 0:1], scalar2=None,
                op0=ALU.mult), waits=w)
        ev_qz.append(eqz)
    e_qz_all = vop(lambda e: e.tensor_copy(out=grow[0:1, 8:9], in_=grow[0:1, 0:1]),
                   waits=[ev_qz[-1], ev_qz[-2], e_grow[0]])
    e_grow[0] = e_qz_all

    xq8v = xq8[:, :, :]
    ev_f1 = []
    e_bank_free = {0: e_pa_free, 1: e_ecp2_prev}
    e_r_prev = {0: None, 1: None}
    e_amr0 = vop(lambda e: e.memset(am_r[:, :], 0.0), waits=[e_qz_all])
    e_amr_prev = e_amr0
    ev_wl = []
    e_f1_mm = {}
    for mc in range(64):
        w = [e_ag1]
        if mc < 2:
            w.append(e_wo_last)
        if mc >= 2:
            w.append(e_f1_mm[mc - 2])
        ewl = dma("sync", "d_wfc", f1w[mc % 2][:, :], wfc8[mc * 128:(mc + 1) * 128, :],
                  waits=w)
        ev_wl.append(ewl)
        ps = pA if (mc % 2 == 0) else pB
        emm = None
        for cc2 in range(8):
            lhs = f1w[mc % 2][:, cc2 * 256:(cc2 + 1) * 256].rearrange(
                "p (kk m) -> p kk m", kk=2)
            for tq in range(4):
                w2_ = []
                if cc2 == 0 and tq == 0:
                    w2_ = [ewl, e_qz_all]
                    if e_bank_free[mc % 2] is not None:
                        w2_.append(e_bank_free[mc % 2])
                emm = top(lambda e, ps=ps, lhs=lhs, cc2=cc2, tq=tq: e.matmul(
                    ps[:, tq * 512:(tq + 1) * 512], lhs,
                    xq8v[:, 2 * cc2:2 * cc2 + 2, tq * 512:(tq + 1) * 512],
                    start=(cc2 == 0), stop=(cc2 == 7), perf_mode=DR), waits=w2_)
        e_f1_mm[mc] = emm
        rb = f1r[mc % 2]
        w = [emm, e_bs1]
        if mc < 2:
            w.append(ev_x1[-1][3])
            w.append(ev_x1[-2][3])
            w.append(ev_x1[-1][5])
            w.append(ev_x1[-2][5])
        if mc == 0:
            w.append(e_p8_free)
        if e_r_prev[mc % 2] is not None:
            w.append(e_r_prev[mc % 2])
        ee = vop(lambda e, ps=ps, rb=rb: e.tensor_scalar(
            out=rb[:, :], in0=ps[:, :], scalar1=0.0, scalar2=bc[:, 1:2],
            op0=ALU.max, op1=ALU.mult), waits=w)
        e_bank_free[mc % 2] = ee
        ea = vop(lambda e, rb=rb: e.tensor_reduce(
            out=pam[:, 34:35], in_=rb[:, :], axis=AX.X, op=ALU.max),
            waits=[ee, e_amr_prev])
        eam = vop(lambda e: e.tensor_tensor(
            out=am_r[:, :], in0=am_r[:, :], in1=pam[:, 34:35], op=ALU.max), waits=[ea])
        e_amr_prev = eam
        es = dma("scalar", f"st_r{mc % 2}", r_spill[mc * 128:(mc + 1) * 128, :], rb[:, :],
                 waits=[ee])
        e_r_prev[mc % 2] = es
        ev_f1.append((emm, es, eam))
    e_f1_last = e_f1_mm[63]

    e_gr = part_max(am_r[:, 0:1], 16, [e_amr_prev])
    e_arr = allred_max(2, 16, 17, [e_gr])
    e_mh = vop(lambda e: e.tensor_tensor(
        out=scp[0:1, 18:19], in0=scp[0:1, 17:18], in1=scp[0:1, 17:18], op=ALU.mult),
        waits=[e_arr])
    e_sh = vop(lambda e: e.tensor_scalar(
        out=scp[0:1, 18:19], in0=scp[0:1, 18:19], scalar1=float(1.0 / FMAX), scalar2=1e-6,
        op0=ALU.mult, op1=ALU.max), waits=[e_mh])
    e_kh0 = vop(lambda e: e.reciprocal(out=scp[0:1, 19:20], in_=scp[0:1, 18:19]), waits=[e_sh])
    e_kh = vop(lambda e: e.tensor_scalar(
        out=scp[0:1, 19:20], in0=scp[0:1, 19:20], scalar1=0.5, scalar2=None, op0=ALU.mult),
        waits=[e_kh0])
    e_lk = sop(lambda e: e.activation(out=scp[0:1, 20:21], in_=scp[0:1, 19:20], func=AF.Ln),
               waits=[e_kh])
    e_sk = sop(lambda e: e.activation(out=scp[0:1, 20:21], in_=scp[0:1, 20:21], func=AF.Exp,
                                      scale=0.5), waits=[e_lk])
    e_bsk = bcast(20, 4, [e_sk])
    e_bk6 = bcast(19, 6, [e_kh, e_bsk])
    e_s2a = vop(lambda e: e.tensor_tensor(
        out=scp[0:1, 21:22], in0=scp[0:1, 18:19], in1=scp[0:1, 4:5], op=ALU.mult),
        waits=[e_sh, e_swp])
    e_s2b = vop(lambda e: e.tensor_scalar(
        out=scp[0:1, 21:22], in0=scp[0:1, 21:22], scalar1=4.0, scalar2=None, op0=ALU.mult),
        waits=[e_s2a])
    e_bs2 = bcast(21, 5, [e_s2b])

    wp8v = wp8[:, :].rearrange("(kc q p) n -> kc q p n", kc=32, q=4)
    h8v = h8[:, :, :, :]
    ev_y = []
    e_h8_mm_done = None
    e_o2_free = {}
    e_pq_free = {t: None for t in range(8)}
    e_sq2_rd = None
    e_f2t_rd = None
    e_eq8 = {}
    e_wp_mm = {}
    e_r_ld = {0: None, 1: None}
    for blk in range(2):
        for mc in range(64):
            w = [ev_f1[mc][1]]
            if e_r_ld[mc % 2] is not None:
                w.append(e_r_ld[mc % 2])
            elr = dma("sync", f"d_r{mc % 2}", f2r[mc % 2][:, :],
                      r_spill[mc * 128:(mc + 1) * 128, blk * 1024:(blk + 1) * 1024],
                      waits=w)
            wq_ = [elr, e_bsk]
            if mc < 2:
                wq_.append(e_qz_all if blk == 0 else e_h8_mm_done)
            eq8 = sop(lambda e, mc=mc: e.activation(
                out=h8v[:, mc // 2, mc % 2, :], in_=f2r[mc % 2][:, :], func=AF.Square,
                scale=bc[:, 4:5]), waits=wq_)
            e_r_ld[mc % 2] = eq8
            e_eq8[mc] = eq8
        ev_o2 = {}
        for q in range(4):
            for kc2 in range(32):
                w = [e_ag2]
                if blk == 0 and q == 0 and kc2 < 2:
                    w.append(e_qz_all)
                    w.append(ev_x1[-1][3])
                    w.append(ev_x1[-1][5])
                prev = e_wp_mm.get((q, kc2 - 2)) if kc2 >= 2 else (
                    e_wp_mm.get((q - 1, 30 + kc2)) if q > 0 else
                    (e_wp_mm.get((3, 30 + kc2)) if blk > 0 else None))
                if prev is not None:
                    w.append(prev)
                ewp = dma("sync", f"d_wp{kc2 % 2}", f2w[kc2 % 2][:, :],
                          wp8v[kc2, q, :, :], waits=w)
                rhs = f2w[kc2 % 2][:, :].rearrange("p (kk n) -> p kk n", kk=2)
                emm = None
                for t in range(8):
                    ps = pA if t < 4 else pB
                    w2_ = []
                    if t == 0:
                        w2_ = [ewp, e_eq8[2 * kc2 + 1], e_eq8[min(2 * kc2, 63)]]
                    if kc2 == 0:
                        if q == 0 and blk == 0:
                            if t < 4:
                                w2_.append(e_bank_free[0])
                            else:
                                w2_.append(e_bank_free[1])
                            w2_.append(e_f1_last)
                        elif e_pq_free[t] is not None:
                            w2_.append(e_pq_free[t])
                    emm = top(lambda e, ps=ps, t=t, kc2=kc2, rhs=rhs: e.matmul(
                        ps[:, (t % 4) * 512:(t % 4 + 1) * 512],
                        h8v[:, kc2, :, t * 128:(t + 1) * 128], rhs,
                        start=(kc2 == 0), stop=(kc2 == 31), perf_mode=DR), waits=w2_)
                    e_wp_mm[(q, kc2)] = emm
            for t in range(8):
                ps = pA if t < 4 else pB
                w = [e_wp_mm[(q, 31)], e_bs2]
                if q == 0:
                    w.append(e_f1_last)
                    if blk > 0:
                        w.append(e_o2_free[t])
                ev_o2[(q, t)] = vop(lambda e, ps=ps, t=t, q=q: e.tensor_scalar(
                    out=o2[:, t, q * 512:(q + 1) * 512],
                    in0=ps[:, (t % 4) * 512:(t % 4 + 1) * 512], scalar1=bc[:, 5:6],
                    scalar2=None, op0=ALU.mult), waits=w)
                e_pq_free[t] = ev_o2[(q, t)]
        e_h8_mm_done = e_wp_mm[(3, 31)]
        for t in range(8):
            n = blk * 8 + t
            strip = o2[:, t, :]
            w = [ev_o2[(q, t)] for q in range(4)]
            es1_ = vop(lambda e, s=strip, n=n: e.reduce_sum(
                out=f3a[:, n:n + 1], in_=s, axis=AX.X), waits=w)
            wsq = list(w)
            if e_sq2_rd is not None:
                wsq.append(e_sq2_rd)
            if e_f2t_rd is not None:
                wsq.append(e_f2t_rd)
            if blk == 0 and t == 0:
                wsq.append(e_qz_all)
                wsq.append(ev_f1[-1][2])
            esq_ = sop(lambda e, s=strip: e.activation(out=f2t[:, :], in_=s, func=AF.Square),
                       waits=wsq)
            es2_ = vop(lambda e, n=n: e.reduce_sum(
                out=f3b[:, n:n + 1], in_=f2t[:, :], axis=AX.X), waits=[esq_])
            e_sq2_rd = es2_
            em_ = vop(lambda e, n=n: e.tensor_scalar(
                out=f3m[:, n:n + 1], in0=f3a[:, n:n + 1], scalar1=float(1.0 / D),
                scalar2=None, op0=ALU.mult), waits=[es1_])
            eqq = vop(lambda e, n=n: e.tensor_tensor(
                out=f3t[:, n:n + 1], in0=f3m[:, n:n + 1], in1=f3m[:, n:n + 1], op=ALU.mult),
                waits=[em_])
            eee = vop(lambda e, n=n: e.tensor_scalar(
                out=f3b[:, n:n + 1], in0=f3b[:, n:n + 1], scalar1=float(1.0 / D),
                scalar2=None, op0=ALU.mult), waits=[es2_])
            evv = vop(lambda e, n=n: e.tensor_tensor(
                out=f3b[:, n:n + 1], in0=f3b[:, n:n + 1], in1=f3t[:, n:n + 1],
                op=ALU.subtract), waits=[eqq, eee])
            ell = sop(lambda e, n=n: e.activation(out=f3r[:, n:n + 1], in_=f3b[:, n:n + 1],
                                                  func=AF.Ln, bias=cst_eps[:, 0:1]),
                      waits=[evv])
            err = sop(lambda e, n=n: e.activation(out=f3r[:, n:n + 1], in_=f3r[:, n:n + 1],
                                                  func=AF.Exp, scale=-0.5), waits=[ell])
            xb = f2x[t % 2]
            wx = [ev_x1[n][3], e_qz_all]
            if len(ev_y) >= 2:
                wx.append(ev_y[-2])
            ex = dma("sync", f"d_x{t % 2}", xb[:, :], x1_scr[n * 128:(n + 1) * 128, :],
                     waits=wx)
            wty = [err, es2_]
            if e_f2t_rd is not None:
                wty.append(e_f2t_rd)
            ety = vop(lambda e, s=strip, n=n: e.tensor_scalar(
                out=f2t[:, :], in0=s, scalar1=f3m[:, n:n + 1], scalar2=f3r[:, n:n + 1],
                op0=ALU.subtract, op1=ALU.mult), waits=wty)
            e_o2_free[t] = ety
            eyy = vop(lambda e, xb=xb: e.tensor_tensor(
                out=xb[:, :], in0=f2t[:, :], in1=xb[:, :], op=ALU.add), waits=[ety, ex])
            e_f2t_rd = eyy
            est = dma("sync", f"st_y{t % 2}", y_out[n * 128:(n + 1) * 128, :], xb[:, :],
                      waits=[eyy])
            ev_y.append(est)

    v0, v1 = sch.cnt["st_y0"], sch.cnt["st_y1"]
    sch.do("sync", lambda e, a=v0, b=v1: (e.wait_ge("st_y0", a), e.wait_ge("st_y1", b)),
           waits=(), inc=None)
    return nc, sch


class _Wrap:
    def __init__(self, eng, sems):
        self._e = eng
        self._s = sems

    def __getattr__(self, k):
        return getattr(self._e, k)

    def wait_ge(self, s, v):
        if isinstance(s, str):
            s = self._s[s]
        return self._e.wait_ge(s, v)


def _emit(nc, sch):
    with contextlib.ExitStack() as es:
        sems = {name: es.enter_context(nc.semaphore(name)) for name in sch.cnt}
        block = es.enter_context(nc.Block())

        def make(eng_name):
            ops = sch.seq[eng_name]

            def _f(e):
                waited = {}
                wrap = _Wrap(e, sems)
                for w, fn, inc, amt in ops:
                    for s, v in w.items():
                        if waited.get(s, 0) < v:
                            e.wait_ge(sems[s], v)
                            waited[s] = v
                    ins = fn(wrap)
                    if inc is not None:
                        if amt != 1:
                            ins.then_inc(sems[inc], amt)
                        else:
                            ins.then_inc(sems[inc])
            return _f

        block.sync(make("sync"))
        block.scalar(make("scalar"))
        block.vector(make("vector"))
        block.tensor(make("tensor"))
        block.gpsimd(make("gpsimd"))
    return nc


def _host_tables(s0):
    inv = (1.0 / (np.float32(THETA) ** (np.arange(0, HD, 2, dtype=np.float32)
                                        / np.float32(HD)))).astype(np.float32)
    pos = (np.arange(N_TOK, dtype=np.float32) + np.float32(s0 - HALO))
    freqs = pos[None, :] * inv[:, None]
    ang = np.concatenate([freqs, freqs], axis=0)
    cos = np.cos(ang).astype(np.float32)
    sin = np.sin(ang).astype(np.float32)
    sin[0:64, :] *= np.float32(-1.0)
    return np.ascontiguousarray(cos), np.ascontiguousarray(sin)


def _host_masks(s0):
    out = np.zeros((22, HD, 512), dtype=np.float32)
    cc = np.arange(128)[:, None]
    r = np.arange(128)[None, :]
    m0 = np.where(cc > r, 0.0, NEG)
    m4 = np.where(cc <= r, 0.0, NEG)
    out[20] = np.tile(m0, (1, 4))
    out[21] = np.tile(m4, (1, 4))
    for n in range(4):
        for j in range(5):
            i = s0 + 128 * n + r
            jg = s0 - HALO + 128 * (n + j) + cc
            allowed = (jg <= i) & (jg > i - WIN) & (jg >= 0)
            out[n * 5 + j] = np.tile(np.where(allowed, 0.0, NEG), (1, 4))
    return out.astype(ml_dtypes.bfloat16)


_NC_CACHE = None


def kernel(L0_attn_hidden_states, L0_attn_wq, L0_attn_wk, L0_attn_wv, L0_attn_wo,
           L0_ffn_w_fc, L0_ffn_w_proj):
    global _NC_CACHE
    x = np.ascontiguousarray(L0_attn_hidden_states, dtype=np.float32)
    wq = np.asarray(L0_attn_wq, dtype=np.float32)
    wk = np.asarray(L0_attn_wk, dtype=np.float32)
    wv = np.asarray(L0_attn_wv, dtype=np.float32)
    wo = np.asarray(L0_attn_wo, dtype=np.float32)
    wfc = np.asarray(L0_ffn_w_fc, dtype=np.float32)
    wproj = np.asarray(L0_ffn_w_proj, dtype=np.float32)

    wqT = np.ascontiguousarray(wq.T)
    wkT = np.ascontiguousarray(wk.T)
    wvT = np.ascontiguousarray(wv.T)
    woT = np.ascontiguousarray(wo.T)
    wfc_perm = np.ascontiguousarray(
        wfc.reshape(64, 128, 8, 2, 128).transpose(0, 4, 2, 3, 1).reshape(64, 128, D))
    wp_perm = np.ascontiguousarray(
        wproj.reshape(4, 512, 32, 2, 128).transpose(2, 0, 4, 3, 1).reshape(32, 4, 128, 1024))

    if _NC_CACHE is None:
        nc, sch = build_nc()
        _emit(nc, sch)
        _NC_CACHE = nc
    nc = _NC_CACHE

    ones_bf = np.ones((HD, HD), dtype=ml_dtypes.bfloat16)
    ident_bf = np.eye(HD, dtype=ml_dtypes.bfloat16)
    in_maps = []
    for c in range(8):
        b, half = c // 2, c % 2
        s0 = half * N_OWN
        xp = np.zeros((N_TOK, D), dtype=np.float32)
        lo = max(0, s0 - HALO)
        xp[HALO - (s0 - lo):, :] = x[b, lo:s0 + N_OWN, :]
        cos, sin = _host_tables(s0)
        in_maps.append({
            "x": xp,
            "cosT": cos, "sinT": sin,
            "masks": _host_masks(s0),
            "ones_in": ones_bf, "ident_in": ident_bf,
            "wqT": wqT, "wkT": wkT, "wvT": wvT, "woT": woT,
            "wfc_p": np.ascontiguousarray(wfc_perm[c * 8:(c + 1) * 8]),
            "wp_p": np.ascontiguousarray(wp_perm[c * 4:(c + 1) * 4]),
        })
    res = run_bass_kernel_spmd(nc, in_maps, core_ids=list(range(8)))
    out = np.empty((B, S, D), dtype=np.float32)
    for c in range(8):
        b, half = c // 2, c % 2
        out[b, half * N_OWN:(half + 1) * N_OWN, :] = res.results[c]["y"]
    return out


# version 13 (final)
# speedup vs baseline: 1.0150x; 1.0053x over previous
import numpy as np
import ml_dtypes
import contextlib
import concourse.bass as bass
import concourse.mybir as mybir
from concourse.bass_utils import run_bass_kernel_spmd

F32 = mybir.dt.float32
BF16 = mybir.dt.bfloat16
FP8 = mybir.dt.float8e4
AF = mybir.ActivationFunctionType
ALU = mybir.AluOpType
AX = mybir.AxisListType
DR = mybir.MatmulPerfMode.DoubleRow

B, S, D = 4, 4096, 2048
HQ, HKV, HD = 16, 4, 128
WIN = 512
CAP = 50.0
THETA = 10000.0
EPS = 1e-5
N_OWN, HALO, N_TOK = 2048, 512, 2560
T_OWN, T_ALL = 16, 20
NCH = 4
FMAX = 448.0
NEG = -1e30

ENGINES = ("sync", "scalar", "vector", "tensor", "gpsimd")


class Sched:
    def __init__(self):
        self.seq = {e: [] for e in ENGINES}
        self.cnt = {}
        self.known = set()

    def new_sem(self, name):
        self.cnt[name] = 0

    def do(self, eng, fn, waits=(), inc=None, amt=1):
        w = {}
        for e in waits:
            if e is None:
                continue
            s, v = e
            assert (s, v) in self.known, f"wait on future/unknown event {s}:{v}"
            w[s] = max(w.get(s, 0), v)
        self.seq[eng].append((w, fn, inc, amt))
        if inc is not None:
            self.cnt[inc] += amt
            e = (inc, self.cnt[inc])
            self.known.add(e)
            return e
        return None


def build_nc():
    nc = bass.Bass("TRN2", target_bir_lowering=False)
    sch = Sched()

    x_in = nc.dram_tensor("x", [N_TOK, D], F32, kind="ExternalInput")
    cosT_d = nc.dram_tensor("cosT", [HD, N_TOK], F32, kind="ExternalInput")
    sinT_d = nc.dram_tensor("sinT", [HD, N_TOK], F32, kind="ExternalInput")
    masks_d = nc.dram_tensor("masks", [22, HD, 512], BF16, kind="ExternalInput")
    ones_d = nc.dram_tensor("ones_in", [HD, HD], BF16, kind="ExternalInput")
    ident_d = nc.dram_tensor("ident_in", [HD, HD], BF16, kind="ExternalInput")
    wqT_d = nc.dram_tensor("wqT", [D, D], F32, kind="ExternalInput")
    wkT_d = nc.dram_tensor("wkT", [D, 512], F32, kind="ExternalInput")
    wvT_d = nc.dram_tensor("wvT", [D, 512], F32, kind="ExternalInput")
    woT_d = nc.dram_tensor("woT", [D, D], F32, kind="ExternalInput")
    wfc_p = nc.dram_tensor("wfc_p", [8, 128, D], F32, kind="ExternalInput")
    wp_p = nc.dram_tensor("wp_p", [4, 4, 128, 1024], F32, kind="ExternalInput")
    y_out = nc.dram_tensor("y", [N_OWN, D], F32, kind="ExternalOutput")

    wqT_bf = nc.dram_tensor("wqT_bf", [D, D], BF16)
    wkT_bf = nc.dram_tensor("wkT_bf", [D, 512], BF16)
    wvT_bf = nc.dram_tensor("wvT_bf", [D, 512], BF16)
    woT_bf = nc.dram_tensor("woT_bf", [D, D], BF16)
    wfc8_ag = nc.dram_tensor("wfc8_ag", [8 * 128, D], FP8)
    wfc8 = nc.dram_tensor("wfc8", [64 * 128, D], FP8)
    wp8_ag = nc.dram_tensor("wp8_ag", [16 * 128, 1024], FP8)
    wp8 = nc.dram_tensor("wp8", [128 * 128, 1024], FP8)
    xnT_scr = nc.dram_tensor("xnT_scr", [16, 128, N_TOK], BF16)
    x1_scr = nc.dram_tensor("x1_scr", [N_OWN, D], F32)
    r_spill = nc.dram_tensor("r_spill", [4 * D, N_OWN], BF16)
    cc_in = [nc.dram_tensor(f"cc{i}_in", [1, 8], F32) for i in range(3)]
    cc_out = [nc.dram_tensor(f"cc{i}_out", [1, 8], F32) for i in range(3)]
    col_scr = nc.dram_tensor("col_scr", [HD, 1], F32)
    row_scr = nc.dram_tensor("row_scr", [HD, 1], F32)

    SB_BASE = 16512
    def sb(name, shape, dtype, off_kb):
        return nc.alloc_sbuf_tensor_at(name, shape, dtype, offset=SB_BASE + int(off_kb * 1024))

    _cur = [SB_BASE]

    def sbp(name, shape, dtype):
        ds = {F32: 4, BF16: 2, FP8: 1}[dtype]
        n = 1
        for s_ in shape[1:]:
            n *= s_
        t = nc.alloc_sbuf_tensor_at(name, shape, dtype, offset=_cur[0])
        _cur[0] += -(-n * ds // 32) * 32
        assert _cur[0] <= SB_BASE + 6 * 1024
        return t

    ones_bf = sbp("ones_bf", [HD, HD], BF16)
    ident = sbp("ident", [HD, HD], BF16)
    identf = sbp("identf", [HD, HD], F32)
    onesrow = sbp("onesrow", [1, 128], F32)
    scp = sbp("scp", [1, 64], F32)
    bc = sbp("bc", [HD, 8], F32)
    grow = sbp("grow", [1, 128], F32)
    pam = sbp("pam", [HD, 40], F32)
    s1a = sbp("s1a", [HD, 24], F32)
    s1b = sbp("s1b", [HD, 24], F32)
    s1m = sbp("s1m", [HD, 24], F32)
    s1r = sbp("s1r", [HD, 24], F32)
    s1t = sbp("s1t", [HD, 24], F32)
    w1 = sbp("wst1", [HD, 16], F32)
    w2 = sbp("wst2", [HD, 16], F32)
    wm = sbp("wstm", [HD, 16], F32)
    wr = sbp("wstr", [HD, 16], F32)
    wt = sbp("wstt", [HD, 16], F32)
    m2a = sbp("m2a", [HD, 16], F32)
    m2b = sbp("m2b", [HD, 16], F32)
    m2m = sbp("m2m", [HD, 16], F32)
    m2r = sbp("m2r", [HD, 16], F32)
    m2t = sbp("m2t", [HD, 16], F32)
    ntb = sbp("ntb", [HD, 16], F32)
    ntb2 = sbp("ntb2", [HD, 16], F32)
    am_x = sbp("am_x", [HD, 16], F32)
    am_r = sbp("am_r", [HD, 1], F32)
    f3a = sbp("f3a", [HD, 16], F32)
    f3b = sbp("f3b", [HD, 16], F32)
    f3m = sbp("f3m", [HD, 16], F32)
    f3r = sbp("f3r", [HD, 16], F32)
    f3t = sbp("f3t", [HD, 16], F32)
    cst_eps = sbp("cst_eps", [HD, 1], F32)
    cst_ncap = sbp("cst_ncap", [HD, 1], F32)

    ax = [sb("ax0", [HD, D], F32, 6), sb("ax1", [HD, D], F32, 14)]
    xnc = [sb("xnc0", [HD, 16, 512], BF16, 22), sb("xnc1", [HD, 16, 512], BF16, 38)]
    xn_tile = sb("xn_tile", [HD, D], F32, 54)
    at3 = sb("at3", [HD, 16, 16, 128], BF16, 6)
    h8 = sb("h8", [HD, 32, 2, 1024], FP8, 6)
    pwb = [sb("pw0", [HD, 1024], F32, 62), sb("pw1", [HD, 1024], F32, 66)]
    p8 = sb("p8", [HD, 1024], FP8, 134)
    wqs = sb("wqs", [HD, 2, 16, 512], BF16, 70)
    xnT_sb = sb("xnT_sb", [HD, 16, 512], BF16, 102)
    qc = sb("qc", [HD, 16 * 512], BF16, 118)
    xn_tile_b = sb("xn_tile_b", [HD, D], F32, 118)
    woT_sb = sb("woT_sb", [HD, 16, D], BF16, 70)
    xq8 = sb("xq8", [HD, 16, N_OWN], FP8, 70)
    f1w = [sb("f1w0", [HD, D], FP8, 102), sb("f1w1", [HD, D], FP8, 104)]
    o2 = sb("o2", [HD, 8, D], F32, 70)
    kT = sb("kT", [HD, HKV * N_TOK], BF16, 134)
    vT = sb("vT", [HD, T_ALL * 512], BF16, 154)
    a4x = [sb("a4x0", [HD, D], F32, 134), sb("a4x1", [HD, D], F32, 142)]
    sq2 = sb("sq2", [HD, D], F32, 150)
    o1b = [sb("o10", [HD, D], F32, 158), sb("o11", [HD, D], F32, 166)]
    f1r = [sb("f1r0", [HD, D], BF16, 134), sb("f1r1", [HD, D], BF16, 138)]
    f2w = [sb("f2w0", [HD, 1024], FP8, 142), sb("f2w1", [HD, 1024], FP8, 143)]
    f2r = [sb("f2r0", [HD, 1024], BF16, 144), sb("f2r1", [HD, 1024], BF16, 146)]
    f2x = [sb("f2x0", [HD, D], F32, 148), sb("f2x1", [HD, D], F32, 156)]
    f2t = sb("f2t", [HD, D], F32, 164)
    tabc = sb("tabc", [HD, 2 * 1024], F32, 174)
    sq1 = sb("sq1", [HD, D], F32, 182)
    mask5 = sb("mask5", [HD, 5 * 512], BF16, 182)
    mpers = sb("mpers", [HD, 2 * 512], BF16, 187)
    pTs = sb("pTs", [HD, 5 * 512], BF16, 189)
    ropa = sb("ropa", [HD, 512], F32, 194)
    ropb = sb("ropb", [HD, 512], F32, 196)
    rinv2 = [sb("rinv", [HD, 512], F32, 198), sb("rinvB", [HD, 512], F32, 205)]
    pTs2 = sb("pTsB", [HD, 5 * 512], BF16, 200)
    xn2_tile = sb("xn2_tile", [HD, D], F32, 182)

    pA = nc.alloc_psum_tensor("pA", [HD, D], F32)
    pB = nc.alloc_psum_tensor("pB", [HD, D], F32)

    SEM_NAMES = ("g_d", "s_d", "cc", "te", "ve", "se", "ge",
                 "d_x0", "d_x1", "st_xnt", "d_xnt", "d_wkv",
                 "d_wq0", "d_wq1", "d_wo", "d_wfc", "d_wp0", "d_wp1",
                 "d_r0", "d_r1", "st_r0", "st_r1", "st_x10", "st_x11",
                 "st_y0", "st_y1", "d_tab0", "d_tab1", "d_mask")
    CHAINQ = {"gpsimd": "g_d", "scalar": "s_d"}
    _chain_last = {}
    for s in SEM_NAMES:
        sch.new_sem(s)

    def dma(q, sem, out, in_, waits=()):
        waits = [w for w in waits if w is not None]
        if q in CHAINQ:
            sem = CHAINQ[q]
            if sem in _chain_last:
                waits = waits + [_chain_last[sem]]
        ev = sch.do(q, lambda e, o=out, i=in_: e.dma_start(out=o, in_=i),
                    waits=waits, inc=sem, amt=16)
        if q in CHAINQ:
            _chain_last[sem] = ev
        return ev

    def vop(fn, waits=()):
        return sch.do("vector", fn, waits=waits, inc="ve")

    def sop(fn, waits=()):
        return sch.do("scalar", fn, waits=waits, inc="se")

    def top(fn, waits=()):
        return sch.do("tensor", fn, waits=waits, inc="te")

    def gop(fn, waits=()):
        return sch.do("gpsimd", fn, waits=waits, inc="ge")

    def coll(kind, op, i_, o_, waits):
        ev = sch.do("gpsimd", lambda e, k=kind, o=op, a=i_, b=o_: e.collective_compute(
            k, o, replica_groups=[list(range(8))], ins=[a], outs=[b]),
            waits=waits + ([_chain_last["g_d"]] if "g_d" in _chain_last else []),
            inc="cc", amt=1)
        return ev

    e_grow = [None]

    def part_max(src_col, slot, waits):
        w = list(waits) + ([e_grow[0]] if e_grow[0] else [])
        e1 = dma("gpsimd", "g_d", col_scr[:, :], src_col, waits=w)
        e2 = dma("gpsimd", "g_d", grow[0:1, 0:128], col_scr[:, :], waits=[e1])
        e3 = vop(lambda e, sl=slot: e.tensor_reduce(
            out=scp[0:1, sl:sl + 1], in_=grow[0:1, 0:128], axis=AX.X, op=ALU.max),
            waits=[e2])
        e_grow[0] = e3
        return e3

    def bcast(slot, col, waits):
        w = list(waits) + ([e_grow[0]] if e_grow[0] else [])
        e1 = vop(lambda e, sl=slot: e.tensor_scalar(
            out=grow[0:1, 0:128], in0=onesrow[0:1, 0:128],
            scalar1=scp[0:1, sl:sl + 1], scalar2=None, op0=ALU.mult), waits=w)
        e2 = dma("gpsimd", "g_d", row_scr[:, :], grow[0:1, 0:128], waits=[e1])
        e3 = dma("gpsimd", "g_d", bc[:, col:col + 1], row_scr[:, :], waits=[e2])
        e_grow[0] = e3
        return e3

    def allred_max(idx, slot_in, slot_out, waits):
        w = list(waits) + ([e_grow[0]] if e_grow[0] else [])
        e1 = vop(lambda e, sl=slot_in: e.tensor_scalar(
            out=grow[0:1, 0:8], in0=onesrow[0:1, 0:8],
            scalar1=scp[0:1, sl:sl + 1], scalar2=None, op0=ALU.mult), waits=w)
        e2 = dma("gpsimd", "g_d", cc_in[idx][:, :], grow[0:1, 0:8], waits=[e1])
        e3 = coll("AllReduce", ALU.max, cc_in[idx][:, :], cc_out[idx][:, :], waits=[e2])
        e4 = dma("gpsimd", "g_d", grow[0:1, 0:8], cc_out[idx][:, :], waits=[e3])
        e5 = vop(lambda e, sl=slot_out: e.tensor_copy(
            out=scp[0:1, sl:sl + 1], in_=grow[0:1, 0:1]), waits=[e4])
        e_grow[0] = e5
        return e5

    e_ones = dma("gpsimd", "g_d", ones_bf[:, :], ones_d[:, :])
    e_id0 = dma("gpsimd", "g_d", ident[:, :], ident_d[:, :])
    e_ident = vop(lambda e: e.tensor_copy(out=identf[:, :], in_=ident[:, :]),
                  waits=[e_id0])
    e_or = gop(lambda e: e.memset(onesrow[0:1, 0:128], 1.0))
    e_ceps = gop(lambda e: e.memset(cst_eps[:, :], float(EPS)), waits=[e_or])
    e_ncap = gop(lambda e: e.memset(cst_ncap[:, :], float(-8.0)), waits=[e_ceps])

    e_wkbf = dma("gpsimd", "g_d", wkT_bf[:, :], wkT_d[:, :])
    e_wvbf = dma("gpsimd", "g_d", wvT_bf[:, :], wvT_d[:, :])
    e_wqbf = []
    for hq in range(4):
        e_wqbf.append(dma("gpsimd", "g_d", wqT_bf[:, hq * 512:(hq + 1) * 512],
                          wqT_d[:, hq * 512:(hq + 1) * 512]))
    e_wobf = dma("gpsimd", "g_d", woT_bf[:, :], woT_d[:, :])

    def prep_src(j):
        if j < 16:
            return wfc_p[j // 2, :, (j % 2) * 1024:(j % 2 + 1) * 1024]
        i, q = (j - 16) // 4, (j - 16) % 4
        return wp_p[i, q, :, :]

    def prep_dst(j):
        if j < 16:
            return wfc8_ag[(j // 2) * 128:(j // 2 + 1) * 128,
                           (j % 2) * 1024:(j % 2 + 1) * 1024]
        i, q = (j - 16) // 4, (j - 16) % 4
        base = (i * 4 + q) * 128
        return wp8_ag[base:base + 128, :]

    prep_state = {"red": {}, "last_red": None}

    def emit_amax(j):
        w = []
        if j >= 2:
            w.append(prep_state["red"][j - 2])
        el = dma("gpsimd", "g_d", pwb[j % 2][:, :], prep_src(j), waits=w)
        er = vop(lambda e, j=j: e.tensor_reduce(
            out=pam[:, j:j + 1], in_=pwb[j % 2][:, :], axis=AX.X, op=ALU.max,
            apply_absolute_value=True), waits=[el])
        prep_state["red"][j] = er
        prep_state["last_red"] = er

    at4 = at3[:, :, :, :]
    wkv3 = wqs[:, :, :, :].rearrange("p a c m -> p (a c) m")
    e_wkv = dma("sync", "d_wkv", wkv3[:, 0:16, :],
                wkT_bf[:, :].rearrange("(cc p) m -> p cc m", cc=16), waits=[e_wkbf])
    e_wvv = dma("sync", "d_wkv", wkv3[:, 16:32, :],
                wvT_bf[:, :].rearrange("(cc p) m -> p cc m", cc=16), waits=[e_wvbf])

    def rope_evac(pt, pcol, tb, out_ap, waits):
        ea = vop(lambda e: e.tensor_tensor(
            out=ropa[:, :], in0=pt[0:128, pcol:pcol + 512],
            in1=tabc[0:128, tb:tb + 512], op=ALU.mult), waits=waits)
        eb = vop(lambda e: e.tensor_tensor(
            out=ropb[0:64, :], in0=pt[64:128, pcol:pcol + 512],
            in1=tabc[0:64, tb + 512:tb + 1024], op=ALU.mult), waits=[ea])
        ec = vop(lambda e: e.tensor_tensor(
            out=ropb[64:128, :], in0=pt[0:64, pcol:pcol + 512],
            in1=tabc[64:128, tb + 512:tb + 1024], op=ALU.mult), waits=[eb])
        ed = vop(lambda e: e.tensor_tensor(
            out=out_ap, in0=ropa[:, :], in1=ropb[:, :], op=ALU.add), waits=[ec])
        return ed

    ev_tab_prev = None
    e_prev_kmm = None
    e_prev_rope = None
    e_prev_vcopy = None
    ev_xload = {}
    ev_xn_ecp = {}
    ev_xnc_store = {}
    e_exn = {}
    ev_sq = {}
    e_tp_evac = {"A": None, "B": None}
    e_sq1_rd = None
    e_ln_prev = None
    for c in range(5):
        c0 = c * 4
        for tt in range(4):
            t = c0 + tt
            w = []
            if t >= 2:
                w.append(e_exn.get(t - 2))
                w.append(ev_sq.get(t - 2))
            el = dma("sync", f"d_x{t % 2}", ax[t % 2][:, :], x_in[t * 128:(t + 1) * 128, :],
                     waits=w)
            ev_xload[t] = el
            e1 = vop(lambda e, b=ax[t % 2], t=t: e.reduce_sum(
                out=s1a[:, t:t + 1], in_=b[:, :], axis=AX.X), waits=[el])
            wsq = [el]
            if e_sq1_rd is not None:
                wsq.append(e_sq1_rd)
            esq = sop(lambda e, b=ax[t % 2]: e.activation(out=sq1[:, :], in_=b[:, :],
                                                          func=AF.Square), waits=wsq)
            e2 = vop(lambda e, t=t: e.reduce_sum(out=s1b[:, t:t + 1], in_=sq1[:, :],
                                                 axis=AX.X), waits=[esq])
            e_sq1_rd = e2
            ev_sq[t] = e2
            eM = vop(lambda e, t=t: e.tensor_scalar(
                out=s1m[:, t:t + 1], in0=s1a[:, t:t + 1], scalar1=float(1.0 / D),
                scalar2=None, op0=ALU.mult), waits=[e1])
            eE = vop(lambda e, t=t: e.tensor_scalar(
                out=s1b[:, t:t + 1], in0=s1b[:, t:t + 1], scalar1=float(1.0 / D),
                scalar2=None, op0=ALU.mult), waits=[e2])
            eQ = vop(lambda e, t=t: e.tensor_tensor(
                out=s1t[:, t:t + 1], in0=s1m[:, t:t + 1], in1=s1m[:, t:t + 1],
                op=ALU.mult), waits=[eM])
            eV = vop(lambda e, t=t: e.tensor_tensor(
                out=s1b[:, t:t + 1], in0=s1b[:, t:t + 1], in1=s1t[:, t:t + 1],
                op=ALU.subtract), waits=[eE, eQ])
            wl_ = [eV, e_ceps]
            if e_ln_prev is not None:
                wl_.append(e_ln_prev)
            eL = sop(lambda e, t=t: e.activation(out=s1r[:, t:t + 1], in_=s1b[:, t:t + 1],
                                                 func=AF.Ln, bias=cst_eps[:, 0:1]),
                     waits=wl_)
            eR = sop(lambda e, t=t: e.activation(out=s1r[:, t:t + 1], in_=s1r[:, t:t + 1],
                                                 func=AF.Exp, scale=-0.5), waits=[eL])
            e_ln_prev = eR
            xnb = xn_tile if (t % 2 == 0) else xn_tile_b
            wn = [eR, el]
            if t >= 2:
                wn.append(ev_xn_ecp.get(t - 2))
            exn = vop(lambda e, b=ax[t % 2], o=xnb, t=t: e.tensor_scalar(
                out=o[:, :], in0=b[:, :], scalar1=s1m[:, t:t + 1],
                scalar2=s1r[:, t:t + 1], op0=ALU.subtract, op1=ALU.mult), waits=wn)
            e_exn[t] = exn
            ps = pA if (tt % 2 == 0) else pB
            key = "A" if (tt % 2 == 0) else "B"
            wtp = [exn, e_ident]
            if e_tp_evac[key] is not None:
                wtp.append(e_tp_evac[key])
            if c > 0:
                if key == "A" and e_prev_rope is not None:
                    wtp.append(e_prev_rope)
                if key == "B" and e_prev_vcopy is not None:
                    wtp.append(e_prev_vcopy)
            etp = None
            for cc in range(16):
                etp = top(lambda e, ps=ps, o=xnb, cc=cc: e.matmul(
                    ps[:, cc * 128:(cc + 1) * 128], o[:, cc * 128:(cc + 1) * 128],
                    identf[:, :], start=True, stop=True, is_transpose=True),
                    waits=(wtp if cc == 0 else []))
            wev = [etp]
            if c >= 2:
                wev.append(e_prev_kmm)
                wev.append(ev_xnc_store.get(c - 2))
            ecp = sop(lambda e, ps=ps, c=c, tt=tt: e.copy(
                out=xnc[c % 2][:, :, tt * 128:(tt + 1) * 128],
                in_=ps[:, :].rearrange("p (cc m) -> p cc m", cc=16)), waits=wev)
            ev_xn_ecp[t] = ecp
            e_tp_evac[key] = ecp
        ev_xnc_store[c] = dma("scalar", "st_xnt",
                              xnT_scr[:, :, c * 512:(c + 1) * 512]
                              .rearrange("cc p t -> p cc t"),
                              xnc[c % 2][:, :, :], waits=[ev_xn_ecp[c0 + 3]])
        tb = (c % 2) * 1024
        wtab = []
        if ev_tab_prev:
            wtab.append(ev_tab_prev)
        if c >= 2:
            wtab.append(e_prev_rope)
        etc = dma("sync", f"d_tab{c % 2}", tabc[:, tb:tb + 512],
                  cosT_d[:, c * 512:(c + 1) * 512], waits=wtab)
        ets = dma("sync", f"d_tab{c % 2}", tabc[:, tb + 512:tb + 1024],
                  sinT_d[:, c * 512:(c + 1) * 512], waits=[etc])
        ev_tab_prev = ets
        mmw = [ev_xn_ecp[c0 + 3], e_wkv, e_wvv]
        if e_prev_rope is not None:
            mmw += [e_prev_rope, e_prev_vcopy]
        ek = []
        for hh in range(HKV):
            for cc in range(16):
                ek.append(top(lambda e, hh=hh, cc=cc, c=c: e.matmul(
                    pA[:, hh * 512:(hh + 1) * 512],
                    wkv3[:, cc, hh * 128:(hh + 1) * 128],
                    xnc[c % 2][:, cc, :],
                    start=(cc == 0), stop=(cc == 15)),
                    waits=(mmw if (hh == 0 and cc == 0) else [])))
        evm = []
        for i in range(4):
            for cc in range(16):
                evm.append(top(lambda e, i=i, cc=cc, c=c: e.matmul(
                    pB[:, i * 512:(i + 1) * 512],
                    xnc[c % 2][:, cc, i * 128:(i + 1) * 128],
                    wkv3[:, 16 + cc, :],
                    start=(cc == 0), stop=(cc == 15)), waits=[]))
        e_prev_kmm = evm[-1]
        for hh in range(HKV):
            w = [ek[hh * 16 + 15], ets]
            if e_prev_rope is not None:
                w.append(e_prev_rope)
            e_prev_rope = rope_evac(
                pA, hh * 512, tb,
                kT[:, hh * N_TOK + c * 512:hh * N_TOK + c * 512 + 512], w)
        for i in range(4):
            w = [evm[i * 16 + 15]]
            if e_prev_vcopy is not None:
                w.append(e_prev_vcopy)
            e_prev_vcopy = sop(lambda e, i=i, c=c: e.copy(
                out=vT[:, (c * 4 + i) * 512:(c * 4 + i) * 512 + 512],
                in_=pB[:, i * 512:(i + 1) * 512]), waits=w)
        for j in range(c * 7, min(32, c * 7 + 7)):
            emit_amax(j)
    e_a1_alldone = ev_xn_ecp[19]
    e_fam = vop(lambda e: e.tensor_reduce(
        out=pam[:, 32:33], in_=pam[:, 0:16], axis=AX.X, op=ALU.max),
        waits=[prep_state["last_red"]])
    e_pam = vop(lambda e: e.tensor_reduce(
        out=pam[:, 33:34], in_=pam[:, 16:32], axis=AX.X, op=ALU.max), waits=[e_fam])
    e_fm = part_max(pam[:, 32:33], 0, [e_fam])
    e_pm = part_max(pam[:, 33:34], 1, [e_pam])
    e_r0 = vop(lambda e: e.tensor_scalar(
        out=grow[0:1, 0:8], in0=onesrow[0:1, 0:8], scalar1=scp[0:1, 0:1],
        scalar2=None, op0=ALU.mult), waits=[e_or, e_fm, e_pm, e_grow[0]])
    e_r1 = vop(lambda e: e.tensor_scalar(
        out=grow[0:1, 1:2], in0=onesrow[0:1, 1:2], scalar1=scp[0:1, 1:2],
        scalar2=None, op0=ALU.mult), waits=[e_r0])
    e_grow[0] = e_r1
    e_ci = dma("gpsimd", "g_d", cc_in[0][:, :], grow[0:1, 0:8], waits=[e_r1])
    e_ar1 = coll("AllReduce", ALU.max, cc_in[0][:, :], cc_out[0][:, :], waits=[e_ci])
    e_mp0 = dma("sync", "d_mask", mpers[:, 0:512], masks_d[20, :, :],
                waits=[e_a1_alldone])
    e_mp = dma("sync", "d_mask", mpers[:, 512:1024], masks_d[21, :, :], waits=[e_mp0])

    qc3 = qc[:, :].rearrange("p (h t) -> p h t", h=HQ)
    at4f = at3[:, :, :, :].rearrange("p a b c -> p (a b c)")
    ev_a3_attn_last = None
    e_prev_qrope = e_prev_rope
    e_attn_mm_last = None
    ev_mask_prev = None
    e_tanh = {}
    e_exp_prev = None
    e_pT_use = {}
    e_sum_rd = {0: None, 1: None}
    e_pv_rd = {0: None, 1: None}
    e_rinv_rd = {0: None, 1: None}
    e_first_score = True
    e_mask_tt_last = None
    e_xnt_rd = None
    prev_chunk_qmm = None
    e_q_mm_prev = None
    e_prep_sbuf_free = None
    e_p8_free = None
    pending_ea = [None]
    A3S = {"last": None}

    def flush_ea():
        if pending_ea[0] is not None:
            fn, waits, pp = pending_ea[0]
            pending_ea[0] = None
            ea = vop(fn, waits=waits)
            A3S["last"] = ea
            e_pv_rd[pp] = ea
            e_rinv_rd[pp] = ea
        return A3S["last"]

    for c in range(NCH):
        if c == 2:
            e_c1l = dma("gpsimd", "g_d", grow[0:1, 0:8], cc_out[0][:, :],
                        waits=[e_ar1, e_grow[0]])
            e_grow[0] = e_c1l
            e_swf = vop(lambda e: e.tensor_scalar(
                out=scp[0:1, 2:3], in0=grow[0:1, 0:1], scalar1=float(1.0 / FMAX),
                scalar2=1e-6, op0=ALU.mult, op1=ALU.max), waits=[e_c1l])
            e_swp = vop(lambda e: e.tensor_scalar(
                out=scp[0:1, 4:5], in0=grow[0:1, 1:2], scalar1=float(1.0 / FMAX),
                scalar2=1e-6, op0=ALU.mult, op1=ALU.max), waits=[e_swf])
            e_grow[0] = e_swp
            e_k0 = vop(lambda e: e.reciprocal(out=scp[0:1, 3:4], in_=scp[0:1, 2:3]),
                       waits=[e_swf])
            e_kwf = vop(lambda e: e.tensor_scalar(
                out=scp[0:1, 3:4], in0=scp[0:1, 3:4], scalar1=0.5, scalar2=None,
                op0=ALU.mult), waits=[e_k0])
            e_k1 = vop(lambda e: e.reciprocal(out=scp[0:1, 5:6], in_=scp[0:1, 4:5]),
                       waits=[e_swp])
            e_kwp = vop(lambda e: e.tensor_scalar(
                out=scp[0:1, 5:6], in0=scp[0:1, 5:6], scalar1=0.5, scalar2=None,
                op0=ALU.mult), waits=[e_k1])
            e_bwf = bcast(3, 2, [e_kwf])
            e_bwp = bcast(5, 3, [e_kwp])
            e_q_prev = {}
            e_st_prev = None
            for j in range(32):
                w = [e_bwf if j < 16 else e_bwp]
                if j >= 2:
                    w.append(e_q_prev[j - 2])
                el = dma("gpsimd", "g_d", pwb[j % 2][:, :], prep_src(j), waits=w)
                eq = vop(lambda e, j=j: e.tensor_scalar(
                    out=p8[:, :], in0=pwb[j % 2][:, :],
                    scalar1=bc[:, 2:3] if j < 16 else bc[:, 3:4],
                    scalar2=None, op0=ALU.mult), waits=[el, e_bwf, e_bwp] +
                    ([e_st_prev] if e_st_prev else []))
                e_q_prev[j] = eq
                e_st_prev = dma("gpsimd", "g_d", prep_dst(j), p8[:, :], waits=[eq])
            e_prep_sbuf_free = e_q_prev[31]
            e_p8_free = e_st_prev
            e_ag1 = coll("AllGather", ALU.bypass, wfc8_ag[:, :], wfc8[:, :],
                         waits=[e_st_prev])
            e_ag2 = coll("AllGather", ALU.bypass, wp8_ag[:, :], wp8[:, :], waits=[e_ag1])
        tb = (c % 2) * 1024
        wtab = [ev_tab_prev]
        wtab.append(e_prev_qrope if c >= 1 else e_prev_rope)
        etc = dma("sync", f"d_tab{c % 2}", tabc[:, tb:tb + 512],
                  cosT_d[:, 512 + c * 512:512 + (c + 1) * 512], waits=wtab)
        ets = dma("sync", f"d_tab{c % 2}", tabc[:, tb + 512:tb + 1024],
                  sinT_d[:, 512 + c * 512:512 + (c + 1) * 512], waits=[etc])
        ev_tab_prev = ets
        wxl = [ev_xnc_store[c + 1]]
        if e_xnt_rd is not None:
            wxl.append(e_xnt_rd)
        exl = dma("sync", "d_xnt", xnT_sb[:, :, :],
                  xnT_scr[:, :, 512 + c * 512:512 + (c + 1) * 512]
                  .rearrange("cc p t -> p cc t"), waits=wxl)
        eq_mm_done = []
        for hq in range(4):
            wb = hq % 2
            w = [e_wqbf[hq]]
            if c == 0 and hq < 2:
                w.append(e_prev_kmm)
            if hq >= 2:
                w.append(eq_mm_done[hq - 2])
            elif c > 0:
                w.append(prev_chunk_qmm[hq - 2])
            ewq = dma("sync", f"d_wq{hq % 2}", wqs[:, wb, :, :],
                      wqT_bf[:, hq * 512:(hq + 1) * 512]
                      .rearrange("(cc p) m -> p cc m", cc=16), waits=w)
            mmw = [exl, ewq]
            if e_prev_qrope is not None:
                mmw.append(e_prev_qrope)
            if c == 0 and hq == 0:
                mmw.append(e_prev_kmm)
                mmw.append(e_prev_rope)
            if c > 0 and hq == 0:
                mmw.append(e_exp_prev)
                mmw.append(flush_ea())
            emm = []
            for cc in range(16):
                for hh in range(4):
                    emm.append(top(lambda e, cc=cc, hh=hh, wb=wb: e.matmul(
                        pA[:, hh * 512:(hh + 1) * 512],
                        wqs[:, wb, cc, hh * 128:(hh + 1) * 128],
                        xnT_sb[:, cc, :],
                        start=(cc == 0), stop=(cc == 15)),
                        waits=(mmw if (cc == 0 and hh == 0) else [])))
            eq_mm_done.append(emm[-1])
            e_xnt_rd = emm[-1]
            for hh in range(4):
                hg = hq * 4 + hh
                w = [emm[15 * 4 + hh], ets]
                if e_prev_qrope is not None:
                    w.append(e_prev_qrope)
                e_prev_qrope = rope_evac(pA, hh * 512, tb,
                                         qc[:, hg * 512:hg * 512 + 512], w)
        prev_chunk_qmm = eq_mm_done
        e_q_mm_prev = eq_mm_done[-1]
        for i in range(4):
            n = 4 * c + i
            if n < 4:
                ml = []
                for j in range(5):
                    if j == 0:
                        w0 = [e_mp, e_a1_alldone]
                        if e_mask_tt_last is not None:
                            w0 = [e_mask_tt_last]
                    else:
                        w0 = [ml[-1]]
                    ml.append(dma("sync", "d_mask", mask5[:, j * 512:j * 512 + 512],
                                  masks_d[n * 5 + j, :, :], waits=w0))
                ev_mask_prev = ml[-1]
            for g in range(HKV):
                rhs_q = qc3[:, 4 * g:4 * g + 4, i * 128:(i + 1) * 128]
                p2 = (n * 4 + g) % 2
                e_expj = {}
                PS = pB if p2 == 0 else pA
                pt = pTs if p2 == 0 else pTs2
                JORD = (1, 2, 3, 0, 4) if n >= 4 else (0, 1, 2, 3, 4)
                for j in JORD:
                    kt = n + j
                    bk = (j % 2)
                    w = [e_prev_qrope]
                    if e_first_score:
                        w.append(e_prev_vcopy)
                        e_first_score = False
                    if e_tanh.get((p2, bk)) is not None:
                        w.append(e_tanh[(p2, bk)])
                    emm = top(lambda e, kt=kt, g=g, bk=bk, r=rhs_q, PS=PS: e.matmul(
                        PS[:, bk * 512:bk * 512 + 512],
                        kT[:, g * N_TOK + kt * 128:g * N_TOK + kt * 128 + 128], r,
                        start=True, stop=True), waits=w)
                    wt_ = [emm]
                    if n == 0 and g == 0 and j == 0:
                        wt_.append(e_a1_alldone)
                    if e_pT_use.get((p2, j)) is not None:
                        wt_.append(e_pT_use[(p2, j)])
                    et = sop(lambda e, bk=bk, j=j, PS=PS, pt=pt: e.activation(
                        out=pt[:, j * 512:j * 512 + 512],
                        in_=PS[:, bk * 512:bk * 512 + 512], func=AF.Tanh,
                        scale=float(1.0 / (CAP * float(np.sqrt(HD))))), waits=wt_)
                    e_tanh[(p2, bk)] = et
                    em = None
                    if n < 4:
                        em = vop(lambda e, j=j, pt=pt: e.tensor_tensor(
                            out=pt[:, j * 512:j * 512 + 512],
                            in0=pt[:, j * 512:j * 512 + 512],
                            in1=mask5[:, j * 512:j * 512 + 512], op=ALU.add),
                            waits=[et, ev_mask_prev])
                        e_mask_tt_last = em
                    elif j == 0:
                        em = vop(lambda e, pt=pt: e.tensor_tensor(
                            out=pt[:, 0:512], in0=pt[:, 0:512],
                            in1=mpers[:, 0:512], op=ALU.add), waits=[et, e_mp])
                    elif j == 4:
                        em = vop(lambda e, pt=pt: e.tensor_tensor(
                            out=pt[:, 4 * 512:5 * 512], in0=pt[:, 4 * 512:5 * 512],
                            in1=mpers[:, 512:1024], op=ALU.add), waits=[et, e_mp])
                    we_ = [em if em is not None else et, e_ncap]
                    ee = sop(lambda e, j=j, pt=pt: e.activation(
                        out=pt[:, j * 512:j * 512 + 512],
                        in_=pt[:, j * 512:j * 512 + 512],
                        func=AF.Exp, scale=float(CAP), bias=cst_ncap[:, 0:1]), waits=we_)
                    e_exp_prev = ee
                    e_expj[j] = ee
                flush_ea()
                e_sum_last = None
                e_pv_last = None
                for ji, j in enumerate(JORD):
                    kt = n + j
                    first, last = ji == 0, ji == 4
                    wsum = [e_expj[j]]
                    if first and e_sum_rd[p2] is None:
                        wsum.append(e_ones)
                    if first and e_sum_rd[p2] is not None:
                        wsum.append(e_sum_rd[p2])
                    e_sum_last = top(lambda e, j=j, PS=PS, pt=pt, f=first, l=last: e.matmul(
                        PS[:, 2 * 512:3 * 512], ones_bf[:, :],
                        pt[:, j * 512:j * 512 + 512],
                        start=f, stop=l), waits=wsum)
                    wpv = []
                    if first and e_pv_rd[p2] is not None:
                        wpv.append(e_pv_rd[p2])
                    e_pv_last = top(lambda e, j=j, kt=kt, g=g, PS=PS, pt=pt, f=first,
                                    l=last: e.matmul(
                        PS[:, 3 * 512:4 * 512],
                        vT[:, kt * 512 + g * 128:kt * 512 + g * 128 + 128],
                        pt[:, j * 512:j * 512 + 512],
                        start=f, stop=l), waits=wpv)
                    e_pT_use[(p2, j)] = e_pv_last
                e_attn_mm_last = e_pv_last
                w = [e_sum_last]
                if e_rinv_rd[p2] is not None:
                    w.append(e_rinv_rd[p2])
                el1 = vop(lambda e, PS=PS, p2=p2: e.reciprocal(
                    out=rinv2[p2][:, :], in_=PS[:, 2 * 512:3 * 512]), waits=w)
                e_sum_rd[p2] = el1
                w = [e_pv_last, el1]
                if A3S["last"] is not None:
                    w.append(A3S["last"])
                if n == 0 and g == 0:
                    w.append(e_a1_alldone)
                if n == 14 and g == 0:
                    w.append(e_prep_sbuf_free)
                pending_ea[0] = (
                    (lambda e, g=g, n=n, PS=PS, p2=p2: e.tensor_tensor(
                        out=at4f[:, (n * 16 + 4 * g) * 128:(n * 16 + 4 * g + 4) * 128],
                        in0=PS[:, 3 * 512:4 * 512], in1=rinv2[p2][:, :], op=ALU.mult)),
                    w, p2)
        flush_ea()
    ev_a3_attn_last_final = A3S["last"]

    e_wot = []
    for kq in range(4):
        w = [e_wobf, e_q_mm_prev]
        if kq == 3:
            w.append(e_attn_mm_last)
        e_wot.append(dma("sync", "d_wo", woT_sb[:, kq * 4:(kq + 1) * 4, :],
                         woT_bf[kq * 512:(kq + 1) * 512, :]
                         .rearrange("(ck p) m -> p ck m", ck=4), waits=w))
    ev_x1 = []
    e_pa_free = None
    e_sq_rd = None
    e_ecp2_prev = None
    e_wo_mm = {}
    e_xn2_rd = None
    for n in range(T_OWN):
        mm = []
        for ck in range(16):
            for fq in range(4):
                w = []
                if ck == 0 and fq == 0:
                    w = [e_wot[-1], ev_a3_attn_last_final]
                    if e_pa_free is not None:
                        w.append(e_pa_free)
                mm.append(top(lambda e, ck=ck, fq=fq, n=n: e.matmul(
                    pA[:, fq * 512:(fq + 1) * 512],
                    at4[:, n, ck, :],
                    woT_sb[:, ck, fq * 512:(fq + 1) * 512],
                    start=(ck == 0), stop=(ck == 15)), waits=w))
        e_wo_mm[n] = mm[-1]
        o1 = o1b[n % 2]
        w = [mm[-1]]
        if n < 2:
            w.append(e_attn_mm_last)
        if n >= 2:
            w.append(ev_x1[n - 2][1])
        eo = sop(lambda e, o=o1: e.copy(out=o[:, :], in_=pA[:, :]), waits=w)
        e_pa_free = eo
        es1 = vop(lambda e, o=o1, n=n: e.reduce_sum(out=w1[:, n:n + 1], in_=o[:, :],
                                                    axis=AX.X), waits=[eo])
        wsq = [eo]
        if e_sq_rd is not None:
            wsq.append(e_sq_rd)
        if n == 0:
            wsq.append(e_attn_mm_last)
        esq = sop(lambda e, o=o1: e.activation(out=sq2[:, :], in_=o[:, :], func=AF.Square),
                  waits=wsq)
        es2 = vop(lambda e, n=n: e.reduce_sum(out=w2[:, n:n + 1], in_=sq2[:, :], axis=AX.X),
                  waits=[esq])
        e_sq_rd = es2
        em_ = vop(lambda e, n=n: e.tensor_scalar(
            out=wm[:, n:n + 1], in0=w1[:, n:n + 1], scalar1=float(1.0 / D), scalar2=None,
            op0=ALU.mult), waits=[es1])
        eq_ = vop(lambda e, n=n: e.tensor_tensor(
            out=wt[:, n:n + 1], in0=wm[:, n:n + 1], in1=wm[:, n:n + 1], op=ALU.mult),
            waits=[em_])
        ee_ = vop(lambda e, n=n: e.tensor_scalar(
            out=w2[:, n:n + 1], in0=w2[:, n:n + 1], scalar1=float(1.0 / D), scalar2=None,
            op0=ALU.mult), waits=[es2])
        ev_ = vop(lambda e, n=n: e.tensor_tensor(
            out=w2[:, n:n + 1], in0=w2[:, n:n + 1], in1=wt[:, n:n + 1], op=ALU.subtract),
            waits=[ee_, eq_])
        el_ = sop(lambda e, n=n: e.activation(out=wr[:, n:n + 1], in_=w2[:, n:n + 1],
                                              func=AF.Ln, bias=cst_eps[:, 0:1]), waits=[ev_])
        er_ = sop(lambda e, n=n: e.activation(out=wr[:, n:n + 1], in_=wr[:, n:n + 1],
                                              func=AF.Exp, scale=-0.5), waits=[el_])
        xb = a4x[n % 2]
        wx = [e_attn_mm_last] if n < 2 else [ev_x1[n - 2][3], ev_x1[n - 2][5]]
        ex = dma("sync", f"d_x{n % 2}", xb[:, :], x_in[512 + n * 128:512 + (n + 1) * 128, :],
                 waits=wx)
        et_ = vop(lambda e, o=o1, n=n: e.tensor_scalar(
            out=sq2[:, :], in0=o[:, :], scalar1=wm[:, n:n + 1], scalar2=wr[:, n:n + 1],
            op0=ALU.subtract, op1=ALU.mult), waits=[er_, es2, eo])
        ex1 = vop(lambda e, xb=xb: e.tensor_tensor(
            out=xb[:, :], in0=sq2[:, :], in1=xb[:, :], op=ALU.add), waits=[et_, ex, es1])
        es_ = dma("sync", f"st_x1{n % 2}", x1_scr[n * 128:(n + 1) * 128, :], xb[:, :],
                  waits=[ex1])
        eb1 = vop(lambda e, xb=xb, n=n: e.reduce_sum(
            out=m2a[:, n:n + 1], in_=xb[:, :], axis=AX.X), waits=[ex1])
        esqb = sop(lambda e, xb=xb: e.activation(out=sq2[:, :], in_=xb[:, :],
                                                 func=AF.Square), waits=[ex1])
        eb2 = vop(lambda e, n=n: e.reduce_sum(
            out=m2b[:, n:n + 1], in_=sq2[:, :], axis=AX.X), waits=[esqb])
        e_sq_rd = eb2
        eb3 = vop(lambda e, n=n: e.tensor_scalar(
            out=m2m[:, n:n + 1], in0=m2a[:, n:n + 1], scalar1=float(1.0 / D),
            scalar2=None, op0=ALU.mult), waits=[eb1])
        eb4 = vop(lambda e, n=n: e.tensor_tensor(
            out=m2t[:, n:n + 1], in0=m2m[:, n:n + 1], in1=m2m[:, n:n + 1], op=ALU.mult),
            waits=[eb3])
        eb5 = vop(lambda e, n=n: e.tensor_scalar(
            out=m2b[:, n:n + 1], in0=m2b[:, n:n + 1], scalar1=float(1.0 / D),
            scalar2=None, op0=ALU.mult), waits=[eb2])
        eb6 = vop(lambda e, n=n: e.tensor_tensor(
            out=m2b[:, n:n + 1], in0=m2b[:, n:n + 1], in1=m2t[:, n:n + 1],
            op=ALU.subtract), waits=[eb4, eb5])
        eb7 = sop(lambda e, n=n: e.activation(out=m2r[:, n:n + 1], in_=m2b[:, n:n + 1],
                                              func=AF.Ln, bias=cst_eps[:, 0:1]), waits=[eb6])
        eb8 = sop(lambda e, n=n: e.activation(out=m2r[:, n:n + 1], in_=m2r[:, n:n + 1],
                                              func=AF.Exp, scale=-0.5), waits=[eb7])
        wxn2 = [eb8, ex1, eb2]
        if e_xn2_rd is not None:
            wxn2.append(e_xn2_rd)
        if n == 0:
            wxn2.append(e_a1_alldone)
        exn2 = vop(lambda e, xb=xb, n=n: e.tensor_scalar(
            out=xn2_tile[:, :], in0=xb[:, :], scalar1=m2m[:, n:n + 1],
            scalar2=m2r[:, n:n + 1], op0=ALU.subtract, op1=ALU.mult), waits=wxn2)
        ea2 = vop(lambda e, n=n: e.tensor_reduce(
            out=am_x[:, n:n + 1], in_=xn2_tile[:, :], axis=AX.X, op=ALU.max,
            apply_absolute_value=True), waits=[exn2])
        wtp = [exn2]
        if n == 0:
            wtp.append(ev_a3_attn_last_final)
        else:
            wtp.append(e_ecp2_prev)
        etp = None
        for cc in range(16):
            etp = top(lambda e, cc=cc: e.matmul(
                pB[:, cc * 128:(cc + 1) * 128], xn2_tile[:, cc * 128:(cc + 1) * 128],
                identf[:, :], start=True, stop=True, is_transpose=True),
                waits=(wtp if cc == 0 else []))
        ecp2 = sop(lambda e, n=n: e.copy(
            out=at4[:, n, :, :],
            in_=pB[:, :].rearrange("p (cc m) -> p cc m", cc=16)),
            waits=[etp, e_wo_mm[n]])
        e_ecp2_prev = ecp2
        e_xn2_rd = ecp2
        ev_x1.append((ex, eo, ex1, es_, ea2, exn2))
    e_wo_last = e_wo_mm[T_OWN - 1]

    eax2 = vop(lambda e: e.tensor_reduce(
        out=am_x[:, 0:1], in_=am_x[:, 0:16], axis=AX.X, op=ALU.max),
        waits=[ev_x1[-1][4], ev_x1[-2][4]])
    e_gx = part_max(am_x[:, 0:1], 8, [eax2])
    e_arx = allred_max(1, 8, 9, [e_gx])
    e_sx = vop(lambda e: e.tensor_scalar(
        out=scp[0:1, 10:11], in0=scp[0:1, 9:10], scalar1=float(1.0 / FMAX), scalar2=1e-6,
        op0=ALU.mult, op1=ALU.max), waits=[e_arx])
    e_kx0 = vop(lambda e: e.reciprocal(out=scp[0:1, 11:12], in_=scp[0:1, 10:11]), waits=[e_sx])
    e_kx = vop(lambda e: e.tensor_scalar(
        out=scp[0:1, 11:12], in0=scp[0:1, 11:12], scalar1=0.5, scalar2=None, op0=ALU.mult),
        waits=[e_kx0])
    e_bkx = bcast(11, 0, [e_kx])
    e_s1a_ = vop(lambda e: e.tensor_tensor(
        out=scp[0:1, 12:13], in0=scp[0:1, 10:11], in1=scp[0:1, 2:3], op=ALU.mult),
        waits=[e_sx, e_swf])
    e_s1b_ = vop(lambda e: e.tensor_scalar(
        out=scp[0:1, 12:13], in0=scp[0:1, 12:13], scalar1=4.0, scalar2=None, op0=ALU.mult),
        waits=[e_s1a_])
    e_bs1 = bcast(12, 1, [e_s1b_])
    ev_qz = []
    for cc in range(16):
        w = [e_bkx, e_ecp2_prev]
        if cc < 2:
            w.append(e_wo_last)
        if cc % 2 == 0:
            eqz = sop(lambda e, cc=cc: e.activation(
                out=xq8[:, cc, :].rearrange("p (n m) -> p n m", n=16),
                in_=at4[:, :, cc, :], func=AF.Copy, scale=bc[:, 0:1]), waits=w)
        else:
            eqz = vop(lambda e, cc=cc: e.tensor_scalar(
                out=xq8[:, cc, :].rearrange("p (n m) -> p n m", n=16),
                in0=at4[:, :, cc, :], scalar1=bc[:, 0:1], scalar2=None,
                op0=ALU.mult), waits=w)
        ev_qz.append(eqz)
    e_qz_all = vop(lambda e: e.tensor_copy(out=grow[0:1, 8:9], in_=grow[0:1, 0:1]),
                   waits=[ev_qz[-1], ev_qz[-2], e_grow[0]])
    e_grow[0] = e_qz_all

    xq8v = xq8[:, :, :]
    ev_f1 = []
    e_bank_free = {0: e_pa_free, 1: e_ecp2_prev}
    e_r_prev = {0: None, 1: None}
    e_amr0 = vop(lambda e: e.memset(am_r[:, :], 0.0), waits=[e_qz_all])
    e_amr_prev = e_amr0
    ev_wl = []
    e_f1_mm = {}
    for mc in range(64):
        w = [e_ag1]
        if mc < 2:
            w.append(e_wo_last)
        if mc >= 2:
            w.append(e_f1_mm[mc - 2])
        ewl = dma("sync", "d_wfc", f1w[mc % 2][:, :], wfc8[mc * 128:(mc + 1) * 128, :],
                  waits=w)
        ev_wl.append(ewl)
        ps = pA if (mc % 2 == 0) else pB
        emm = None
        for cc2 in range(8):
            lhs = f1w[mc % 2][:, cc2 * 256:(cc2 + 1) * 256].rearrange(
                "p (kk m) -> p kk m", kk=2)
            for tq in range(4):
                w2_ = []
                if cc2 == 0 and tq == 0:
                    w2_ = [ewl, e_qz_all]
                    if e_bank_free[mc % 2] is not None:
                        w2_.append(e_bank_free[mc % 2])
                emm = top(lambda e, ps=ps, lhs=lhs, cc2=cc2, tq=tq: e.matmul(
                    ps[:, tq * 512:(tq + 1) * 512], lhs,
                    xq8v[:, 2 * cc2:2 * cc2 + 2, tq * 512:(tq + 1) * 512],
                    start=(cc2 == 0), stop=(cc2 == 7), perf_mode=DR), waits=w2_)
        e_f1_mm[mc] = emm
        rb = f1r[mc % 2]
        w = [emm, e_bs1]
        if mc < 2:
            w.append(ev_x1[-1][3])
            w.append(ev_x1[-2][3])
            w.append(ev_x1[-1][5])
            w.append(ev_x1[-2][5])
        if mc == 0:
            w.append(e_p8_free)
        if e_r_prev[mc % 2] is not None:
            w.append(e_r_prev[mc % 2])
        ee = vop(lambda e, ps=ps, rb=rb: e.tensor_scalar(
            out=rb[:, :], in0=ps[:, :], scalar1=0.0, scalar2=bc[:, 1:2],
            op0=ALU.max, op1=ALU.mult), waits=w)
        e_bank_free[mc % 2] = ee
        ea = vop(lambda e, rb=rb: e.tensor_reduce(
            out=pam[:, 34:35], in_=rb[:, :], axis=AX.X, op=ALU.max),
            waits=[ee, e_amr_prev])
        eam = vop(lambda e: e.tensor_tensor(
            out=am_r[:, :], in0=am_r[:, :], in1=pam[:, 34:35], op=ALU.max), waits=[ea])
        e_amr_prev = eam
        es = dma("scalar", f"st_r{mc % 2}", r_spill[mc * 128:(mc + 1) * 128, :], rb[:, :],
                 waits=[ee])
        e_r_prev[mc % 2] = es
        ev_f1.append((emm, es, eam))
    e_f1_last = e_f1_mm[63]

    e_gr = part_max(am_r[:, 0:1], 16, [e_amr_prev])
    e_arr = allred_max(2, 16, 17, [e_gr])
    e_mh = vop(lambda e: e.tensor_tensor(
        out=scp[0:1, 18:19], in0=scp[0:1, 17:18], in1=scp[0:1, 17:18], op=ALU.mult),
        waits=[e_arr])
    e_sh = vop(lambda e: e.tensor_scalar(
        out=scp[0:1, 18:19], in0=scp[0:1, 18:19], scalar1=float(1.0 / FMAX), scalar2=1e-6,
        op0=ALU.mult, op1=ALU.max), waits=[e_mh])
    e_kh0 = vop(lambda e: e.reciprocal(out=scp[0:1, 19:20], in_=scp[0:1, 18:19]), waits=[e_sh])
    e_kh = vop(lambda e: e.tensor_scalar(
        out=scp[0:1, 19:20], in0=scp[0:1, 19:20], scalar1=0.5, scalar2=None, op0=ALU.mult),
        waits=[e_kh0])
    e_lk = sop(lambda e: e.activation(out=scp[0:1, 20:21], in_=scp[0:1, 19:20], func=AF.Ln),
               waits=[e_kh])
    e_sk = sop(lambda e: e.activation(out=scp[0:1, 20:21], in_=scp[0:1, 20:21], func=AF.Exp,
                                      scale=0.5), waits=[e_lk])
    e_bsk = bcast(20, 4, [e_sk])
    e_bk6 = bcast(19, 6, [e_kh, e_bsk])
    e_s2a = vop(lambda e: e.tensor_tensor(
        out=scp[0:1, 21:22], in0=scp[0:1, 18:19], in1=scp[0:1, 4:5], op=ALU.mult),
        waits=[e_sh, e_swp])
    e_s2b = vop(lambda e: e.tensor_scalar(
        out=scp[0:1, 21:22], in0=scp[0:1, 21:22], scalar1=4.0, scalar2=None, op0=ALU.mult),
        waits=[e_s2a])
    e_bs2 = bcast(21, 5, [e_s2b])

    wp8v = wp8[:, :].rearrange("(kc q p) n -> kc q p n", kc=32, q=4)
    h8v = h8[:, :, :, :]
    ev_y = []
    e_h8_mm_done = None
    e_o2_free = {}
    e_pq_free = {t: None for t in range(8)}
    e_sq2_rd = None
    e_f2t_rd = None
    e_eq8 = {}
    e_wp_mm = {}
    e_r_ld = {0: None, 1: None}
    for blk in range(2):
        for mc in range(64):
            w = [ev_f1[mc][1]]
            if e_r_ld[mc % 2] is not None:
                w.append(e_r_ld[mc % 2])
            elr = dma("sync", f"d_r{mc % 2}", f2r[mc % 2][:, :],
                      r_spill[mc * 128:(mc + 1) * 128, blk * 1024:(blk + 1) * 1024],
                      waits=w)
            wq_ = [elr, e_bsk]
            if mc < 2:
                wq_.append(e_qz_all if blk == 0 else e_h8_mm_done)
            eq8 = sop(lambda e, mc=mc: e.activation(
                out=h8v[:, mc // 2, mc % 2, :], in_=f2r[mc % 2][:, :], func=AF.Square,
                scale=bc[:, 4:5]), waits=wq_)
            e_r_ld[mc % 2] = eq8
            e_eq8[mc] = eq8
        ev_o2 = {}
        for q in range(4):
            for kc2 in range(32):
                w = [e_ag2]
                if blk == 0 and q == 0 and kc2 < 2:
                    w.append(e_qz_all)
                    w.append(ev_x1[-1][3])
                    w.append(ev_x1[-1][5])
                prev = e_wp_mm.get((q, kc2 - 2)) if kc2 >= 2 else (
                    e_wp_mm.get((q - 1, 30 + kc2)) if q > 0 else
                    (e_wp_mm.get((3, 30 + kc2)) if blk > 0 else None))
                if prev is not None:
                    w.append(prev)
                ewp = dma("sync", f"d_wp{kc2 % 2}", f2w[kc2 % 2][:, :],
                          wp8v[kc2, q, :, :], waits=w)
                rhs = f2w[kc2 % 2][:, :].rearrange("p (kk n) -> p kk n", kk=2)
                emm = None
                for t in range(8):
                    ps = pA if t < 4 else pB
                    w2_ = []
                    if t == 0:
                        w2_ = [ewp, e_eq8[2 * kc2 + 1], e_eq8[min(2 * kc2, 63)]]
                    if kc2 == 0:
                        if q == 0 and blk == 0:
                            if t < 4:
                                w2_.append(e_bank_free[0])
                            else:
                                w2_.append(e_bank_free[1])
                            w2_.append(e_f1_last)
                        elif e_pq_free[t] is not None:
                            w2_.append(e_pq_free[t])
                    emm = top(lambda e, ps=ps, t=t, kc2=kc2, rhs=rhs: e.matmul(
                        ps[:, (t % 4) * 512:(t % 4 + 1) * 512],
                        h8v[:, kc2, :, t * 128:(t + 1) * 128], rhs,
                        start=(kc2 == 0), stop=(kc2 == 31), perf_mode=DR), waits=w2_)
                    e_wp_mm[(q, kc2)] = emm
            for t in range(8):
                ps = pA if t < 4 else pB
                w = [e_wp_mm[(q, 31)], e_bs2]
                if q == 0:
                    w.append(e_f1_last)
                    if blk > 0:
                        w.append(e_o2_free[t])
                ev_o2[(q, t)] = vop(lambda e, ps=ps, t=t, q=q: e.tensor_scalar(
                    out=o2[:, t, q * 512:(q + 1) * 512],
                    in0=ps[:, (t % 4) * 512:(t % 4 + 1) * 512], scalar1=bc[:, 5:6],
                    scalar2=None, op0=ALU.mult), waits=w)
                e_pq_free[t] = ev_o2[(q, t)]
        e_h8_mm_done = e_wp_mm[(3, 31)]
        for t in range(8):
            n = blk * 8 + t
            strip = o2[:, t, :]
            w = [ev_o2[(q, t)] for q in range(4)]
            es1_ = vop(lambda e, s=strip, n=n: e.reduce_sum(
                out=f3a[:, n:n + 1], in_=s, axis=AX.X), waits=w)
            wsq = list(w)
            if e_sq2_rd is not None:
                wsq.append(e_sq2_rd)
            if e_f2t_rd is not None:
                wsq.append(e_f2t_rd)
            if blk == 0 and t == 0:
                wsq.append(e_qz_all)
                wsq.append(ev_f1[-1][2])
            esq_ = sop(lambda e, s=strip: e.activation(out=f2t[:, :], in_=s, func=AF.Square),
                       waits=wsq)
            es2_ = vop(lambda e, n=n: e.reduce_sum(
                out=f3b[:, n:n + 1], in_=f2t[:, :], axis=AX.X), waits=[esq_])
            e_sq2_rd = es2_
            em_ = vop(lambda e, n=n: e.tensor_scalar(
                out=f3m[:, n:n + 1], in0=f3a[:, n:n + 1], scalar1=float(1.0 / D),
                scalar2=None, op0=ALU.mult), waits=[es1_])
            eqq = vop(lambda e, n=n: e.tensor_tensor(
                out=f3t[:, n:n + 1], in0=f3m[:, n:n + 1], in1=f3m[:, n:n + 1], op=ALU.mult),
                waits=[em_])
            eee = vop(lambda e, n=n: e.tensor_scalar(
                out=f3b[:, n:n + 1], in0=f3b[:, n:n + 1], scalar1=float(1.0 / D),
                scalar2=None, op0=ALU.mult), waits=[es2_])
            evv = vop(lambda e, n=n: e.tensor_tensor(
                out=f3b[:, n:n + 1], in0=f3b[:, n:n + 1], in1=f3t[:, n:n + 1],
                op=ALU.subtract), waits=[eqq, eee])
            ell = sop(lambda e, n=n: e.activation(out=f3r[:, n:n + 1], in_=f3b[:, n:n + 1],
                                                  func=AF.Ln, bias=cst_eps[:, 0:1]),
                      waits=[evv])
            err = sop(lambda e, n=n: e.activation(out=f3r[:, n:n + 1], in_=f3r[:, n:n + 1],
                                                  func=AF.Exp, scale=-0.5), waits=[ell])
            xb = f2x[t % 2]
            wx = [ev_x1[n][3], e_qz_all]
            if len(ev_y) >= 2:
                wx.append(ev_y[-2])
            ex = dma("sync", f"d_x{t % 2}", xb[:, :], x1_scr[n * 128:(n + 1) * 128, :],
                     waits=wx)
            wty = [err, es2_]
            if e_f2t_rd is not None:
                wty.append(e_f2t_rd)
            ety = vop(lambda e, s=strip, n=n: e.tensor_scalar(
                out=f2t[:, :], in0=s, scalar1=f3m[:, n:n + 1], scalar2=f3r[:, n:n + 1],
                op0=ALU.subtract, op1=ALU.mult), waits=wty)
            e_o2_free[t] = ety
            eyy = vop(lambda e, xb=xb: e.tensor_tensor(
                out=xb[:, :], in0=f2t[:, :], in1=xb[:, :], op=ALU.add), waits=[ety, ex])
            e_f2t_rd = eyy
            est = dma("sync", f"st_y{t % 2}", y_out[n * 128:(n + 1) * 128, :], xb[:, :],
                      waits=[eyy])
            ev_y.append(est)

    v0, v1 = sch.cnt["st_y0"], sch.cnt["st_y1"]
    sch.do("sync", lambda e, a=v0, b=v1: (e.wait_ge("st_y0", a), e.wait_ge("st_y1", b)),
           waits=(), inc=None)
    return nc, sch


class _Wrap:
    def __init__(self, eng, sems):
        self._e = eng
        self._s = sems

    def __getattr__(self, k):
        return getattr(self._e, k)

    def wait_ge(self, s, v):
        if isinstance(s, str):
            s = self._s[s]
        return self._e.wait_ge(s, v)


def _emit(nc, sch):
    with contextlib.ExitStack() as es:
        sems = {name: es.enter_context(nc.semaphore(name)) for name in sch.cnt}
        block = es.enter_context(nc.Block())

        def make(eng_name):
            ops = sch.seq[eng_name]

            def _f(e):
                waited = {}
                wrap = _Wrap(e, sems)
                for w, fn, inc, amt in ops:
                    for s, v in w.items():
                        if waited.get(s, 0) < v:
                            e.wait_ge(sems[s], v)
                            waited[s] = v
                    ins = fn(wrap)
                    if inc is not None:
                        if amt != 1:
                            ins.then_inc(sems[inc], amt)
                        else:
                            ins.then_inc(sems[inc])
            return _f

        block.sync(make("sync"))
        block.scalar(make("scalar"))
        block.vector(make("vector"))
        block.tensor(make("tensor"))
        block.gpsimd(make("gpsimd"))
    return nc


def _host_tables(s0):
    inv = (1.0 / (np.float32(THETA) ** (np.arange(0, HD, 2, dtype=np.float32)
                                        / np.float32(HD)))).astype(np.float32)
    pos = (np.arange(N_TOK, dtype=np.float32) + np.float32(s0 - HALO))
    freqs = pos[None, :] * inv[:, None]
    ang = np.concatenate([freqs, freqs], axis=0)
    cos = np.cos(ang).astype(np.float32)
    sin = np.sin(ang).astype(np.float32)
    sin[0:64, :] *= np.float32(-1.0)
    return np.ascontiguousarray(cos), np.ascontiguousarray(sin)


def _host_masks(s0):
    out = np.zeros((22, HD, 512), dtype=np.float32)
    cc = np.arange(128)[:, None]
    r = np.arange(128)[None, :]
    m0 = np.where(cc > r, 0.0, NEG)
    m4 = np.where(cc <= r, 0.0, NEG)
    out[20] = np.tile(m0, (1, 4))
    out[21] = np.tile(m4, (1, 4))
    for n in range(4):
        for j in range(5):
            i = s0 + 128 * n + r
            jg = s0 - HALO + 128 * (n + j) + cc
            allowed = (jg <= i) & (jg > i - WIN) & (jg >= 0)
            out[n * 5 + j] = np.tile(np.where(allowed, 0.0, NEG), (1, 4))
    return out.astype(ml_dtypes.bfloat16)


_NC_CACHE = None


def kernel(L0_attn_hidden_states, L0_attn_wq, L0_attn_wk, L0_attn_wv, L0_attn_wo,
           L0_ffn_w_fc, L0_ffn_w_proj):
    global _NC_CACHE
    x = np.ascontiguousarray(L0_attn_hidden_states, dtype=np.float32)
    wq = np.asarray(L0_attn_wq, dtype=np.float32)
    wk = np.asarray(L0_attn_wk, dtype=np.float32)
    wv = np.asarray(L0_attn_wv, dtype=np.float32)
    wo = np.asarray(L0_attn_wo, dtype=np.float32)
    wfc = np.asarray(L0_ffn_w_fc, dtype=np.float32)
    wproj = np.asarray(L0_ffn_w_proj, dtype=np.float32)

    wqT = np.ascontiguousarray(wq.T)
    wkT = np.ascontiguousarray(wk.T)
    wvT = np.ascontiguousarray(wv.T)
    woT = np.ascontiguousarray(wo.T)
    wfc_perm = np.ascontiguousarray(
        wfc.reshape(64, 128, 8, 2, 128).transpose(0, 4, 2, 3, 1).reshape(64, 128, D))
    wp_perm = np.ascontiguousarray(
        wproj.reshape(4, 512, 32, 2, 128).transpose(2, 0, 4, 3, 1).reshape(32, 4, 128, 1024))

    if _NC_CACHE is None:
        nc, sch = build_nc()
        _emit(nc, sch)
        _NC_CACHE = nc
    nc = _NC_CACHE

    ones_bf = np.ones((HD, HD), dtype=ml_dtypes.bfloat16)
    ident_bf = np.eye(HD, dtype=ml_dtypes.bfloat16)
    in_maps = []
    for c in range(8):
        b, half = c // 2, c % 2
        s0 = half * N_OWN
        xp = np.zeros((N_TOK, D), dtype=np.float32)
        lo = max(0, s0 - HALO)
        xp[HALO - (s0 - lo):, :] = x[b, lo:s0 + N_OWN, :]
        cos, sin = _host_tables(s0)
        in_maps.append({
            "x": xp,
            "cosT": cos, "sinT": sin,
            "masks": _host_masks(s0),
            "ones_in": ones_bf, "ident_in": ident_bf,
            "wqT": wqT, "wkT": wkT, "wvT": wvT, "woT": woT,
            "wfc_p": np.ascontiguousarray(wfc_perm[c * 8:(c + 1) * 8]),
            "wp_p": np.ascontiguousarray(wp_perm[c * 4:(c + 1) * 4]),
        })
    res = run_bass_kernel_spmd(nc, in_maps, core_ids=list(range(8)))
    out = np.empty((B, S, D), dtype=np.float32)
    for c in range(8):
        b, half = c // 2, c % 2
        out[b, half * N_OWN:(half + 1) * N_OWN, :] = res.results[c]["y"]
    return out


# version 14
# speedup vs baseline: 1.0293x; 1.0141x over previous
import numpy as np
import ml_dtypes
import contextlib
import concourse.bass as bass
import concourse.mybir as mybir
from concourse.bass_utils import run_bass_kernel_spmd

F32 = mybir.dt.float32
BF16 = mybir.dt.bfloat16
FP8 = mybir.dt.float8e4
AF = mybir.ActivationFunctionType
ALU = mybir.AluOpType
AX = mybir.AxisListType
DR = mybir.MatmulPerfMode.DoubleRow

B, S, D = 4, 4096, 2048
HQ, HKV, HD = 16, 4, 128
WIN = 512
CAP = 50.0
THETA = 10000.0
EPS = 1e-5
N_OWN, HALO, N_TOK = 2048, 512, 2560
T_OWN, T_ALL = 16, 20
NCH = 4
FMAX = 448.0
NEG = -1e30

ENGINES = ("sync", "scalar", "vector", "tensor", "gpsimd")


class Sched:
    def __init__(self):
        self.seq = {e: [] for e in ENGINES}
        self.cnt = {}
        self.known = set()

    def new_sem(self, name):
        self.cnt[name] = 0

    def do(self, eng, fn, waits=(), inc=None, amt=1):
        w = {}
        for e in waits:
            if e is None:
                continue
            s, v = e
            assert (s, v) in self.known, f"wait on future/unknown event {s}:{v}"
            w[s] = max(w.get(s, 0), v)
        self.seq[eng].append((w, fn, inc, amt))
        if inc is not None:
            self.cnt[inc] += amt
            e = (inc, self.cnt[inc])
            self.known.add(e)
            return e
        return None


def build_nc():
    nc = bass.Bass("TRN2", target_bir_lowering=False)
    sch = Sched()

    x_in = nc.dram_tensor("x", [N_TOK, D], F32, kind="ExternalInput")
    cosT_d = nc.dram_tensor("cosT", [HD, N_TOK], F32, kind="ExternalInput")
    sinT_d = nc.dram_tensor("sinT", [HD, N_TOK], F32, kind="ExternalInput")
    masks_d = nc.dram_tensor("masks", [22, HD, 512], BF16, kind="ExternalInput")
    ones_d = nc.dram_tensor("ones_in", [HD, HD], BF16, kind="ExternalInput")
    ident_d = nc.dram_tensor("ident_in", [HD, HD], BF16, kind="ExternalInput")
    wqT_d = nc.dram_tensor("wqT", [D, D], F32, kind="ExternalInput")
    wkT_d = nc.dram_tensor("wkT", [D, 512], F32, kind="ExternalInput")
    wvT_d = nc.dram_tensor("wvT", [D, 512], F32, kind="ExternalInput")
    woT_d = nc.dram_tensor("woT", [D, D], F32, kind="ExternalInput")
    wfc_p = nc.dram_tensor("wfc_p", [8, 128, D], F32, kind="ExternalInput")
    wp_p = nc.dram_tensor("wp_p", [4, 4, 128, 1024], F32, kind="ExternalInput")
    y_out = nc.dram_tensor("y", [N_OWN, D], F32, kind="ExternalOutput")

    wqT_bf = nc.dram_tensor("wqT_bf", [D, D], BF16)
    wkT_bf = nc.dram_tensor("wkT_bf", [D, 512], BF16)
    wvT_bf = nc.dram_tensor("wvT_bf", [D, 512], BF16)
    woT_bf = nc.dram_tensor("woT_bf", [D, D], BF16)
    wfc8_ag = nc.dram_tensor("wfc8_ag", [8 * 128, D], FP8)
    wfc8 = nc.dram_tensor("wfc8", [64 * 128, D], FP8)
    wp8_ag = nc.dram_tensor("wp8_ag", [16 * 128, 1024], FP8)
    wp8 = nc.dram_tensor("wp8", [128 * 128, 1024], FP8)
    xnT_scr = nc.dram_tensor("xnT_scr", [16, 128, N_TOK], BF16)
    x1_scr = nc.dram_tensor("x1_scr", [N_OWN, D], F32)
    r_spill = nc.dram_tensor("r_spill", [4 * D, N_OWN], BF16)
    cc_in = [nc.dram_tensor(f"cc{i}_in", [1, 8], F32) for i in range(3)]
    cc_out = [nc.dram_tensor(f"cc{i}_out", [1, 8], F32) for i in range(3)]
    col_scr = nc.dram_tensor("col_scr", [HD, 1], F32)
    row_scr = nc.dram_tensor("row_scr", [HD, 1], F32)

    SB_BASE = 16512
    def sb(name, shape, dtype, off_kb):
        return nc.alloc_sbuf_tensor_at(name, shape, dtype, offset=SB_BASE + int(off_kb * 1024))

    _cur = [SB_BASE]

    def sbp(name, shape, dtype):
        ds = {F32: 4, BF16: 2, FP8: 1}[dtype]
        n = 1
        for s_ in shape[1:]:
            n *= s_
        t = nc.alloc_sbuf_tensor_at(name, shape, dtype, offset=_cur[0])
        _cur[0] += -(-n * ds // 32) * 32
        assert _cur[0] <= SB_BASE + 6 * 1024
        return t

    ones_bf = sbp("ones_bf", [HD, HD], BF16)
    ident = sbp("ident", [HD, HD], BF16)
    identf = sbp("identf", [HD, HD], F32)
    onesrow = sbp("onesrow", [1, 128], F32)
    scp = sbp("scp", [1, 64], F32)
    bc = sbp("bc", [HD, 8], F32)
    grow = sbp("grow", [1, 128], F32)
    pam = sbp("pam", [HD, 40], F32)
    s1a = sbp("s1a", [HD, 24], F32)
    s1b = sbp("s1b", [HD, 24], F32)
    s1m = sbp("s1m", [HD, 24], F32)
    s1r = sbp("s1r", [HD, 24], F32)
    s1t = sbp("s1t", [HD, 24], F32)
    w1 = sbp("wst1", [HD, 16], F32)
    w2 = sbp("wst2", [HD, 16], F32)
    wm = sbp("wstm", [HD, 16], F32)
    wr = sbp("wstr", [HD, 16], F32)
    wt = sbp("wstt", [HD, 16], F32)
    m2a = sbp("m2a", [HD, 16], F32)
    m2b = sbp("m2b", [HD, 16], F32)
    m2m = sbp("m2m", [HD, 16], F32)
    m2r = sbp("m2r", [HD, 16], F32)
    m2t = sbp("m2t", [HD, 16], F32)
    ntb = sbp("ntb", [HD, 16], F32)
    ntb2 = sbp("ntb2", [HD, 16], F32)
    am_x = sbp("am_x", [HD, 16], F32)
    am_r = sbp("am_r", [HD, 1], F32)
    f3a = sbp("f3a", [HD, 16], F32)
    f3b = sbp("f3b", [HD, 16], F32)
    f3m = sbp("f3m", [HD, 16], F32)
    f3r = sbp("f3r", [HD, 16], F32)
    f3t = sbp("f3t", [HD, 16], F32)
    cst_eps = sbp("cst_eps", [HD, 1], F32)
    cst_ncap = sbp("cst_ncap", [HD, 1], F32)

    ax = [sb("ax0", [HD, D], F32, 6), sb("ax1", [HD, D], F32, 14)]
    xnc = [sb("xnc0", [HD, 16, 512], BF16, 22), sb("xnc1", [HD, 16, 512], BF16, 38)]
    xn_tile = sb("xn_tile", [HD, D], F32, 54)
    at3 = sb("at3", [HD, 16, 16, 128], BF16, 6)
    h8 = sb("h8", [HD, 32, 2, 1024], FP8, 6)
    pwb = [sb("pw0", [HD, 1024], F32, 62), sb("pw1", [HD, 1024], F32, 66)]
    p8 = sb("p8", [HD, 1024], FP8, 134)
    wqs = sb("wqs", [HD, 2, 16, 512], BF16, 70)
    xnT_sb = sb("xnT_sb", [HD, 16, 512], BF16, 102)
    qc = sb("qc", [HD, 16 * 512], BF16, 118)
    xn_tile_b = sb("xn_tile_b", [HD, D], F32, 118)
    woT_sb = sb("woT_sb", [HD, 16, D], BF16, 70)
    xq8 = sb("xq8", [HD, 16, N_OWN], FP8, 70)
    f1w = [sb("f1w0", [HD, D], FP8, 102), sb("f1w1", [HD, D], FP8, 104)]
    o2 = sb("o2", [HD, 8, D], F32, 70)
    kT = sb("kT", [HD, HKV * N_TOK], BF16, 134)
    vT = sb("vT", [HD, T_ALL * 512], BF16, 154)
    a4x = [sb("a4x0", [HD, D], F32, 134), sb("a4x1", [HD, D], F32, 142)]
    sq2 = sb("sq2", [HD, D], F32, 150)
    o1b = [sb("o10", [HD, D], F32, 158), sb("o11", [HD, D], F32, 166)]
    f1r = [sb("f1r0", [HD, D], BF16, 134), sb("f1r1", [HD, D], BF16, 138)]
    f2w = [sb("f2w0", [HD, 1024], FP8, 142), sb("f2w1", [HD, 1024], FP8, 143)]
    f2r = [sb("f2r0", [HD, 1024], BF16, 144), sb("f2r1", [HD, 1024], BF16, 146)]
    f2x = [sb("f2x0", [HD, D], F32, 148), sb("f2x1", [HD, D], F32, 156)]
    f2t = sb("f2t", [HD, D], F32, 164)
    tabc = sb("tabc", [HD, 2 * 1024], F32, 174)
    sq1 = sb("sq1", [HD, D], F32, 182)
    mask5 = sb("mask5", [HD, 5 * 512], BF16, 182)
    mpers = sb("mpers", [HD, 2 * 512], BF16, 187)
    pTs = sb("pTs", [HD, 5 * 512], BF16, 189)
    ropa = sb("ropa", [HD, 512], F32, 194)
    ropb = sb("ropb", [HD, 512], F32, 196)
    rinv2 = [sb("rinv", [HD, 512], F32, 198), sb("rinvB", [HD, 512], F32, 205)]
    pTs2 = sb("pTsB", [HD, 5 * 512], BF16, 200)
    xn2_tile = sb("xn2_tile", [HD, D], F32, 182)

    pA = nc.alloc_psum_tensor("pA", [HD, D], F32)
    pB = nc.alloc_psum_tensor("pB", [HD, D], F32)

    SEM_NAMES = ("g_d", "s_d", "cc", "te", "ve", "se", "ge",
                 "d_x0", "d_x1", "st_xnt", "d_xnt", "d_wkv",
                 "d_wq0", "d_wq1", "d_wo", "d_wfc", "d_wp0", "d_wp1",
                 "d_r0", "d_r1", "st_r0", "st_r1", "st_x10", "st_x11",
                 "st_y0", "st_y1", "d_tab0", "d_tab1", "d_mask")
    CHAINQ = {"gpsimd": "g_d", "scalar": "s_d"}
    _chain_last = {}
    for s in SEM_NAMES:
        sch.new_sem(s)

    def dma(q, sem, out, in_, waits=()):
        waits = [w for w in waits if w is not None]
        if q in CHAINQ:
            sem = CHAINQ[q]
            if sem in _chain_last:
                waits = waits + [_chain_last[sem]]
        ev = sch.do(q, lambda e, o=out, i=in_: e.dma_start(out=o, in_=i),
                    waits=waits, inc=sem, amt=16)
        if q in CHAINQ:
            _chain_last[sem] = ev
        return ev

    def vop(fn, waits=()):
        return sch.do("vector", fn, waits=waits, inc="ve")

    def sop(fn, waits=()):
        return sch.do("scalar", fn, waits=waits, inc="se")

    def top(fn, waits=()):
        return sch.do("tensor", fn, waits=waits, inc="te")

    def gop(fn, waits=()):
        return sch.do("gpsimd", fn, waits=waits, inc="ge")

    def coll(kind, op, i_, o_, waits):
        ev = sch.do("gpsimd", lambda e, k=kind, o=op, a=i_, b=o_: e.collective_compute(
            k, o, replica_groups=[list(range(8))], ins=[a], outs=[b]),
            waits=waits + ([_chain_last["g_d"]] if "g_d" in _chain_last else []),
            inc="cc", amt=1)
        return ev

    e_grow = [None]

    def part_max(src_col, slot, waits):
        w = list(waits) + ([e_grow[0]] if e_grow[0] else [])
        e1 = dma("gpsimd", "g_d", col_scr[:, :], src_col, waits=w)
        e2 = dma("gpsimd", "g_d", grow[0:1, 0:128], col_scr[:, :], waits=[e1])
        e3 = vop(lambda e, sl=slot: e.tensor_reduce(
            out=scp[0:1, sl:sl + 1], in_=grow[0:1, 0:128], axis=AX.X, op=ALU.max),
            waits=[e2])
        e_grow[0] = e3
        return e3

    def bcast(slot, col, waits):
        w = list(waits) + ([e_grow[0]] if e_grow[0] else [])
        e1 = vop(lambda e, sl=slot: e.tensor_scalar(
            out=grow[0:1, 0:128], in0=onesrow[0:1, 0:128],
            scalar1=scp[0:1, sl:sl + 1], scalar2=None, op0=ALU.mult), waits=w)
        e2 = dma("gpsimd", "g_d", row_scr[:, :], grow[0:1, 0:128], waits=[e1])
        e3 = dma("gpsimd", "g_d", bc[:, col:col + 1], row_scr[:, :], waits=[e2])
        e_grow[0] = e3
        return e3

    def allred_max(idx, slot_in, slot_out, waits):
        e2 = dma("gpsimd", "g_d", cc_in[idx][0:1, 0:1], scp[0:1, slot_in:slot_in + 1],
                 waits=list(waits))
        e2b = dma("gpsimd", "g_d", cc_in[idx][0:1, 1:8], onesrow[0:1, 0:7], waits=[e2])
        e3 = coll("AllReduce", ALU.max, cc_in[idx][:, :], cc_out[idx][:, :], waits=[e2b])
        e4 = dma("gpsimd", "g_d", grow[0:1, 0:8], cc_out[idx][:, :],
                 waits=[e3] + ([e_grow[0]] if e_grow[0] else []))
        e5 = vop(lambda e, sl=slot_out: e.tensor_copy(
            out=scp[0:1, sl:sl + 1], in_=grow[0:1, 0:1]), waits=[e4])
        e_grow[0] = e5
        return e5

    e_ones = dma("gpsimd", "g_d", ones_bf[:, :], ones_d[:, :])
    e_id0 = dma("gpsimd", "g_d", ident[:, :], ident_d[:, :])
    e_ident = vop(lambda e: e.tensor_copy(out=identf[:, :], in_=ident[:, :]),
                  waits=[e_id0])
    e_or = gop(lambda e: e.memset(onesrow[0:1, 0:128], 1.0))
    e_ceps = gop(lambda e: e.memset(cst_eps[:, :], float(EPS)), waits=[e_or])
    e_ncap = gop(lambda e: e.memset(cst_ncap[:, :], float(-8.0)), waits=[e_ceps])

    e_wkbf = dma("gpsimd", "g_d", wkT_bf[:, :], wkT_d[:, :])
    e_wvbf = dma("gpsimd", "g_d", wvT_bf[:, :], wvT_d[:, :])
    e_wqbf = []
    for hq in range(4):
        e_wqbf.append(dma("gpsimd", "g_d", wqT_bf[:, hq * 512:(hq + 1) * 512],
                          wqT_d[:, hq * 512:(hq + 1) * 512]))
    e_wobf = dma("gpsimd", "g_d", woT_bf[:, :], woT_d[:, :])

    def prep_src(j):
        if j < 16:
            return wfc_p[j // 2, :, (j % 2) * 1024:(j % 2 + 1) * 1024]
        i, q = (j - 16) // 4, (j - 16) % 4
        return wp_p[i, q, :, :]

    def prep_dst(j):
        if j < 16:
            return wfc8_ag[(j // 2) * 128:(j // 2 + 1) * 128,
                           (j % 2) * 1024:(j % 2 + 1) * 1024]
        i, q = (j - 16) // 4, (j - 16) % 4
        base = (i * 4 + q) * 128
        return wp8_ag[base:base + 128, :]

    prep_state = {"red": {}, "last_red": None}

    def emit_amax(j):
        w = []
        if j >= 2:
            w.append(prep_state["red"][j - 2])
        el = dma("gpsimd", "g_d", pwb[j % 2][:, :], prep_src(j), waits=w)
        er = vop(lambda e, j=j: e.tensor_reduce(
            out=pam[:, j:j + 1], in_=pwb[j % 2][:, :], axis=AX.X, op=ALU.max,
            apply_absolute_value=True), waits=[el])
        prep_state["red"][j] = er
        prep_state["last_red"] = er

    at4 = at3[:, :, :, :]
    wkv3 = wqs[:, :, :, :].rearrange("p a c m -> p (a c) m")
    e_wkv = dma("sync", "d_wkv", wkv3[:, 0:16, :],
                wkT_bf[:, :].rearrange("(cc p) m -> p cc m", cc=16), waits=[e_wkbf])
    e_wvv = dma("sync", "d_wkv", wkv3[:, 16:32, :],
                wvT_bf[:, :].rearrange("(cc p) m -> p cc m", cc=16), waits=[e_wvbf])

    def rope_evac(pt, pcol, tb, out_ap, waits):
        ea = vop(lambda e: e.tensor_tensor(
            out=ropa[:, :], in0=pt[0:128, pcol:pcol + 512],
            in1=tabc[0:128, tb:tb + 512], op=ALU.mult), waits=waits)
        eb = vop(lambda e: e.tensor_tensor(
            out=ropb[0:64, :], in0=pt[64:128, pcol:pcol + 512],
            in1=tabc[0:64, tb + 512:tb + 1024], op=ALU.mult), waits=[ea])
        ec = vop(lambda e: e.tensor_tensor(
            out=ropb[64:128, :], in0=pt[0:64, pcol:pcol + 512],
            in1=tabc[64:128, tb + 512:tb + 1024], op=ALU.mult), waits=[eb])
        ed = vop(lambda e: e.tensor_tensor(
            out=out_ap, in0=ropa[:, :], in1=ropb[:, :], op=ALU.add), waits=[ec])
        return ed

    ev_tab_prev = None
    e_prev_kmm = None
    e_prev_rope = None
    e_prev_vcopy = None
    ev_xload = {}
    ev_xn_ecp = {}
    ev_xnc_store = {}
    e_exn = {}
    ev_sq = {}
    e_tp_evac = {"A": None, "B": None}
    e_sq1_rd = None
    e_ln_prev = None
    for c in range(5):
        c0 = c * 4
        for tt in range(4):
            t = c0 + tt
            w = []
            if t >= 2:
                w.append(e_exn.get(t - 2))
                w.append(ev_sq.get(t - 2))
            el = dma("sync", f"d_x{t % 2}", ax[t % 2][:, :], x_in[t * 128:(t + 1) * 128, :],
                     waits=w)
            ev_xload[t] = el
            e1 = vop(lambda e, b=ax[t % 2], t=t: e.reduce_sum(
                out=s1a[:, t:t + 1], in_=b[:, :], axis=AX.X), waits=[el])
            wsq = [el]
            if e_sq1_rd is not None:
                wsq.append(e_sq1_rd)
            esq = sop(lambda e, b=ax[t % 2]: e.activation(out=sq1[:, :], in_=b[:, :],
                                                          func=AF.Square), waits=wsq)
            e2 = vop(lambda e, t=t: e.reduce_sum(out=s1b[:, t:t + 1], in_=sq1[:, :],
                                                 axis=AX.X), waits=[esq])
            e_sq1_rd = e2
            ev_sq[t] = e2
            eM = vop(lambda e, t=t: e.tensor_scalar(
                out=s1m[:, t:t + 1], in0=s1a[:, t:t + 1], scalar1=float(1.0 / D),
                scalar2=None, op0=ALU.mult), waits=[e1])
            eE = vop(lambda e, t=t: e.tensor_scalar(
                out=s1b[:, t:t + 1], in0=s1b[:, t:t + 1], scalar1=float(1.0 / D),
                scalar2=None, op0=ALU.mult), waits=[e2])
            eQ = vop(lambda e, t=t: e.tensor_tensor(
                out=s1t[:, t:t + 1], in0=s1m[:, t:t + 1], in1=s1m[:, t:t + 1],
                op=ALU.mult), waits=[eM])
            eV = vop(lambda e, t=t: e.tensor_tensor(
                out=s1b[:, t:t + 1], in0=s1b[:, t:t + 1], in1=s1t[:, t:t + 1],
                op=ALU.subtract), waits=[eE, eQ])
            wl_ = [eV, e_ceps]
            if e_ln_prev is not None:
                wl_.append(e_ln_prev)
            eL = sop(lambda e, t=t: e.activation(out=s1r[:, t:t + 1], in_=s1b[:, t:t + 1],
                                                 func=AF.Ln, bias=cst_eps[:, 0:1]),
                     waits=wl_)
            eR = sop(lambda e, t=t: e.activation(out=s1r[:, t:t + 1], in_=s1r[:, t:t + 1],
                                                 func=AF.Exp, scale=-0.5), waits=[eL])
            e_ln_prev = eR
            xnb = xn_tile if (t % 2 == 0) else xn_tile_b
            wn = [eR, el]
            if t >= 2:
                wn.append(ev_xn_ecp.get(t - 2))
            exn = vop(lambda e, b=ax[t % 2], o=xnb, t=t: e.tensor_scalar(
                out=o[:, :], in0=b[:, :], scalar1=s1m[:, t:t + 1],
                scalar2=s1r[:, t:t + 1], op0=ALU.subtract, op1=ALU.mult), waits=wn)
            e_exn[t] = exn
            ps = pA if (tt % 2 == 0) else pB
            key = "A" if (tt % 2 == 0) else "B"
            wtp = [exn, e_ident]
            if e_tp_evac[key] is not None:
                wtp.append(e_tp_evac[key])
            if c > 0:
                if key == "A" and e_prev_rope is not None:
                    wtp.append(e_prev_rope)
                if key == "B" and e_prev_vcopy is not None:
                    wtp.append(e_prev_vcopy)
            etp = None
            for cc in range(16):
                etp = top(lambda e, ps=ps, o=xnb, cc=cc: e.matmul(
                    ps[:, cc * 128:(cc + 1) * 128], o[:, cc * 128:(cc + 1) * 128],
                    identf[:, :], start=True, stop=True, is_transpose=True),
                    waits=(wtp if cc == 0 else []))
            wev = [etp]
            if c >= 2:
                wev.append(e_prev_kmm)
                wev.append(ev_xnc_store.get(c - 2))
            ecp = sop(lambda e, ps=ps, c=c, tt=tt: e.copy(
                out=xnc[c % 2][:, :, tt * 128:(tt + 1) * 128],
                in_=ps[:, :].rearrange("p (cc m) -> p cc m", cc=16)), waits=wev)
            ev_xn_ecp[t] = ecp
            e_tp_evac[key] = ecp
        ev_xnc_store[c] = dma("scalar", "st_xnt",
                              xnT_scr[:, :, c * 512:(c + 1) * 512]
                              .rearrange("cc p t -> p cc t"),
                              xnc[c % 2][:, :, :], waits=[ev_xn_ecp[c0 + 3]])
        tb = (c % 2) * 1024
        wtab = []
        if ev_tab_prev:
            wtab.append(ev_tab_prev)
        if c >= 2:
            wtab.append(e_prev_rope)
        etc = dma("sync", f"d_tab{c % 2}", tabc[:, tb:tb + 512],
                  cosT_d[:, c * 512:(c + 1) * 512], waits=wtab)
        ets = dma("sync", f"d_tab{c % 2}", tabc[:, tb + 512:tb + 1024],
                  sinT_d[:, c * 512:(c + 1) * 512], waits=[etc])
        ev_tab_prev = ets
        mmw = [ev_xn_ecp[c0 + 3], e_wkv, e_wvv]
        if e_prev_rope is not None:
            mmw += [e_prev_rope, e_prev_vcopy]
        ek = []
        for hh in range(HKV):
            for cc in range(16):
                ek.append(top(lambda e, hh=hh, cc=cc, c=c: e.matmul(
                    pA[:, hh * 512:(hh + 1) * 512],
                    wkv3[:, cc, hh * 128:(hh + 1) * 128],
                    xnc[c % 2][:, cc, :],
                    start=(cc == 0), stop=(cc == 15)),
                    waits=(mmw if (hh == 0 and cc == 0) else [])))
        evm = []
        for i in range(4):
            for cc in range(16):
                evm.append(top(lambda e, i=i, cc=cc, c=c: e.matmul(
                    pB[:, i * 512:(i + 1) * 512],
                    xnc[c % 2][:, cc, i * 128:(i + 1) * 128],
                    wkv3[:, 16 + cc, :],
                    start=(cc == 0), stop=(cc == 15)), waits=[]))
        e_prev_kmm = evm[-1]
        for hh in range(HKV):
            w = [ek[hh * 16 + 15], ets]
            if e_prev_rope is not None:
                w.append(e_prev_rope)
            e_prev_rope = rope_evac(
                pA, hh * 512, tb,
                kT[:, hh * N_TOK + c * 512:hh * N_TOK + c * 512 + 512], w)
        for i in range(4):
            w = [evm[i * 16 + 15]]
            if e_prev_vcopy is not None:
                w.append(e_prev_vcopy)
            e_prev_vcopy = sop(lambda e, i=i, c=c: e.copy(
                out=vT[:, (c * 4 + i) * 512:(c * 4 + i) * 512 + 512],
                in_=pB[:, i * 512:(i + 1) * 512]), waits=w)
        for j in range(c * 7, min(32, c * 7 + 7)):
            emit_amax(j)
    e_a1_alldone = ev_xn_ecp[19]
    e_fam = vop(lambda e: e.tensor_reduce(
        out=pam[:, 32:33], in_=pam[:, 0:16], axis=AX.X, op=ALU.max),
        waits=[prep_state["last_red"]])
    e_pam = vop(lambda e: e.tensor_reduce(
        out=pam[:, 33:34], in_=pam[:, 16:32], axis=AX.X, op=ALU.max), waits=[e_fam])
    e_fm = part_max(pam[:, 32:33], 0, [e_fam])
    e_pm = part_max(pam[:, 33:34], 1, [e_pam])
    e_r0 = vop(lambda e: e.tensor_scalar(
        out=grow[0:1, 0:8], in0=onesrow[0:1, 0:8], scalar1=scp[0:1, 0:1],
        scalar2=None, op0=ALU.mult), waits=[e_or, e_fm, e_pm, e_grow[0]])
    e_r1 = vop(lambda e: e.tensor_scalar(
        out=grow[0:1, 1:2], in0=onesrow[0:1, 1:2], scalar1=scp[0:1, 1:2],
        scalar2=None, op0=ALU.mult), waits=[e_r0])
    e_grow[0] = e_r1
    e_ci = dma("gpsimd", "g_d", cc_in[0][:, :], grow[0:1, 0:8], waits=[e_r1])
    e_ar1 = coll("AllReduce", ALU.max, cc_in[0][:, :], cc_out[0][:, :], waits=[e_ci])
    e_mp0 = dma("sync", "d_mask", mpers[:, 0:512], masks_d[20, :, :],
                waits=[e_a1_alldone])
    e_mp = dma("sync", "d_mask", mpers[:, 512:1024], masks_d[21, :, :], waits=[e_mp0])

    qc3 = qc[:, :].rearrange("p (h t) -> p h t", h=HQ)
    at4f = at3[:, :, :, :].rearrange("p a b c -> p (a b c)")
    ev_a3_attn_last = None
    e_prev_qrope = e_prev_rope
    e_attn_mm_last = None
    ev_mask_prev = None
    e_tanh = {}
    e_exp_prev = None
    e_pT_use = {}
    e_sum_rd = {0: None, 1: None}
    e_pv_rd = {0: None, 1: None}
    e_rinv_rd = {0: None, 1: None}
    e_first_score = True
    e_mask_tt_last = None
    e_xnt_rd = None
    prev_chunk_qmm = None
    e_q_mm_prev = None
    e_prep_sbuf_free = None
    e_p8_free = None
    pending_ea = [None]
    A3S = {"last": None}

    def flush_ea():
        if pending_ea[0] is not None:
            fn, waits, pp = pending_ea[0]
            pending_ea[0] = None
            ea = vop(fn, waits=waits)
            A3S["last"] = ea
            e_pv_rd[pp] = ea
            e_rinv_rd[pp] = ea
        return A3S["last"]

    for c in range(NCH):
        if c == 2:
            e_c1l = dma("gpsimd", "g_d", grow[0:1, 0:8], cc_out[0][:, :],
                        waits=[e_ar1, e_grow[0]])
            e_grow[0] = e_c1l
            e_swf = vop(lambda e: e.tensor_scalar(
                out=scp[0:1, 2:3], in0=grow[0:1, 0:1], scalar1=float(1.0 / FMAX),
                scalar2=1e-6, op0=ALU.mult, op1=ALU.max), waits=[e_c1l])
            e_swp = vop(lambda e: e.tensor_scalar(
                out=scp[0:1, 4:5], in0=grow[0:1, 1:2], scalar1=float(1.0 / FMAX),
                scalar2=1e-6, op0=ALU.mult, op1=ALU.max), waits=[e_swf])
            e_grow[0] = e_swp
            e_k0 = vop(lambda e: e.reciprocal(out=scp[0:1, 3:4], in_=scp[0:1, 2:3]),
                       waits=[e_swf])
            e_kwf = vop(lambda e: e.tensor_scalar(
                out=scp[0:1, 3:4], in0=scp[0:1, 3:4], scalar1=0.5, scalar2=None,
                op0=ALU.mult), waits=[e_k0])
            e_k1 = vop(lambda e: e.reciprocal(out=scp[0:1, 5:6], in_=scp[0:1, 4:5]),
                       waits=[e_swp])
            e_kwp = vop(lambda e: e.tensor_scalar(
                out=scp[0:1, 5:6], in0=scp[0:1, 5:6], scalar1=0.5, scalar2=None,
                op0=ALU.mult), waits=[e_k1])
            e_bwf = bcast(3, 2, [e_kwf])
            e_bwp = bcast(5, 3, [e_kwp])
            e_q_prev = {}
            e_st_prev = None
            for j in range(32):
                w = [e_bwf if j < 16 else e_bwp]
                if j >= 2:
                    w.append(e_q_prev[j - 2])
                el = dma("gpsimd", "g_d", pwb[j % 2][:, :], prep_src(j), waits=w)
                eq = vop(lambda e, j=j: e.tensor_scalar(
                    out=p8[:, :], in0=pwb[j % 2][:, :],
                    scalar1=bc[:, 2:3] if j < 16 else bc[:, 3:4],
                    scalar2=None, op0=ALU.mult), waits=[el, e_bwf, e_bwp] +
                    ([e_st_prev] if e_st_prev else []))
                e_q_prev[j] = eq
                e_st_prev = dma("gpsimd", "g_d", prep_dst(j), p8[:, :], waits=[eq])
            e_prep_sbuf_free = e_q_prev[31]
            e_p8_free = e_st_prev
            e_ag1 = coll("AllGather", ALU.bypass, wfc8_ag[:, :], wfc8[:, :],
                         waits=[e_st_prev])
            e_ag2 = coll("AllGather", ALU.bypass, wp8_ag[:, :], wp8[:, :], waits=[e_ag1])
        tb = (c % 2) * 1024
        wtab = [ev_tab_prev]
        wtab.append(e_prev_qrope if c >= 1 else e_prev_rope)
        etc = dma("sync", f"d_tab{c % 2}", tabc[:, tb:tb + 512],
                  cosT_d[:, 512 + c * 512:512 + (c + 1) * 512], waits=wtab)
        ets = dma("sync", f"d_tab{c % 2}", tabc[:, tb + 512:tb + 1024],
                  sinT_d[:, 512 + c * 512:512 + (c + 1) * 512], waits=[etc])
        ev_tab_prev = ets
        wxl = [ev_xnc_store[c + 1]]
        if e_xnt_rd is not None:
            wxl.append(e_xnt_rd)
        exl = dma("sync", "d_xnt", xnT_sb[:, :, :],
                  xnT_scr[:, :, 512 + c * 512:512 + (c + 1) * 512]
                  .rearrange("cc p t -> p cc t"), waits=wxl)
        eq_mm_done = []
        for hq in range(4):
            wb = hq % 2
            w = [e_wqbf[hq]]
            if c == 0 and hq < 2:
                w.append(e_prev_kmm)
            if hq >= 2:
                w.append(eq_mm_done[hq - 2])
            elif c > 0:
                w.append(prev_chunk_qmm[hq - 2])
            ewq = dma("sync", f"d_wq{hq % 2}", wqs[:, wb, :, :],
                      wqT_bf[:, hq * 512:(hq + 1) * 512]
                      .rearrange("(cc p) m -> p cc m", cc=16), waits=w)
            mmw = [exl, ewq]
            if e_prev_qrope is not None:
                mmw.append(e_prev_qrope)
            if c == 0 and hq == 0:
                mmw.append(e_prev_kmm)
                mmw.append(e_prev_rope)
            if c > 0 and hq == 0:
                mmw.append(e_exp_prev)
                mmw.append(flush_ea())
            emm = []
            for cc in range(16):
                for hh in range(4):
                    emm.append(top(lambda e, cc=cc, hh=hh, wb=wb: e.matmul(
                        pA[:, hh * 512:(hh + 1) * 512],
                        wqs[:, wb, cc, hh * 128:(hh + 1) * 128],
                        xnT_sb[:, cc, :],
                        start=(cc == 0), stop=(cc == 15)),
                        waits=(mmw if (cc == 0 and hh == 0) else [])))
            eq_mm_done.append(emm[-1])
            e_xnt_rd = emm[-1]
            for hh in range(4):
                hg = hq * 4 + hh
                w = [emm[15 * 4 + hh], ets]
                if e_prev_qrope is not None:
                    w.append(e_prev_qrope)
                e_prev_qrope = rope_evac(pA, hh * 512, tb,
                                         qc[:, hg * 512:hg * 512 + 512], w)
        prev_chunk_qmm = eq_mm_done
        e_q_mm_prev = eq_mm_done[-1]
        for i in range(4):
            n = 4 * c + i
            if n < 4:
                ml = []
                for j in range(5):
                    if j == 0:
                        w0 = [e_mp, e_a1_alldone]
                        if e_mask_tt_last is not None:
                            w0 = [e_mask_tt_last]
                    else:
                        w0 = [ml[-1]]
                    ml.append(dma("sync", "d_mask", mask5[:, j * 512:j * 512 + 512],
                                  masks_d[n * 5 + j, :, :], waits=w0))
                ev_mask_prev = ml[-1]
            for g in range(HKV):
                rhs_q = qc3[:, 4 * g:4 * g + 4, i * 128:(i + 1) * 128]
                p2 = (n * 4 + g) % 2
                e_expj = {}
                PS = pB if p2 == 0 else pA
                pt = pTs if p2 == 0 else pTs2
                JORD = (1, 2, 3, 0, 4) if n >= 4 else (0, 1, 2, 3, 4)
                for j in JORD:
                    kt = n + j
                    bk = (j % 2)
                    w = [e_prev_qrope]
                    if e_first_score:
                        w.append(e_prev_vcopy)
                        e_first_score = False
                    if e_tanh.get((p2, bk)) is not None:
                        w.append(e_tanh[(p2, bk)])
                    emm = top(lambda e, kt=kt, g=g, bk=bk, r=rhs_q, PS=PS: e.matmul(
                        PS[:, bk * 512:bk * 512 + 512],
                        kT[:, g * N_TOK + kt * 128:g * N_TOK + kt * 128 + 128], r,
                        start=True, stop=True), waits=w)
                    wt_ = [emm]
                    if n == 0 and g == 0 and j == 0:
                        wt_.append(e_a1_alldone)
                    if e_pT_use.get((p2, j)) is not None:
                        wt_.append(e_pT_use[(p2, j)])
                    et = sop(lambda e, bk=bk, j=j, PS=PS, pt=pt: e.activation(
                        out=pt[:, j * 512:j * 512 + 512],
                        in_=PS[:, bk * 512:bk * 512 + 512], func=AF.Tanh,
                        scale=float(1.0 / (CAP * float(np.sqrt(HD))))), waits=wt_)
                    e_tanh[(p2, bk)] = et
                    em = None
                    if n < 4:
                        em = vop(lambda e, j=j, pt=pt: e.tensor_tensor(
                            out=pt[:, j * 512:j * 512 + 512],
                            in0=pt[:, j * 512:j * 512 + 512],
                            in1=mask5[:, j * 512:j * 512 + 512], op=ALU.add),
                            waits=[et, ev_mask_prev])
                        e_mask_tt_last = em
                    elif j == 0:
                        em = vop(lambda e, pt=pt: e.tensor_tensor(
                            out=pt[:, 0:512], in0=pt[:, 0:512],
                            in1=mpers[:, 0:512], op=ALU.add), waits=[et, e_mp])
                    elif j == 4:
                        em = vop(lambda e, pt=pt: e.tensor_tensor(
                            out=pt[:, 4 * 512:5 * 512], in0=pt[:, 4 * 512:5 * 512],
                            in1=mpers[:, 512:1024], op=ALU.add), waits=[et, e_mp])
                    we_ = [em if em is not None else et, e_ncap]
                    ee = sop(lambda e, j=j, pt=pt: e.activation(
                        out=pt[:, j * 512:j * 512 + 512],
                        in_=pt[:, j * 512:j * 512 + 512],
                        func=AF.Exp, scale=float(CAP), bias=cst_ncap[:, 0:1]), waits=we_)
                    e_exp_prev = ee
                    e_expj[j] = ee
                flush_ea()
                e_sum_last = None
                e_pv_last = None
                for ji, j in enumerate(JORD):
                    kt = n + j
                    first, last = ji == 0, ji == 4
                    wsum = [e_expj[j]]
                    if first and e_sum_rd[p2] is None:
                        wsum.append(e_ones)
                    if first and e_sum_rd[p2] is not None:
                        wsum.append(e_sum_rd[p2])
                    e_sum_last = top(lambda e, j=j, PS=PS, pt=pt, f=first, l=last: e.matmul(
                        PS[:, 2 * 512:3 * 512], ones_bf[:, :],
                        pt[:, j * 512:j * 512 + 512],
                        start=f, stop=l), waits=wsum)
                    wpv = []
                    if first and e_pv_rd[p2] is not None:
                        wpv.append(e_pv_rd[p2])
                    e_pv_last = top(lambda e, j=j, kt=kt, g=g, PS=PS, pt=pt, f=first,
                                    l=last: e.matmul(
                        PS[:, 3 * 512:4 * 512],
                        vT[:, kt * 512 + g * 128:kt * 512 + g * 128 + 128],
                        pt[:, j * 512:j * 512 + 512],
                        start=f, stop=l), waits=wpv)
                    e_pT_use[(p2, j)] = e_pv_last
                e_attn_mm_last = e_pv_last
                w = [e_sum_last]
                if e_rinv_rd[p2] is not None:
                    w.append(e_rinv_rd[p2])
                el1 = vop(lambda e, PS=PS, p2=p2: e.reciprocal(
                    out=rinv2[p2][:, :], in_=PS[:, 2 * 512:3 * 512]), waits=w)
                e_sum_rd[p2] = el1
                w = [e_pv_last, el1]
                if A3S["last"] is not None:
                    w.append(A3S["last"])
                if n == 0 and g == 0:
                    w.append(e_a1_alldone)
                if n == 14 and g == 0:
                    w.append(e_prep_sbuf_free)
                pending_ea[0] = (
                    (lambda e, g=g, n=n, PS=PS, p2=p2: e.tensor_tensor(
                        out=at4f[:, (n * 16 + 4 * g) * 128:(n * 16 + 4 * g + 4) * 128],
                        in0=PS[:, 3 * 512:4 * 512], in1=rinv2[p2][:, :], op=ALU.mult)),
                    w, p2)
        flush_ea()
    ev_a3_attn_last_final = A3S["last"]

    e_wot = []
    for kq in range(4):
        w = [e_wobf, e_q_mm_prev]
        if kq == 3:
            w.append(e_attn_mm_last)
        e_wot.append(dma("sync", "d_wo", woT_sb[:, kq * 4:(kq + 1) * 4, :],
                         woT_bf[kq * 512:(kq + 1) * 512, :]
                         .rearrange("(ck p) m -> p ck m", ck=4), waits=w))
    ev_x1 = []
    e_pa_free = None
    e_sq_rd = None
    e_ecp2_prev = None
    e_wo_mm = {}
    e_xn2_rd = None
    for n in range(T_OWN):
        mm = []
        for ck in range(16):
            for fq in range(4):
                w = []
                if ck == 0 and fq == 0:
                    w = [e_wot[-1], ev_a3_attn_last_final]
                    if e_pa_free is not None:
                        w.append(e_pa_free)
                mm.append(top(lambda e, ck=ck, fq=fq, n=n: e.matmul(
                    pA[:, fq * 512:(fq + 1) * 512],
                    at4[:, n, ck, :],
                    woT_sb[:, ck, fq * 512:(fq + 1) * 512],
                    start=(ck == 0), stop=(ck == 15)), waits=w))
        e_wo_mm[n] = mm[-1]
        o1 = o1b[n % 2]
        w = [mm[-1]]
        if n < 2:
            w.append(e_attn_mm_last)
        if n >= 2:
            w.append(ev_x1[n - 2][1])
        eo = sop(lambda e, o=o1: e.copy(out=o[:, :], in_=pA[:, :]), waits=w)
        e_pa_free = eo
        es1 = vop(lambda e, o=o1, n=n: e.reduce_sum(out=w1[:, n:n + 1], in_=o[:, :],
                                                    axis=AX.X), waits=[eo])
        wsq = [eo]
        if e_sq_rd is not None:
            wsq.append(e_sq_rd)
        if n == 0:
            wsq.append(e_attn_mm_last)
        esq = sop(lambda e, o=o1: e.activation(out=sq2[:, :], in_=o[:, :], func=AF.Square),
                  waits=wsq)
        es2 = vop(lambda e, n=n: e.reduce_sum(out=w2[:, n:n + 1], in_=sq2[:, :], axis=AX.X),
                  waits=[esq])
        e_sq_rd = es2
        em_ = vop(lambda e, n=n: e.tensor_scalar(
            out=wm[:, n:n + 1], in0=w1[:, n:n + 1], scalar1=float(1.0 / D), scalar2=None,
            op0=ALU.mult), waits=[es1])
        eq_ = vop(lambda e, n=n: e.tensor_tensor(
            out=wt[:, n:n + 1], in0=wm[:, n:n + 1], in1=wm[:, n:n + 1], op=ALU.mult),
            waits=[em_])
        ee_ = vop(lambda e, n=n: e.tensor_scalar(
            out=w2[:, n:n + 1], in0=w2[:, n:n + 1], scalar1=float(1.0 / D), scalar2=None,
            op0=ALU.mult), waits=[es2])
        ev_ = vop(lambda e, n=n: e.tensor_tensor(
            out=w2[:, n:n + 1], in0=w2[:, n:n + 1], in1=wt[:, n:n + 1], op=ALU.subtract),
            waits=[ee_, eq_])
        el_ = sop(lambda e, n=n: e.activation(out=wr[:, n:n + 1], in_=w2[:, n:n + 1],
                                              func=AF.Ln, bias=cst_eps[:, 0:1]), waits=[ev_])
        er_ = sop(lambda e, n=n: e.activation(out=wr[:, n:n + 1], in_=wr[:, n:n + 1],
                                              func=AF.Exp, scale=-0.5), waits=[el_])
        xb = a4x[n % 2]
        wx = [e_attn_mm_last] if n < 2 else [ev_x1[n - 2][3], ev_x1[n - 2][5]]
        ex = dma("sync", f"d_x{n % 2}", xb[:, :], x_in[512 + n * 128:512 + (n + 1) * 128, :],
                 waits=wx)
        et_ = vop(lambda e, o=o1, n=n: e.tensor_scalar(
            out=sq2[:, :], in0=o[:, :], scalar1=wm[:, n:n + 1], scalar2=wr[:, n:n + 1],
            op0=ALU.subtract, op1=ALU.mult), waits=[er_, es2, eo])
        ex1 = vop(lambda e, xb=xb: e.tensor_tensor(
            out=xb[:, :], in0=sq2[:, :], in1=xb[:, :], op=ALU.add), waits=[et_, ex, es1])
        es_ = dma("sync", f"st_x1{n % 2}", x1_scr[n * 128:(n + 1) * 128, :], xb[:, :],
                  waits=[ex1])
        eb1 = vop(lambda e, xb=xb, n=n: e.reduce_sum(
            out=m2a[:, n:n + 1], in_=xb[:, :], axis=AX.X), waits=[ex1])
        esqb = sop(lambda e, xb=xb: e.activation(out=sq2[:, :], in_=xb[:, :],
                                                 func=AF.Square), waits=[ex1])
        eb2 = vop(lambda e, n=n: e.reduce_sum(
            out=m2b[:, n:n + 1], in_=sq2[:, :], axis=AX.X), waits=[esqb])
        e_sq_rd = eb2
        eb3 = vop(lambda e, n=n: e.tensor_scalar(
            out=m2m[:, n:n + 1], in0=m2a[:, n:n + 1], scalar1=float(1.0 / D),
            scalar2=None, op0=ALU.mult), waits=[eb1])
        eb4 = vop(lambda e, n=n: e.tensor_tensor(
            out=m2t[:, n:n + 1], in0=m2m[:, n:n + 1], in1=m2m[:, n:n + 1], op=ALU.mult),
            waits=[eb3])
        eb5 = vop(lambda e, n=n: e.tensor_scalar(
            out=m2b[:, n:n + 1], in0=m2b[:, n:n + 1], scalar1=float(1.0 / D),
            scalar2=None, op0=ALU.mult), waits=[eb2])
        eb6 = vop(lambda e, n=n: e.tensor_tensor(
            out=m2b[:, n:n + 1], in0=m2b[:, n:n + 1], in1=m2t[:, n:n + 1],
            op=ALU.subtract), waits=[eb4, eb5])
        eb7 = sop(lambda e, n=n: e.activation(out=m2r[:, n:n + 1], in_=m2b[:, n:n + 1],
                                              func=AF.Ln, bias=cst_eps[:, 0:1]), waits=[eb6])
        eb8 = sop(lambda e, n=n: e.activation(out=m2r[:, n:n + 1], in_=m2r[:, n:n + 1],
                                              func=AF.Exp, scale=-0.5), waits=[eb7])
        wxn2 = [eb8, ex1, eb2]
        if e_xn2_rd is not None:
            wxn2.append(e_xn2_rd)
        if n == 0:
            wxn2.append(e_a1_alldone)
        exn2 = vop(lambda e, xb=xb, n=n: e.tensor_scalar(
            out=xn2_tile[:, :], in0=xb[:, :], scalar1=m2m[:, n:n + 1],
            scalar2=m2r[:, n:n + 1], op0=ALU.subtract, op1=ALU.mult), waits=wxn2)
        ea2 = vop(lambda e, n=n: e.tensor_reduce(
            out=am_x[:, n:n + 1], in_=xn2_tile[:, :], axis=AX.X, op=ALU.max,
            apply_absolute_value=True), waits=[exn2])
        wtp = [exn2]
        if n == 0:
            wtp.append(ev_a3_attn_last_final)
        else:
            wtp.append(e_ecp2_prev)
        etp = None
        for cc in range(16):
            etp = top(lambda e, cc=cc: e.matmul(
                pB[:, cc * 128:(cc + 1) * 128], xn2_tile[:, cc * 128:(cc + 1) * 128],
                identf[:, :], start=True, stop=True, is_transpose=True),
                waits=(wtp if cc == 0 else []))
        ecp2 = sop(lambda e, n=n: e.copy(
            out=at4[:, n, :, :],
            in_=pB[:, :].rearrange("p (cc m) -> p cc m", cc=16)),
            waits=[etp, e_wo_mm[n]])
        e_ecp2_prev = ecp2
        e_xn2_rd = ecp2
        ev_x1.append((ex, eo, ex1, es_, ea2, exn2))
    e_wo_last = e_wo_mm[T_OWN - 1]

    eax2 = vop(lambda e: e.tensor_reduce(
        out=am_x[:, 0:1], in_=am_x[:, 0:16], axis=AX.X, op=ALU.max),
        waits=[ev_x1[-1][4], ev_x1[-2][4]])
    e_gx = part_max(am_x[:, 0:1], 8, [eax2])
    e_arx = allred_max(1, 8, 9, [e_gx])
    e_sx = vop(lambda e: e.tensor_scalar(
        out=scp[0:1, 10:11], in0=scp[0:1, 9:10], scalar1=float(1.0 / FMAX), scalar2=1e-6,
        op0=ALU.mult, op1=ALU.max), waits=[e_arx])
    e_kx0 = vop(lambda e: e.reciprocal(out=scp[0:1, 11:12], in_=scp[0:1, 10:11]), waits=[e_sx])
    e_kx = vop(lambda e: e.tensor_scalar(
        out=scp[0:1, 11:12], in0=scp[0:1, 11:12], scalar1=0.5, scalar2=None, op0=ALU.mult),
        waits=[e_kx0])
    e_bkx = bcast(11, 0, [e_kx])
    e_s1a_ = vop(lambda e: e.tensor_tensor(
        out=scp[0:1, 12:13], in0=scp[0:1, 10:11], in1=scp[0:1, 2:3], op=ALU.mult),
        waits=[e_sx, e_swf])
    e_s1b_ = vop(lambda e: e.tensor_scalar(
        out=scp[0:1, 12:13], in0=scp[0:1, 12:13], scalar1=4.0, scalar2=None, op0=ALU.mult),
        waits=[e_s1a_])
    e_bs1 = bcast(12, 1, [e_s1b_])
    ev_qz = []
    for cc in range(16):
        w = [e_bkx, e_ecp2_prev]
        if cc < 2:
            w.append(e_wo_last)
        if cc % 2 == 0:
            eqz = sop(lambda e, cc=cc: e.activation(
                out=xq8[:, cc, :].rearrange("p (n m) -> p n m", n=16),
                in_=at4[:, :, cc, :], func=AF.Copy, scale=bc[:, 0:1]), waits=w)
        else:
            eqz = vop(lambda e, cc=cc: e.tensor_scalar(
                out=xq8[:, cc, :].rearrange("p (n m) -> p n m", n=16),
                in0=at4[:, :, cc, :], scalar1=bc[:, 0:1], scalar2=None,
                op0=ALU.mult), waits=w)
        ev_qz.append(eqz)
    e_qz_all = vop(lambda e: e.tensor_copy(out=grow[0:1, 8:9], in_=grow[0:1, 0:1]),
                   waits=[ev_qz[-1], ev_qz[-2], e_grow[0]])
    e_grow[0] = e_qz_all

    xq8v = xq8[:, :, :]
    ev_f1 = []
    e_bank_free = {0: e_pa_free, 1: e_ecp2_prev}
    e_r_prev = {0: None, 1: None}
    e_amr0 = vop(lambda e: e.memset(am_r[:, :], 0.0), waits=[e_qz_all])
    e_amr_prev = e_amr0
    ev_wl = []
    e_f1_mm = {}
    for mc in range(64):
        w = [e_ag1]
        if mc < 2:
            w.append(e_wo_last)
        if mc >= 2:
            w.append(e_f1_mm[mc - 2])
        ewl = dma("sync", "d_wfc", f1w[mc % 2][:, :], wfc8[mc * 128:(mc + 1) * 128, :],
                  waits=w)
        ev_wl.append(ewl)
        ps = pA if (mc % 2 == 0) else pB
        emm = None
        for cc2 in range(8):
            lhs = f1w[mc % 2][:, cc2 * 256:(cc2 + 1) * 256].rearrange(
                "p (kk m) -> p kk m", kk=2)
            for tq in range(4):
                w2_ = []
                if cc2 == 0 and tq == 0:
                    w2_ = [ewl, e_qz_all]
                    if e_bank_free[mc % 2] is not None:
                        w2_.append(e_bank_free[mc % 2])
                emm = top(lambda e, ps=ps, lhs=lhs, cc2=cc2, tq=tq: e.matmul(
                    ps[:, tq * 512:(tq + 1) * 512], lhs,
                    xq8v[:, 2 * cc2:2 * cc2 + 2, tq * 512:(tq + 1) * 512],
                    start=(cc2 == 0), stop=(cc2 == 7), perf_mode=DR), waits=w2_)
        e_f1_mm[mc] = emm
        rb = f1r[mc % 2]
        w = [emm, e_bs1]
        if mc < 2:
            w.append(ev_x1[-1][3])
            w.append(ev_x1[-2][3])
            w.append(ev_x1[-1][5])
            w.append(ev_x1[-2][5])
        if mc == 0:
            w.append(e_p8_free)
        if e_r_prev[mc % 2] is not None:
            w.append(e_r_prev[mc % 2])
        ee = vop(lambda e, ps=ps, rb=rb: e.tensor_scalar(
            out=rb[:, :], in0=ps[:, :], scalar1=0.0, scalar2=bc[:, 1:2],
            op0=ALU.max, op1=ALU.mult), waits=w)
        e_bank_free[mc % 2] = ee
        ea = vop(lambda e, rb=rb: e.tensor_reduce(
            out=pam[:, 34:35], in_=rb[:, :], axis=AX.X, op=ALU.max),
            waits=[ee, e_amr_prev])
        eam = vop(lambda e: e.tensor_tensor(
            out=am_r[:, :], in0=am_r[:, :], in1=pam[:, 34:35], op=ALU.max), waits=[ea])
        e_amr_prev = eam
        es = dma("scalar", f"st_r{mc % 2}", r_spill[mc * 128:(mc + 1) * 128, :], rb[:, :],
                 waits=[ee])
        e_r_prev[mc % 2] = es
        ev_f1.append((emm, es, eam))
    e_f1_last = e_f1_mm[63]

    e_gr = part_max(am_r[:, 0:1], 16, [e_amr_prev])
    e_arr = allred_max(2, 16, 17, [e_gr])
    e_mh = vop(lambda e: e.tensor_tensor(
        out=scp[0:1, 18:19], in0=scp[0:1, 17:18], in1=scp[0:1, 17:18], op=ALU.mult),
        waits=[e_arr])
    e_sh = vop(lambda e: e.tensor_scalar(
        out=scp[0:1, 18:19], in0=scp[0:1, 18:19], scalar1=float(1.0 / FMAX), scalar2=1e-6,
        op0=ALU.mult, op1=ALU.max), waits=[e_mh])
    e_kh0 = vop(lambda e: e.reciprocal(out=scp[0:1, 19:20], in_=scp[0:1, 18:19]), waits=[e_sh])
    e_kh = vop(lambda e: e.tensor_scalar(
        out=scp[0:1, 19:20], in0=scp[0:1, 19:20], scalar1=0.5, scalar2=None, op0=ALU.mult),
        waits=[e_kh0])
    e_lk = sop(lambda e: e.activation(out=scp[0:1, 20:21], in_=scp[0:1, 19:20], func=AF.Ln),
               waits=[e_kh])
    e_sk = sop(lambda e: e.activation(out=scp[0:1, 20:21], in_=scp[0:1, 20:21], func=AF.Exp,
                                      scale=0.5), waits=[e_lk])
    e_bsk = bcast(20, 4, [e_sk])
    e_s2a = vop(lambda e: e.tensor_tensor(
        out=scp[0:1, 21:22], in0=scp[0:1, 18:19], in1=scp[0:1, 4:5], op=ALU.mult),
        waits=[e_sh, e_swp])
    e_s2b = vop(lambda e: e.tensor_scalar(
        out=scp[0:1, 21:22], in0=scp[0:1, 21:22], scalar1=4.0, scalar2=None, op0=ALU.mult),
        waits=[e_s2a])
    e_bs2 = bcast(21, 5, [e_s2b])

    wp8v = wp8[:, :].rearrange("(kc q p) n -> kc q p n", kc=32, q=4)
    h8v = h8[:, :, :, :]
    ev_y = []
    e_h8_mm_done = None
    e_o2_free = {}
    e_pq_free = {t: None for t in range(8)}
    e_sq2_rd = None
    e_f2t_rd = None
    e_eq8 = {}
    e_wp_mm = {}
    e_r_ld = {0: None, 1: None}
    for blk in range(2):
        for mc in range(64):
            w = [ev_f1[mc][1]]
            if e_r_ld[mc % 2] is not None:
                w.append(e_r_ld[mc % 2])
            elr = dma("sync", f"d_r{mc % 2}", f2r[mc % 2][:, :],
                      r_spill[mc * 128:(mc + 1) * 128, blk * 1024:(blk + 1) * 1024],
                      waits=w)
            wq_ = [elr, e_bsk]
            if mc < 2:
                wq_.append(e_qz_all if blk == 0 else e_h8_mm_done)
            eq8 = sop(lambda e, mc=mc: e.activation(
                out=h8v[:, mc // 2, mc % 2, :], in_=f2r[mc % 2][:, :], func=AF.Square,
                scale=bc[:, 4:5]), waits=wq_)
            e_r_ld[mc % 2] = eq8
            e_eq8[mc] = eq8
        ev_o2 = {}
        for q in range(4):
            for kc2 in range(32):
                w = [e_ag2]
                if blk == 0 and q == 0 and kc2 < 2:
                    w.append(e_qz_all)
                    w.append(ev_x1[-1][3])
                    w.append(ev_x1[-1][5])
                prev = e_wp_mm.get((q, kc2 - 2)) if kc2 >= 2 else (
                    e_wp_mm.get((q - 1, 30 + kc2)) if q > 0 else
                    (e_wp_mm.get((3, 30 + kc2)) if blk > 0 else None))
                if prev is not None:
                    w.append(prev)
                ewp = dma("sync", f"d_wp{kc2 % 2}", f2w[kc2 % 2][:, :],
                          wp8v[kc2, q, :, :], waits=w)
                rhs = f2w[kc2 % 2][:, :].rearrange("p (kk n) -> p kk n", kk=2)
                emm = None
                for t in range(8):
                    ps = pA if t < 4 else pB
                    w2_ = []
                    if t == 0:
                        w2_ = [ewp, e_eq8[2 * kc2 + 1], e_eq8[min(2 * kc2, 63)]]
                    if kc2 == 0:
                        if q == 0 and blk == 0:
                            if t < 4:
                                w2_.append(e_bank_free[0])
                            else:
                                w2_.append(e_bank_free[1])
                            w2_.append(e_f1_last)
                        elif e_pq_free[t] is not None:
                            w2_.append(e_pq_free[t])
                    emm = top(lambda e, ps=ps, t=t, kc2=kc2, rhs=rhs: e.matmul(
                        ps[:, (t % 4) * 512:(t % 4 + 1) * 512],
                        h8v[:, kc2, :, t * 128:(t + 1) * 128], rhs,
                        start=(kc2 == 0), stop=(kc2 == 31), perf_mode=DR), waits=w2_)
                    e_wp_mm[(q, kc2)] = emm
            for t in range(8):
                ps = pA if t < 4 else pB
                w = [e_wp_mm[(q, 31)], e_bs2]
                if q == 0:
                    w.append(e_f1_last)
                    if blk > 0:
                        w.append(e_o2_free[t])
                ev_o2[(q, t)] = vop(lambda e, ps=ps, t=t, q=q: e.tensor_scalar(
                    out=o2[:, t, q * 512:(q + 1) * 512],
                    in0=ps[:, (t % 4) * 512:(t % 4 + 1) * 512], scalar1=bc[:, 5:6],
                    scalar2=None, op0=ALU.mult), waits=w)
                e_pq_free[t] = ev_o2[(q, t)]
        e_h8_mm_done = e_wp_mm[(3, 31)]
        for t in range(8):
            n = blk * 8 + t
            strip = o2[:, t, :]
            w = [ev_o2[(q, t)] for q in range(4)]
            es1_ = vop(lambda e, s=strip, n=n: e.reduce_sum(
                out=f3a[:, n:n + 1], in_=s, axis=AX.X), waits=w)
            wsq = list(w)
            if e_sq2_rd is not None:
                wsq.append(e_sq2_rd)
            if e_f2t_rd is not None:
                wsq.append(e_f2t_rd)
            if blk == 0 and t == 0:
                wsq.append(e_qz_all)
                wsq.append(ev_f1[-1][2])
            esq_ = sop(lambda e, s=strip: e.activation(out=f2t[:, :], in_=s, func=AF.Square),
                       waits=wsq)
            es2_ = vop(lambda e, n=n: e.reduce_sum(
                out=f3b[:, n:n + 1], in_=f2t[:, :], axis=AX.X), waits=[esq_])
            e_sq2_rd = es2_
            em_ = vop(lambda e, n=n: e.tensor_scalar(
                out=f3m[:, n:n + 1], in0=f3a[:, n:n + 1], scalar1=float(1.0 / D),
                scalar2=None, op0=ALU.mult), waits=[es1_])
            eqq = vop(lambda e, n=n: e.tensor_tensor(
                out=f3t[:, n:n + 1], in0=f3m[:, n:n + 1], in1=f3m[:, n:n + 1], op=ALU.mult),
                waits=[em_])
            eee = vop(lambda e, n=n: e.tensor_scalar(
                out=f3b[:, n:n + 1], in0=f3b[:, n:n + 1], scalar1=float(1.0 / D),
                scalar2=None, op0=ALU.mult), waits=[es2_])
            evv = vop(lambda e, n=n: e.tensor_tensor(
                out=f3b[:, n:n + 1], in0=f3b[:, n:n + 1], in1=f3t[:, n:n + 1],
                op=ALU.subtract), waits=[eqq, eee])
            ell = sop(lambda e, n=n: e.activation(out=f3r[:, n:n + 1], in_=f3b[:, n:n + 1],
                                                  func=AF.Ln, bias=cst_eps[:, 0:1]),
                      waits=[evv])
            err = sop(lambda e, n=n: e.activation(out=f3r[:, n:n + 1], in_=f3r[:, n:n + 1],
                                                  func=AF.Exp, scale=-0.5), waits=[ell])
            xb = f2x[t % 2]
            wx = [ev_x1[n][3], e_qz_all]
            if len(ev_y) >= 2:
                wx.append(ev_y[-2])
            ex = dma("sync", f"d_x{t % 2}", xb[:, :], x1_scr[n * 128:(n + 1) * 128, :],
                     waits=wx)
            wty = [err, es2_]
            if e_f2t_rd is not None:
                wty.append(e_f2t_rd)
            ety = vop(lambda e, s=strip, n=n: e.tensor_scalar(
                out=f2t[:, :], in0=s, scalar1=f3m[:, n:n + 1], scalar2=f3r[:, n:n + 1],
                op0=ALU.subtract, op1=ALU.mult), waits=wty)
            e_o2_free[t] = ety
            eyy = vop(lambda e, xb=xb: e.tensor_tensor(
                out=xb[:, :], in0=f2t[:, :], in1=xb[:, :], op=ALU.add), waits=[ety, ex])
            e_f2t_rd = eyy
            est = dma("sync", f"st_y{t % 2}", y_out[n * 128:(n + 1) * 128, :], xb[:, :],
                      waits=[eyy])
            ev_y.append(est)

    v0, v1 = sch.cnt["st_y0"], sch.cnt["st_y1"]
    sch.do("sync", lambda e, a=v0, b=v1: (e.wait_ge("st_y0", a), e.wait_ge("st_y1", b)),
           waits=(), inc=None)
    return nc, sch


class _Wrap:
    def __init__(self, eng, sems):
        self._e = eng
        self._s = sems

    def __getattr__(self, k):
        return getattr(self._e, k)

    def wait_ge(self, s, v):
        if isinstance(s, str):
            s = self._s[s]
        return self._e.wait_ge(s, v)


def _emit(nc, sch):
    with contextlib.ExitStack() as es:
        sems = {name: es.enter_context(nc.semaphore(name)) for name in sch.cnt}
        block = es.enter_context(nc.Block())

        def make(eng_name):
            ops = sch.seq[eng_name]

            def _f(e):
                waited = {}
                wrap = _Wrap(e, sems)
                for w, fn, inc, amt in ops:
                    for s, v in w.items():
                        if waited.get(s, 0) < v:
                            e.wait_ge(sems[s], v)
                            waited[s] = v
                    ins = fn(wrap)
                    if inc is not None:
                        if amt != 1:
                            ins.then_inc(sems[inc], amt)
                        else:
                            ins.then_inc(sems[inc])
            return _f

        block.sync(make("sync"))
        block.scalar(make("scalar"))
        block.vector(make("vector"))
        block.tensor(make("tensor"))
        block.gpsimd(make("gpsimd"))
    return nc


def _host_tables(s0):
    inv = (1.0 / (np.float32(THETA) ** (np.arange(0, HD, 2, dtype=np.float32)
                                        / np.float32(HD)))).astype(np.float32)
    pos = (np.arange(N_TOK, dtype=np.float32) + np.float32(s0 - HALO))
    freqs = pos[None, :] * inv[:, None]
    ang = np.concatenate([freqs, freqs], axis=0)
    cos = np.cos(ang).astype(np.float32)
    sin = np.sin(ang).astype(np.float32)
    sin[0:64, :] *= np.float32(-1.0)
    return np.ascontiguousarray(cos), np.ascontiguousarray(sin)


def _host_masks(s0):
    out = np.zeros((22, HD, 512), dtype=np.float32)
    cc = np.arange(128)[:, None]
    r = np.arange(128)[None, :]
    m0 = np.where(cc > r, 0.0, NEG)
    m4 = np.where(cc <= r, 0.0, NEG)
    out[20] = np.tile(m0, (1, 4))
    out[21] = np.tile(m4, (1, 4))
    for n in range(4):
        for j in range(5):
            i = s0 + 128 * n + r
            jg = s0 - HALO + 128 * (n + j) + cc
            allowed = (jg <= i) & (jg > i - WIN) & (jg >= 0)
            out[n * 5 + j] = np.tile(np.where(allowed, 0.0, NEG), (1, 4))
    return out.astype(ml_dtypes.bfloat16)


_NC_CACHE = None


def kernel(L0_attn_hidden_states, L0_attn_wq, L0_attn_wk, L0_attn_wv, L0_attn_wo,
           L0_ffn_w_fc, L0_ffn_w_proj):
    global _NC_CACHE
    x = np.ascontiguousarray(L0_attn_hidden_states, dtype=np.float32)
    wq = np.asarray(L0_attn_wq, dtype=np.float32)
    wk = np.asarray(L0_attn_wk, dtype=np.float32)
    wv = np.asarray(L0_attn_wv, dtype=np.float32)
    wo = np.asarray(L0_attn_wo, dtype=np.float32)
    wfc = np.asarray(L0_ffn_w_fc, dtype=np.float32)
    wproj = np.asarray(L0_ffn_w_proj, dtype=np.float32)

    wqT = np.ascontiguousarray(wq.T)
    wkT = np.ascontiguousarray(wk.T)
    wvT = np.ascontiguousarray(wv.T)
    woT = np.ascontiguousarray(wo.T)
    wfc_perm = np.ascontiguousarray(
        wfc.reshape(64, 128, 8, 2, 128).transpose(0, 4, 2, 3, 1).reshape(64, 128, D))
    wp_perm = np.ascontiguousarray(
        wproj.reshape(4, 512, 32, 2, 128).transpose(2, 0, 4, 3, 1).reshape(32, 4, 128, 1024))

    if _NC_CACHE is None:
        nc, sch = build_nc()
        _emit(nc, sch)
        _NC_CACHE = nc
    nc = _NC_CACHE

    ones_bf = np.ones((HD, HD), dtype=ml_dtypes.bfloat16)
    ident_bf = np.eye(HD, dtype=ml_dtypes.bfloat16)
    in_maps = []
    for c in range(8):
        b, half = c // 2, c % 2
        s0 = half * N_OWN
        xp = np.zeros((N_TOK, D), dtype=np.float32)
        lo = max(0, s0 - HALO)
        xp[HALO - (s0 - lo):, :] = x[b, lo:s0 + N_OWN, :]
        cos, sin = _host_tables(s0)
        in_maps.append({
            "x": xp,
            "cosT": cos, "sinT": sin,
            "masks": _host_masks(s0),
            "ones_in": ones_bf, "ident_in": ident_bf,
            "wqT": wqT, "wkT": wkT, "wvT": wvT, "woT": woT,
            "wfc_p": np.ascontiguousarray(wfc_perm[c * 8:(c + 1) * 8]),
            "wp_p": np.ascontiguousarray(wp_perm[c * 4:(c + 1) * 4]),
        })
    res = run_bass_kernel_spmd(nc, in_maps, core_ids=list(range(8)))
    out = np.empty((B, S, D), dtype=np.float32)
    for c in range(8):
        b, half = c // 2, c % 2
        out[b, half * N_OWN:(half + 1) * N_OWN, :] = res.results[c]["y"]
    return out
